# version 02
# baseline (speedup 1.0000x reference)
import numpy as np
from contextlib import ExitStack
import ml_dtypes
import concourse.bass as bass
import concourse.mybir as mybir
from concourse.bass_utils import run_bass_kernel_spmd

F32 = mybir.dt.float32
BF16 = mybir.dt.bfloat16
AF = mybir.ActivationFunctionType
ALU = mybir.AluOpType

NCORES = 8
B, T, D = 2, 8192, 1024
CPS = 4
TC = T // CPS
NCH = D // 128
NEG = -1.0e9
LN_EPS = 1e-5
BF = ml_dtypes.bfloat16


def f32c(a):
    return np.ascontiguousarray(a, dtype=np.float32)


def col_layout(v):
    return f32c(np.asarray(v).reshape(-1, 128).T)


class SemW:
    def __init__(self, h):
        self.h = h
        self.n = 0

    def inc(self, instr, k=1):
        instr.then_inc(self.h, k)
        self.n += k
        return instr


class Ctx:
    def __init__(self, nc):
        self.nc = nc
        self._sems = {}
        self._stack = []
        self._tiles = {}

    def sem(self, name):
        if name not in self._sems:
            cm = self.nc.semaphore(name)
            self._stack.append(cm)
            self._sems[name] = SemW(cm.__enter__())
        return self._sems[name]

    def sbuf(self, name, shape, dtype):
        if name not in self._tiles:
            cm = self.nc.sbuf_tensor(name, list(shape), dtype)
            self._stack.append(cm)
            self._tiles[name] = cm.__enter__()
        return self._tiles[name]

    def close(self):
        for cm in reversed(self._stack):
            cm.__exit__(None, None, None)
        self._stack = []


class PsumPool:

    SERIAL = [0]

    def __init__(self, nc):
        self.nc = nc
        self._cms = []

    def get(self, shape, dtype=F32):
        PsumPool.SERIAL[0] += 1
        cm = self.nc.psum_tensor(f"pp_{PsumPool.SERIAL[0]}", list(shape), dtype)
        self._cms.append(cm)
        return cm.__enter__()

    def close(self):
        for cm in reversed(self._cms):
            cm.__exit__(None, None, None)
        self._cms = []


class XBufs:

    def __init__(self, c, tcols, tag="X"):
        self.tcols = tcols
        self.bufs = [c.sbuf(f"{tag}{i}", [128, tcols], F32) for i in range(NCH)]
        self.scr = c.sbuf(f"{tag}scr", [128, 512], F32)

    def cur(self, j):
        return self.bufs[j]

    def commit(self, j):
        pass

    def chunks(self, off=0):
        return [(self.bufs[j], off) for j in range(NCH)]

    def add_evac(self, e, j, sl, ps):
        w = sl.stop - sl.start
        e.tensor_tensor(self.scr[:, 0:w], ps, self.bufs[j][:, sl], ALU.add)
        return e.tensor_copy(self.bufs[j][:, sl], self.scr[:, 0:w])



def emit_load_fm(c, dst_chunks, x_dram, rows, col_off=0, nch=NCH, dram_row0=0):
    nc = c.nc
    pp = PsumPool(nc)
    stage = c.sbuf("ld_stage", [128, 1024], F32)
    pst = pp.get([128, 128], F32)
    ident = c.sbuf("identf32", [128, 128], F32)
    ld, tr, ev = c.sem("ld_a"), c.sem("ld_b"), c.sem("ld_c")
    l0, t0, e0 = ld.n, tr.n, ev.n
    ntile = (rows + 127) // 128
    with nc.Block() as ph:
        @ph.sync
        def _(e):
            for tt in range(ntile):
                r = min(128, rows - tt * 128)
                if tt > 0:
                    e.wait_ge(ev.h, e0 + tt * nch)
                ld.inc(e.dma_start(
                    out=stage[0:r, 0:nch * 128],
                    in_=x_dram[dram_row0 + tt * 128: dram_row0 + tt * 128 + r, :]), 16)

        @ph.tensor
        def _(e):
            cnt = 0
            for tt in range(ntile):
                r = min(128, rows - tt * 128)
                e.wait_ge(ld.h, l0 + 16 * (tt + 1))
                for j in range(nch):
                    if cnt > 0:
                        e.wait_ge(ev.h, e0 + cnt)
                    tr.inc(e.transpose(pst[0:128, 0:r],
                                       stage[0:r, j * 128:(j + 1) * 128],
                                       ident[0:r, 0:r]), 1)
                    cnt += 1

        @ph.vector
        def _(e):
            cnt = 0
            for tt in range(ntile):
                r = min(128, rows - tt * 128)
                for j in range(nch):
                    cnt += 1
                    e.wait_ge(tr.h, t0 + cnt)
                    tile, base = dst_chunks[j]
                    ev.inc(e.tensor_copy(
                        tile[:, base + col_off + tt * 128: base + col_off + tt * 128 + r],
                        pst[0:128, 0:r]), 1)
    pp.close()


def emit_store_fm(c, src_chunks, x_dram, rows, col_off=0, final_wait=True,
                  nch=NCH, dt=F32):
    nc = c.nc
    pp = PsumPool(nc)
    stage = c.sbuf("ld_stage" if dt == F32 else "st_stageb",
                   [128, 1024], dt)
    pst = pp.get([128, 128], dt)
    ident = c.sbuf("identf32" if dt == F32 else "identbf", [128, 128], dt)
    tr, ev, st = c.sem("st_a"), c.sem("st_b"), c.sem("st_c")
    t0, e0, s0 = tr.n, ev.n, st.n
    ntile = (rows + 127) // 128
    with nc.Block() as ph:
        @ph.tensor
        def _(e):
            cnt = 0
            for tt in range(ntile):
                r = min(128, rows - tt * 128)
                if tt > 0:
                    e.wait_ge(st.h, s0 + 16 * tt)
                for j in range(nch):
                    if cnt > 0:
                        e.wait_ge(ev.h, e0 + cnt)
                    tile, base = src_chunks[j]
                    tr.inc(e.transpose(
                        pst[0:r, 0:128],
                        tile[:, base + col_off + tt * 128: base + col_off + tt * 128 + r],
                        ident[:, :]), 1)
                    cnt += 1

        @ph.vector
        def _(e):
            cnt = 0
            for tt in range(ntile):
                r = min(128, rows - tt * 128)
                for j in range(nch):
                    cnt += 1
                    e.wait_ge(tr.h, t0 + cnt)
                    ev.inc(e.tensor_copy(stage[0:r, j * 128:(j + 1) * 128],
                                         pst[0:r, 0:128]), 1)

        @ph.gpsimd
        def _(e):
            for tt in range(ntile):
                r = min(128, rows - tt * 128)
                e.wait_ge(ev.h, e0 + (tt + 1) * nch)
                st.inc(e.dma_start(out=x_dram[tt * 128: tt * 128 + r, :],
                                   in_=stage[0:r, 0:nch * 128]), 16)
            if final_wait:
                e.wait_ge(st.h, st.n)
    pp.close()


def emit_load_weight(c, w_dram, K, M, wb, wf_name=None, k0=0, m0=0,
                     dram_ap=None):
    nc = c.nc
    kch = (K + 127) // 128
    ld = c.sem("wl_a")
    src_ = dram_ap if dram_ap is not None else w_dram
    with nc.Block() as ph:
        @ph.gpsimd
        def _(e):
            for k in range(kch):
                r = min(128, K - k * 128)
                ld.inc(e.dma_start(out=wb[0:r, k0 + k, m0:m0 + M],
                                   in_=src_[k * 128:k * 128 + r, :]), 16)
            e.wait_ge(ld.h, ld.n)


def emit_load_small(c, dram_ap, shape, name, dtype=F32):
    nc = c.nc
    tile = c.sbuf(name, list(shape), dtype)
    ld = c.sem("small_ld")
    with nc.Block() as ph:
        @ph.gpsimd
        def _(e):
            ld.inc(e.dma_start(out=tile[tuple(slice(0, s) for s in shape)],
                               in_=dram_ap), 16)
            e.wait_ge(ld.h, ld.n)
    return tile



def emit_ln(c, src_chunks, XN, cols, gamma=None, beta=None, dst_off=0,
            shift_chunks=(), tagname="ln"):
    nc = c.nc
    pp = PsumPool(nc)
    ns_tot = (cols + 511) // 512
    rowbuf = c.sbuf("ln_row", [1, 2, 2, 512], F32)
    lnvar = c.sbuf("ln_var", [1, 2, 512], F32)
    scrB = c.sbuf("ln_scrB", [128, 512], F32)
    rowbf = c.sbuf("ln_rbf", [1, 2, 2, 512], BF16)
    scr2 = c.sbuf("ln_scr2", [1, 512], F32)
    sq = c.sbuf("ln_sq", [128, 2, 512], BF16)
    scr = c.sbuf("ln_scr", [128, 512], F32)
    ps_mu = pp.get([1, 512])
    ps_sq = pp.get([1, 512])
    ps_b1 = pp.get([128, 2, 512])
    ps_b2 = pp.get([128, 2, 512])
    ones_col = c.sbuf("ones_col_bf", [128, 1], BF16)
    ones_row = c.sbuf("ones_row_bf", [1, 128], BF16)
    sCast = c.sem("ln_cast")
    sSq = c.sem("ln_sq_s")
    sSt = c.sem("ln_stat")
    sV = c.sem("ln_rows")
    sA = c.sem("ln_rstd")
    sNu = c.sem("ln_nu")
    sBc = c.sem("ln_bc")
    sAp = c.sem("ln_apply")
    bCast, bSq, bSt = sCast.n, sSq.n, sSt.n
    bV, bA, bNu, bBc, bAp = sV.n, sA.n, sNu.n, sBc.n, sAp.n
    shift_chunks = set(shift_chunks)

    def src(j, lo, hi):
        tile, base = src_chunks[j]
        o = base - (1 if j in shift_chunks else 0)
        return tile[:, o + lo:o + hi]

    with nc.Block() as ph:
        @ph.scalar
        def _(e):
            for j in range(NCH):
                sCast.inc(e.activation(XN[:, j, dst_off:dst_off + cols],
                                       src(j, 0, cols), AF.Copy), 1)
            for ns in range(ns_tot):
                w = min(512, cols - ns * 512)
                e.wait_ge(sV.h, bV + ns + 1)
                e.activation(scr2[0:1, 0:w], lnvar[0:1, ns % 2, 0:w],
                             AF.Ln, bias=LN_EPS)
                sA.inc(e.activation(rowbuf[0:1, ns % 2, 1, 0:w], scr2[0:1, 0:w],
                                    AF.Exp, scale=-0.5), 1)

        @ph.gpsimd
        def _(e):
            e.wait_ge(sCast.h, bCast + NCH)
            for ns in range(ns_tot):
                w = min(512, cols - ns * 512)
                for j in range(NCH):
                    fi = ns * NCH + j
                    if fi >= 2:
                        e.wait_ge(sSt.h, bSt + fi - 1)
                    xsl = XN[:, j, dst_off + ns * 512:dst_off + ns * 512 + w]
                    sSq.inc(e.tensor_tensor(sq[:, fi % 2, 0:w], xsl, xsl,
                                            ALU.mult), 1)

        @ph.tensor
        def _(e):
            e.wait_ge(sCast.h, bCast + NCH)
            for ns in range(ns_tot):
                w = min(512, cols - ns * 512)
                if ns > 0:
                    e.wait_ge(sV.h, bV + ns)
                for j in range(NCH):
                    e.matmul(ps_mu[0:1, 0:w], ones_col[:, :],
                             XN[:, j, dst_off + ns * 512:dst_off + ns * 512 + w],
                             start=(j == 0), stop=(j == NCH - 1))
                for j in range(NCH):
                    fi = ns * NCH + j
                    e.wait_ge(sSq.h, bSq + fi + 1)
                    sSt.inc(e.matmul(ps_sq[0:1, 0:w], ones_col[:, :],
                                     sq[:, fi % 2, 0:w],
                                     start=(j == 0), stop=(j == NCH - 1)), 1)
            for ns in range(ns_tot):
                w = min(512, cols - ns * 512)
                e.wait_ge(sNu.h, bNu + ns + 1)
                if ns > 1:
                    e.wait_ge(sAp.h, bAp + (ns - 1) * NCH)
                e.matmul(ps_b1[:, ns % 2, 0:w], ones_row[:, :],
                         rowbf[0:1, ns % 2, 0, 0:w], start=True, stop=True)
                sBc.inc(e.matmul(ps_b2[:, ns % 2, 0:w], ones_row[:, :],
                                 rowbf[0:1, ns % 2, 1, 0:w],
                                 start=True, stop=True), 1)

        @ph.vector
        def _(e):
            for ns in range(ns_tot + 1):
                if ns < ns_tot:
                    w = min(512, cols - ns * 512)
                    e.wait_ge(sSt.h, bSt + (ns + 1) * NCH)
                    e.tensor_scalar(rowbuf[0:1, ns % 2, 0, 0:w], ps_mu[0:1, 0:w],
                                    1.0 / D, None, ALU.mult)
                    e.tensor_scalar(rowbuf[0:1, ns % 2, 1, 0:w], ps_sq[0:1, 0:w],
                                    1.0 / D, None, ALU.mult)
                    e.tensor_tensor(scr[0:1, 0:w], rowbuf[0:1, ns % 2, 0, 0:w],
                                    rowbuf[0:1, ns % 2, 0, 0:w], ALU.mult)
                    sV.inc(e.tensor_tensor(lnvar[0:1, ns % 2, 0:w],
                                           rowbuf[0:1, ns % 2, 1, 0:w],
                                           scr[0:1, 0:w], ALU.subtract), 1)
                if ns >= 1:
                    m = ns - 1
                    w = min(512, cols - m * 512)
                    e.wait_ge(sA.h, bA + m + 1)
                    if m >= 2:
                        e.wait_ge(sBc.h, bBc + m - 1)
                    e.tensor_tensor(scr[0:1, 0:w],
                                    rowbuf[0:1, m % 2, 0, 0:w],
                                    rowbuf[0:1, m % 2, 1, 0:w], ALU.mult)
                    e.tensor_copy(rowbf[0:1, m % 2, 0, 0:w],
                                  rowbuf[0:1, m % 2, 1, 0:w])
                    sNu.inc(e.tensor_copy(rowbf[0:1, m % 2, 1, 0:w],
                                          scr[0:1, 0:w]), 1)
            for ns in range(ns_tot):
                w = min(512, cols - ns * 512)
                e.wait_ge(sBc.h, bBc + ns + 1)
                for j in range(NCH):
                    dsl = XN[:, j, dst_off + ns * 512:dst_off + ns * 512 + w]
                    e.tensor_tensor(scr[:, 0:w], src(j, ns * 512, ns * 512 + w),
                                    ps_b1[:, ns % 2, 0:w], ALU.mult)
                    if gamma is None:
                        sAp.inc(e.tensor_tensor(dsl, scr[:, 0:w],
                                                ps_b2[:, ns % 2, 0:w],
                                                ALU.subtract), 1)
                    else:
                        e.tensor_tensor(scrB[:, 0:w], scr[:, 0:w],
                                        ps_b2[:, ns % 2, 0:w], ALU.subtract)
                        sAp.inc(e.tensor_scalar(dsl, scrB[:, 0:w], gamma(j), beta(j),
                                                ALU.mult, ALU.add), 1)
    pp.close()


def emit_ln1(c, src_col_fn, XN, dst_col, gamma=None, beta=None):
    nc = c.nc
    XP0 = c.sbuf("ln1_src", [128, NCH, 64], F32)
    XN0 = c.sbuf("ln1_dst", [128, NCH, 64], BF16)
    with nc.Block() as ph:
        @ph.vector
        def _(e):
            e.memset(XP0.rearrange("p a b -> p (a b)"), 1.0)
    with nc.Block() as ph:
        @ph.vector
        def _(e):
            for j in range(NCH):
                e.tensor_copy(XP0[:, j, 0:1], src_col_fn(j))
    emit_ln(c, [(XP0.rearrange("p a b -> p (a b)"), j * 64) for j in range(NCH)],
            XN0, 64, gamma=gamma, beta=beta)
    with nc.Block() as ph:
        @ph.vector
        def _(e):
            for j in range(NCH):
                e.tensor_copy(XN[:, j, dst_col:dst_col + 1], XN0[:, j, 0:1])



def mm_loop(c, ph, pp, wb, kch, m_list, mslice, cols, rhs_fn, evac, nbank=2,
            evac_eng="vector", sem_role="mm"):
    ns_tot = (cols + 511) // 512
    ps = pp.get([128, nbank, 512], F32)
    sm, se = c.sem(sem_role + "_m"), c.sem(sem_role + "_e")
    bm, be = sm.n, se.n

    def pe(e):
        g = 0
        for m in m_list:
            c0, mw = mslice(m)
            for ns in range(ns_tot):
                w = min(512, cols - ns * 512)
                if g >= nbank:
                    e.wait_ge(se.h, be + g - nbank + 1)
                for k in range(kch):
                    lhs = wb(k, m) if callable(wb) else wb[:, k, c0:c0 + mw]
                    mm = e.matmul(ps[0:mw, g % nbank, 0:w], lhs, rhs_fn(k, ns, w),
                                  start=(k == 0), stop=(k == kch - 1))
                sm.inc(mm, 1)
                g += 1

    def ve(e):
        g = 0
        for m in m_list:
            c0, mw = mslice(m)
            for ns in range(ns_tot):
                w = min(512, cols - ns * 512)
                e.wait_ge(sm.h, bm + g + 1)
                se.inc(evac(e, m, ns, w, ps[0:mw, g % nbank, 0:w]), 1)
                g += 1

    ph.tensor(pe)
    getattr(ph, evac_eng)(ve)



def emit_a0a(c, xb, io, stage=99, npairs=NCH, do_qkv=True, do_attn=True, do_wout=True, attn_pairs=None, attn_iters=32, attn_sub=9, attn_serial=False, attn_plain=False, attn_nomask=False, attn_nobias=False, attn_noden=False):
    nc = c.nc
    XN = c.sbuf("XN", [128, NCH, TC], BF16)
    VP = c.sbuf("a0a_v", [128, TC], BF16)
    QR = c.sbuf("a0a_qr", [128, TC], BF16)
    KR = c.sbuf("a0a_kr", [128, TC], BF16)
    VT = c.sbuf("a0a_vt", [128, 16, 128], BF16)
    AO = c.sbuf("a0a_ao", [128, NCH, TC], BF16)
    rtmp = c.sbuf("a0a_rt", [32, 2, 512], F32)
    scrS = c.sbuf("a0a_scrs", [128, 2, 256], F32)
    scrS2 = c.sbuf("a0a_scrs2", [128, 2, 256], F32)
    PN = c.sbuf("a0a_pn", [128, 2, 256], BF16)
    PN2 = c.sbuf("a0a_pn2", [128, 2, 256], BF16)
    PT = c.sbuf("a0a_pt", [128, 2, 2, 128], BF16)
    colbuf = c.sbuf("a0a_cols", [128, 2, 4], F32)
    den = c.sbuf("a0a_den", [128, 2, 1], F32)
    wpair = c.sbuf("a0a_wp", [128, NCH, 384], BF16)
    identb = c.sbuf("identbf", [128, 128], BF16)
    rc = io["t_ropec"]
    rs = io["t_ropes"]
    mask = io["t_mask"]

    emit_ln(c, xb.chunks(), XN, TC)
    if stage <= 1:
        return
    for p in range(npairs):
        emit_load_weight(c, None, 1024, 384, wpair, dram_ap=io["wqkvp"][p])
        if not do_qkv:
            continue
        pp = PsumPool(nc)
        with nc.Block() as ph:
            def evac(e, m, ns, w, ps):
                sl = slice(ns * 512, ns * 512 + w)
                if m == 2:
                    return e.tensor_copy(VP[:, sl], ps)
                out = QR if m == 0 else KR
                rcs = rc[:, sl]
                rss = rs[:, sl]
                for base in (0, 64):
                    evs = slice(base, base + 32)
                    ods = slice(base + 32, base + 64)
                    e.tensor_tensor(rtmp[:, 0, 0:w], ps[evs, :], rcs, ALU.mult)
                    e.tensor_tensor(rtmp[:, 1, 0:w], ps[ods, :], rss, ALU.mult)
                    e.tensor_tensor(out[evs, sl], rtmp[:, 0, 0:w], rtmp[:, 1, 0:w],
                                    ALU.subtract)
                    e.tensor_tensor(rtmp[:, 0, 0:w], ps[ods, :], rcs, ALU.mult)
                    e.tensor_tensor(rtmp[:, 1, 0:w], ps[evs, :], rss, ALU.mult)
                    last = e.tensor_tensor(out[ods, sl], rtmp[:, 0, 0:w],
                                           rtmp[:, 1, 0:w], ALU.add)
                return last
            mm_loop(c, ph, pp, wpair, NCH, [0, 1, 2],
                    lambda m: (m * 128, 128), TC,
                    lambda k, ns, w: XN[:, k, ns * 512:ns * 512 + w],
                    evac, sem_role="a0a_qkv")
        pp.close()
        if not do_attn or (attn_pairs is not None and p not in attn_pairs):
            continue

        pp = PsumPool(nc)
        pstv = pp.get([128, 2, 1024], BF16)
        psS = pp.get([128, 2, 512], F32)
        pstp = pp.get([128, 2, 1024], BF16)
        psO = pp.get([128, 2, 512], F32)
        vt_s, vt_e = c.sem("a0a_vts"), c.sem("a0a_vte")
        sS, sM, sE, sN = c.sem("a0a_sS"), c.sem("a0a_sM"), c.sem("a0a_sE"), c.sem("a0a_sN")
        sPT, sPTe, sO, sOe = c.sem("a0a_sPT"), c.sem("a0a_sPTe"), c.sem("a0a_sO"), c.sem("a0a_sOe")
        bvt, bve = vt_s.n, vt_e.n
        bS, bM, bE, bN = sS.n, sM.n, sE.n, sN.n
        bPT, bPTe, bO, bOe = sPT.n, sPTe.n, sO.n, sOe.n
        NIT = attn_iters

        def it(i):
            h, r = divmod(i, 16)
            b, qc = divmod(r, 2)
            return h, b, qc

        with nc.Block() as ph:
            @ph.tensor
            def _(e):
                for tb in range(16):
                    if tb > 1:
                        e.wait_ge(vt_e.h, bve + tb - 1)
                    vt_s.inc(e.transpose(pstv[:, tb % 2, 0:128],
                                         VP[:, tb * 128:(tb + 1) * 128],
                                         identb[:, :]), 1)
                for i in range(NIT + 1):
                    if i < NIT:
                        h, b, qc = it(i)
                        hs = slice(h * 64, h * 64 + 64)
                        q0 = b * 256 + qc * 128
                        if i >= 2:
                            e.wait_ge(sM.h, bM + i - 1)
                        if i == 0:
                            qe = c.sem("a0a_qkv_e")
                            e.wait_ge(qe.h, qe.n)
                        sS.inc(e.matmul(psS[:, i % 2, 0:256], QR[hs, q0:q0 + 128],
                                        KR[hs, b * 256:b * 256 + 256],
                                        start=True, stop=True), 1)
                    if i >= 1:
                        j = i - 1
                        h, b, qc = it(j)
                        e.wait_ge(sN.h, bN + j + 1)
                        if j >= 1:
                            e.wait_ge(sPTe.h, bPTe + 2 * j)
                        for kc in range(2):
                            sPT.inc(e.transpose(pstp[:, kc, 0:128],
                                                PN2[:, j % 2, kc * 128:kc * 128 + 128],
                                                identb[:, :]), 1)
                        e.wait_ge(sPTe.h, bPTe + 2 * j + 2)
                        if j >= 2:
                            e.wait_ge(sOe.h, bOe + j - 1)
                        for kc in range(2):
                            mm = e.matmul(psO[0:64, j % 2, 0:128],
                                          VT[:, 2 * b + kc, h * 64:h * 64 + 64],
                                          PT[:, j % 2, kc, :],
                                          start=(kc == 0), stop=(kc == 1))
                        sO.inc(mm, 1)

            @ph.vector
            def _(e):
                for tb in range(16):
                    e.wait_ge(vt_s.h, bvt + tb + 1)
                    vt_e.inc(e.tensor_copy(VT[:, tb, :], pstv[:, tb % 2, 0:128]), 1)
                for i in range(NIT + 1):
                    if i < NIT:
                        h, b, qc = it(i)
                        e.wait_ge(sS.h, bS + i + 1)
                        if i >= 2:
                            e.wait_ge(sE.h, bE + i - 1)
                        e.tensor_tensor(scrS[:, 0, :], psS[:, i % 2, 0:256],
                                        mask[:, qc, :], ALU.add)
                        e.reduce_max(colbuf[:, i % 2, 0:1], scrS[:, 0, :],
                                     axis=mybir.AxisListType.X)
                    if i >= 1:
                        e.wait_ge(sE.h, bE + i)
                        e.reciprocal(colbuf[:, (i - 1) % 2, 2:3],
                                     den[:, (i - 1) % 2, :])
                    else:
                        e.tensor_copy(scrS[:, 1, 0:64], scrS[:, 0, 0:64])
                    if i < NIT:
                        sM.inc(e.tensor_scalar(scrS2[:, i % 2, :], scrS[:, 0, :],
                                               colbuf[:, i % 2, 0:1], None,
                                               ALU.subtract), 1)
                    else:
                        e.tensor_copy(scrS[:, 1, 0:64], scrS[:, 0, 0:64])
                    if i >= 1:
                        j = i - 1
                        h, b, qc = it(j)
                        sN.inc(e.tensor_scalar(PN2[:, j % 2, :], PN[:, j % 2, :],
                                               colbuf[:, j % 2, 2:3], None,
                                               ALU.mult), 1)
                        if j >= 2:
                            e.wait_ge(sO.h, bO + j - 1)
                        for kc in range(2):
                            e.wait_ge(sPT.h, bPT + 2 * j + kc + 1)
                            sPTe.inc(e.tensor_copy(PT[:, j % 2, kc, :],
                                                   pstp[:, kc, 0:128]), 1)
                        e.wait_ge(sO.h, bO + j + 1)
                        sOe.inc(e.tensor_copy(
                            AO[h * 64:h * 64 + 64, p,
                               b * 256 + qc * 128:b * 256 + qc * 128 + 128],
                            psO[0:64, j % 2, 0:128]), 1)

            @ph.scalar
            def _(e):
                for i in range(NIT):
                    e.wait_ge(sM.h, bM + i + 1)
                    if i >= 2:
                        e.wait_ge(sPT.h, bPT + 2 * (i - 1))
                        e.wait_ge(sN.h, bN + i - 1)
                    sE.inc(e.activation(PN[:, i % 2, :], scrS2[:, i % 2, :],
                                        AF.Exp, scale=0.125,
                                        accum_out=den[:, i % 2, :]), 1)
        pp.close()

    if not do_wout:
        return
    for m in range(NCH):
        emit_load_weight(c, None, 1024, 128, wpair,
                         dram_ap=io["woutT"][:, m * 128:(m + 1) * 128])
        pp = PsumPool(nc)
        with nc.Block() as ph:
            def evac(e, m_, ns, w, ps, m=m):
                return xb.add_evac(e, m, slice(ns * 512, ns * 512 + w), ps)
            mm_loop(c, ph, pp, wpair, NCH, [0], lambda m_: (0, 128), TC,
                    lambda k, ns, w: AO[:, k, ns * 512:ns * 512 + w],
                    evac, sem_role="a0a_wo")
        pp.close()



def emit_f0(c, xb, io, xprev_dram):
    nc = c.nc
    XNE = c.sbuf("f0_A", [128, NCH, TC + 1], BF16)
    XR = c.sbuf("f0_B", [128, NCH, TC], BF16)
    XK = c.sbuf("f0_C", [128, NCH, TC], BF16)
    RS = XNE
    KA = XR
    sscr = c.sbuf("f0_scr", [128, 512], F32)
    prevX = c.sbuf("f0_px", [128, NCH], F32)
    mucols = emit_load_small(c, io["mu_rk"], [128, 2 * NCH], "f0_mu")
    om = c.sbuf("f0_om", [128, 2 * NCH], F32)
    wbig = c.sbuf("f0_w", [128, NCH, 128], BF16)

    with nc.Block() as ph:
        @ph.vector
        def _(e):
            e.tensor_scalar(om[:, :], mucols[:, :], -1.0, 1.0, ALU.mult, ALU.add)

    emit_load_fm(c, [(prevX, j) for j in range(NCH)], xprev_dram, 1)
    emit_ln1(c, lambda j: prevX[:, j:j + 1], XNE, 0)
    emit_ln(c, xb.chunks(), XNE, TC, dst_off=1)

    with nc.Block() as ph:
        @ph.vector
        def _(e):
            for j in range(NCH):
                for half, (dst, mc, oc) in enumerate(
                        ((XR, mucols[:, j:j + 1], om[:, j:j + 1]),
                         (XK, mucols[:, NCH + j:NCH + j + 1], om[:, NCH + j:NCH + j + 1]))):
                    for ns in range((TC + 511) // 512):
                        w = min(512, TC - ns * 512)
                        sl = slice(ns * 512, ns * 512 + w)
                        e.tensor_scalar(sscr[:, 0:w], XNE[:, j, 1 + ns * 512:1 + ns * 512 + w],
                                        mc, None, ALU.mult)
                        e.scalar_tensor_tensor(dst[:, j, sl], XNE[:, j, ns * 512:ns * 512 + w],
                                               oc, sscr[:, 0:w], ALU.mult, ALU.add)

    for m in range(NCH):
        emit_load_weight(c, None, 1024, 128, wbig,
                         dram_ap=io["wrT"][:, m * 128:(m + 1) * 128])
        pp = PsumPool(nc)
        with nc.Block() as ph:
            mm_loop(c, ph, pp, wbig, NCH, [m], lambda m_: (0, 128),
                    TC, lambda k, ns, w: XR[:, k, ns * 512:ns * 512 + w],
                    lambda e, m_, ns, w, ps, m=m: e.activation(
                        RS[:, m, ns * 512:ns * 512 + w], ps, AF.Sigmoid),
                    evac_eng="scalar", sem_role="f0_r")
        pp.close()
    for m in range(NCH):
        emit_load_weight(c, None, 1024, 128, wbig,
                         dram_ap=io["wkT"][:, m * 128:(m + 1) * 128])
        pp = PsumPool(nc)
        with nc.Block() as ph:
            def evac(e, m_, ns, w, ps, m=m):
                e.activation(sscr[:, 0:w], ps, AF.Relu)
                return e.activation(KA[:, m, ns * 512:ns * 512 + w], sscr[:, 0:w],
                                    AF.Square)
            mm_loop(c, ph, pp, wbig, NCH, [m], lambda m_: (0, 128),
                    TC, lambda k, ns, w: XK[:, k, ns * 512:ns * 512 + w],
                    evac, evac_eng="scalar", sem_role="f0_k")
        pp.close()
    for m in range(NCH):
        emit_load_weight(c, None, 1024, 128, wbig,
                         dram_ap=io["wvT"][:, m * 128:(m + 1) * 128])
        pp = PsumPool(nc)
        with nc.Block() as ph:
            def evac(e, m_, ns, w, ps, m=m):
                sl = slice(ns * 512, ns * 512 + w)
                e.tensor_tensor(sscr[:, 0:w], ps, RS[:, m, sl], ALU.mult)
                e.tensor_tensor(xb.scr[:, 0:w], sscr[:, 0:w], xb.cur(m)[:, sl],
                                ALU.add)
                return e.tensor_copy(xb.cur(m)[:, sl], xb.scr[:, 0:w])
            mm_loop(c, ph, pp, wbig, NCH, [m], lambda m_: (0, 128),
                    TC, lambda k, ns, w: KA[:, k, ns * 512:ns * 512 + w],
                    evac, sem_role="f0_v")
        pp.close()



def _host_consts():
    return {"identf": np.eye(128, dtype=np.float32)}


def _rope_tables_a0a():
    inv = (1.0 / 10000.0 ** (np.arange(0, 64, 2, dtype=np.float32) / 64.0)
           ).astype(np.float32)
    t = np.arange(T, dtype=np.float32)
    ang = t[:, None] * inv[None, :]
    return f32c(np.cos(ang).T), f32c(np.sin(ang).T)


def _mask_a0a():
    m = np.zeros((128, 2, 256), np.float32)
    for qc in range(2):
        q = qc * 128 + np.arange(128)[:, None]
        k = np.arange(256)[None, :]
        m[:, qc, :] = np.where(k <= q, 0.0, NEG)
    return f32c(m.reshape(128, 512))


def _perm_evenodd(n=64):
    return np.concatenate([np.arange(0, n, 2), np.arange(1, n, 2)])


def prep_weights(inp):
    w = {}
    Wqkv = np.asarray(inp["L0_attn_a_Wqkv"])
    perm = _perm_evenodd()
    rows = []
    for sect in range(3):
        s = Wqkv[sect * 1024:(sect + 1) * 1024]
        if sect < 2:
            s = s.reshape(16, 64, 1024)[:, perm, :].reshape(1024, 1024)
        rows.append(s)
    WqkvP = np.concatenate(rows, axis=0)
    WqkvT = WqkvP.T
    pairs = [np.concatenate([WqkvT[:, 128 * p:128 * p + 128],
                             WqkvT[:, 1024 + 128 * p:1024 + 128 * p + 128],
                             WqkvT[:, 2048 + 128 * p:2048 + 128 * p + 128]], axis=1)
             for p in range(8)]
    w["a0a_wqkvp"] = f32c(np.stack(pairs))
    w["a0a_woutT"] = f32c(np.asarray(inp["L0_attn_a_Wout"]).T)
    rc, rs = _rope_tables_a0a()
    w["a0a_ropec"], w["a0a_ropes"] = rc, rs
    w["a0a_mask"] = _mask_a0a()
    w["f0_wrT"] = f32c(np.asarray(inp["L0_ffn_W_r"]).T)
    w["f0_wkT"] = f32c(np.asarray(inp["L0_ffn_W_k"]).T)
    w["f0_wvT"] = f32c(np.asarray(inp["L0_ffn_W_v"]).T)
    w["f0_mu"] = f32c(np.concatenate([col_layout(inp["L0_ffn_mu_r"]),
                                      col_layout(inp["L0_ffn_mu_k"])], axis=1))
    w["b_wqT"] = f32c(np.asarray(inp["L0_attn_b_Wq"]).T)
    w["b_wkT"] = f32c(np.asarray(inp["L0_attn_b_Wk"]).T)
    w["b_wvT"] = f32c(np.asarray(inp["L0_attn_b_Wv"]).T)
    w["b_woT"] = f32c(np.asarray(inp["L0_attn_b_Wo"]).T)
    w["b_bq"] = col_layout(inp["L0_attn_b_bq"])
    w["b_bkv"] = f32c(np.concatenate([col_layout(inp["L0_attn_b_bk"]),
                                      col_layout(inp["L0_attn_b_bv"])], axis=1))
    w["b_bogb"] = f32c(np.concatenate([col_layout(inp["L0_attn_b_bo"]),
                                       col_layout(inp["L0_attn_b_ln_gamma"]),
                                       col_layout(inp["L0_attn_b_ln_beta"])],
                                      axis=1))
    return w


def _din(nc, dr, name, shape, dtype=F32):
    t = nc.dram_tensor(name, list(shape), dtype, kind="ExternalInput").ap()
    dr[name] = t
    return t


FLOAT_CONSTS = [LN_EPS, 1.0, 1e-24]


def emit_setup(c, dr):
    nc = c.nc
    idf = emit_load_small(c, dr["identf"], [128, 128], "identf32")
    idb = c.sbuf("identbf", [128, 128], BF16)
    oc = c.sbuf("ones_col_bf", [128, 1], BF16)
    orow = c.sbuf("ones_row_bf", [1, 128], BF16)
    cpool = c.sbuf("constpool", [128, len(FLOAT_CONSTS)], F32)
    with nc.Block() as ph:
        @ph.vector
        def _(e):
            e.tensor_copy(idb[:, :], idf[:, :])
            e.memset(oc[:, :], 1.0)
            e.memset(orow[:, :], 1.0)
            for i, v in enumerate(FLOAT_CONSTS):
                e.memset(cpool[:, i:i + 1], v)
    for i, v in enumerate(FLOAT_CONSTS):
        nc.const_aps.register(F32, v, cpool[:, i:i + 1])


def emit_debug_dump(c, tiles):
    nc = c.nc
    st = c.sem("dbg_st")
    with nc.Block() as ph:
        @ph.gpsimd
        def _(e):
            for dram_ap, sb_ap in tiles:
                st.inc(e.dma_start(out=dram_ap, in_=sb_ap), 16)
            e.wait_ge(st.h, st.n)


def build_a0a(stage=99, debug=False, **kw):
    nc = bass.Bass("TRN2", target_bir_lowering=False)
    c = Ctx(nc)
    dr = {}
    _din(nc, dr, "x", (TC, D))
    _din(nc, dr, "wqkvp", (8, 1024, 384))
    _din(nc, dr, "woutT", (1024, 1024))
    _din(nc, dr, "ropec", (32, TC))
    _din(nc, dr, "ropes", (32, TC))
    _din(nc, dr, "mask", (128, 512))
    _din(nc, dr, "identf", (128, 128))
    y = nc.dram_tensor("y", [TC, D], F32, kind="ExternalOutput").ap()
    emit_setup(c, dr)
    tc_ = emit_load_small(c, dr["ropec"], [32, TC], "t_ropec", BF16)
    ts_ = emit_load_small(c, dr["ropes"], [32, TC], "t_ropes", BF16)
    tm = emit_load_small(c, dr["mask"], [128, 2, 256], "t_mask")
    xb = XBufs(c, TC)
    emit_load_fm(c, xb.chunks(), dr["x"], TC)
    io = {"wqkvp": dr["wqkvp"], "woutT": dr["woutT"],
          "t_ropec": tc_[:, :], "t_ropes": ts_[:, :], "t_mask": tm}
    emit_a0a(c, xb, io, stage=stage, **kw)
    if debug:
        dbg = {}
        for nm, shape in (("d_xn", (128, NCH * TC)), ("d_qr", (128, TC)),
                          ("d_kr", (128, TC)), ("d_vp", (128, TC)),
                          ("d_ao", (128, NCH * TC))):
            dbg[nm] = nc.dram_tensor(nm, list(shape), F32, kind="ExternalOutput").ap()
        t = c._tiles
        emit_debug_dump(c, [
            (dbg["d_xn"], t["XN"].rearrange("p a b -> p (a b)")),
            (dbg["d_qr"], t["a0a_qr"][:, :]),
            (dbg["d_kr"], t["a0a_kr"][:, :]),
            (dbg["d_vp"], t["a0a_v"][:, :]),
            (dbg["d_ao"], t["a0a_ao"].rearrange("p a b -> p (a b)")),
        ])
    emit_store_fm(c, xb.chunks(), y, TC)
    c.close()
    return nc


def build_f0():
    nc = bass.Bass("TRN2", target_bir_lowering=False)
    c = Ctx(nc)
    dr = {}
    _din(nc, dr, "x", (TC, D))
    _din(nc, dr, "xprev", (1, D))
    _din(nc, dr, "wrT", (1024, 1024))
    _din(nc, dr, "wkT", (1024, 1024))
    _din(nc, dr, "wvT", (1024, 1024))
    _din(nc, dr, "mu_rk", (128, 2 * NCH))
    _din(nc, dr, "identf", (128, 128))
    y = nc.dram_tensor("y", [TC, D], F32, kind="ExternalOutput").ap()
    emit_setup(c, dr)
    xb = XBufs(c, TC)
    emit_load_fm(c, xb.chunks(), dr["x"], TC)
    io = {"wrT": dr["wrT"], "wkT": dr["wkT"], "wvT": dr["wvT"],
          "mu_rk": dr["mu_rk"]}
    emit_f0(c, xb, io, dr["xprev"])
    emit_store_fm(c, xb.chunks(), y, TC)
    c.close()
    return nc


def shard_x(x_full):
    out = []
    for cidx in range(NCORES):
        s, q = divmod(cidx, CPS)
        out.append(f32c(x_full[s, q * TC:(q + 1) * TC]))
    return out


def unshard_x(chunks):
    x = np.empty((B, T, D), np.float32)
    for cidx in range(NCORES):
        s, q = divmod(cidx, CPS)
        x[s, q * TC:(q + 1) * TC] = chunks[cidx]
    return x


def prev_rows(x_chunks, nrows=1):
    outs = []
    for cidx in range(NCORES):
        s, q = divmod(cidx, CPS)
        if q == 0:
            outs.append(np.zeros((nrows, D), np.float32))
        else:
            outs.append(f32c(x_chunks[cidx - 1][-nrows:]))
    return outs


_CONSTS = _host_consts()


def run_a0a(x_chunks, w):
    nc = _nc_cache('a0a', build_a0a)
    maps = []
    for cidx in range(NCORES):
        q = cidx % CPS
        m = {"x": x_chunks[cidx], "wqkvp": w["a0a_wqkvp"],
             "woutT": w["a0a_woutT"],
             "ropec": f32c(w["a0a_ropec"][:, q * TC:(q + 1) * TC]),
             "ropes": f32c(w["a0a_ropes"][:, q * TC:(q + 1) * TC]),
             "mask": w["a0a_mask"]}
        m.update(_CONSTS)
        maps.append(m)
    res = run_bass_kernel_spmd(nc, maps, core_ids=list(range(NCORES)))
    return [res.results[i]["y"] for i in range(NCORES)]


def run_f0(x_chunks, w):
    nc = _nc_cache('f0', build_f0)
    prevs = prev_rows(x_chunks, 1)
    maps = []
    for cidx in range(NCORES):
        m = {"x": x_chunks[cidx], "xprev": prevs[cidx], "wrT": w["f0_wrT"],
             "wkT": w["f0_wkT"], "wvT": w["f0_wvT"], "mu_rk": w["f0_mu"]}
        m.update(_CONSTS)
        maps.append(m)
    res = run_bass_kernel_spmd(nc, maps, core_ids=list(range(NCORES)))
    return [res.results[i]["y"] for i in range(NCORES)]


def kernel(**inputs):
    inputs = {k: np.asarray(v) for k, v in inputs.items()}
    w = prep_weights(inputs)
    w.update(prep_f1(inputs))
    w.update(prep_a1a(inputs))
    w.update(prep_a1b(inputs))
    xc = shard_x(inputs["L0_attn_a_x"])
    xc = run_a0a(xc, w)
    xc = run_f0(xc, w)
    xc = run_a0b(xc, w)
    xc = run_f0(xc, w)
    xc = run_a1a(xc, w)

    def f1_block(xc):
        zc = run_ln(xc)
        for layer in range(6):
            zc = run_f1_attn(zc, w, layer)
            zc = run_f1_ffn(zc, w, layer)
        return run_add(xc, zc)
    xc = f1_block(xc)
    xc = run_a1b(xc, w)
    xc = f1_block(xc)
    return unshard_x(xc)


F1H = 257
F1E = TC + F1H
F1K = TC + 256


def emit_f1_attn(c, xb, io):
    nc = c.nc
    XBE = c.sbuf("f1_xbe", [128, NCH, F1E], BF16)
    QP = c.sbuf("f1_qp", [128, F1K], BF16)
    KP = c.sbuf("f1_kp", [128, F1K], BF16)
    QR = QP
    KR = KP
    VP = c.sbuf("f1_vp", [128, F1K], BF16)
    VT = c.sbuf("f1_vt", [128, 18, 2, 65], BF16)
    AO = c.sbuf("f1_ao", [128, NCH, TC], BF16)
    wpair = c.sbuf("f1_wp", [128, NCH, 384], BF16)
    sqb = c.sbuf("f1_sqb", [64, 2, 512], BF16)
    scrr = c.sbuf("f1_scrr", [64, 2, 512], BF16)
    scrq2 = c.sbuf("f1_scrq2", [64, 2, 512], BF16)
    rows = c.sbuf("f1_rows", [1, 3, 512], F32)
    rowbf = c.sbuf("f1_rbf", [1, 2, 512], BF16)
    drow = c.sbuf("f1_drow", [1, 2, 256], F32)
    dbf = c.sbuf("f1_dbf", [1, 2, 256], BF16)
    scrS = c.sbuf("f1_scrs", [128, 2, 256], F32)
    PN = c.sbuf("f1_pn", [128, 2, 256], BF16)
    obuf = c.sbuf("f1_obuf", [64, 2, 256], BF16)
    negM = c.sbuf("f1_negm", [128, 1], F32)
    mrow = c.sbuf("f1_mrow", [1, 8], F32)
    mrowW = c.sbuf("f1_mroww", [1, 64], F32)
    mrow2 = c.sbuf("f1_mrow2", [64, 2], F32)
    identf = c.sbuf("identf32", [128, 128], F32)
    mbf = c.sbuf("f1_mbf", [1, 1], BF16)
    tq = io["t_xq"]
    tk = io["t_xk"]
    qscol = io["t_qs"]
    mask = io["t_fmask"]
    identb = c.sbuf("identbf", [128, 128], BF16)
    ones_row = c.sbuf("ones_row_bf", [1, 128], BF16)
    ones_col = c.sbuf("ones_col_bf", [128, 1], BF16)

    pp = PsumPool(nc)
    psm = pp.get([1, 512])
    psb = pp.get([128, 512])
    with nc.Block() as ph:
        @ph.vector
        def _(e):
            e.tensor_tensor(mrow2[:, 0:1], qscol[:, 0:1], qscol[:, 1:2], ALU.mult)
    with nc.Block() as ph:
        @ph.scalar
        def _(e):
            e.activation(mrow2[:, 1:2], mrow2[:, 0:1], AF.Abs)
    with nc.Block() as ph:
        @ph.tensor
        def _(e):
            e.transpose(psm[0:1, 0:64], mrow2[0:64, 1:2], identf[0:64, 0:64])
    with nc.Block() as ph:
        @ph.vector
        def _(e):
            e.tensor_copy(mrowW[0:1, 0:64], psm[0:1, 0:64])
    with nc.Block() as ph:
        @ph.vector
        def _(e):
            e.reduce_max(mrow[0:1, 0:1], mrowW[0:1, 0:64],
                         axis=mybir.AxisListType.X)
    with nc.Block() as ph:
        @ph.vector
        def _(e):
            e.tensor_scalar(mrow[0:1, 1:2], mrow[0:1, 0:1], -1.0, None, ALU.mult)
    with nc.Block() as ph:
        @ph.vector
        def _(e):
            e.tensor_copy(mbf[0:1, 0:1], mrow[0:1, 1:2])
    with nc.Block() as ph:
        @ph.tensor
        def _(e):
            e.matmul(psb[:, 0:1], ones_row[0:1, :], mbf[0:1, 0:1],
                     start=True, stop=True)
    with nc.Block() as ph:
        @ph.vector
        def _(e):
            e.tensor_copy(negM[:, :], psb[:, 0:1])
    pp.close()

    emit_load_fm(c, [(XBE.rearrange("p a b -> p (a b)"), j * F1E)
                     for j in range(NCH)], io["xprev"], F1H)
    with nc.Block() as ph:
        @ph.scalar
        def _(e):
            for j in range(NCH):
                e.activation(XBE[:, j, F1H:F1E], xb.cur(j)[:, :], AF.Copy)

    NW = TC // 256
    for p in range(NCH):
        emit_load_weight(c, None, 1024, 384, wpair, dram_ap=io["wqkvp"][p])
        pp = PsumPool(nc)
        with nc.Block() as ph:
            specs = [(0, QP, F1H, TC, 256), (1, KP, 1, F1K, 0), (2, VP, 1, F1K, 0)]
            for sect, dst, off, cols, dof in specs:
                def rhs(k, ns, w, off=off):
                    sh = -1 if k >= 4 else 0
                    return XBE[:, k, off + sh + ns * 512: off + sh + ns * 512 + w]
                mm_loop(c, ph, pp, wpair, NCH, [sect],
                        lambda m, sect=sect: (sect * 128, 128), cols, rhs,
                        lambda e, m, ns, w, ps, dst=dst, dof=dof:
                            e.tensor_copy(dst[:, dof + ns * 512:dof + ns * 512 + w], ps),
                        sem_role="f1_qkv")
        pp.close()

        pp = PsumPool(nc)
        psn = pp.get([1, 2, 512])
        psb2 = pp.get([64, 2, 512])
        sSQ, sNM, sR0, sRS = (c.sem(f"f1l{k}") for k in "abcd")
        sBF, sBC, sAP = (c.sem(f"f1l{k}") for k in "efg")
        bSQ, bNM, bR0, bRS = sSQ.n, sNM.n, sR0.n, sRS.n
        bBF, bBC, bAP = sBF.n, sBC.n, sAP.n
        jobs = []
        for h in range(2):
            for srcT, dstT, tab, qi, c0_, cend in (
                    (QP, QR, tq, 0, 256, F1K), (KP, KR, tk, 1, 0, F1K)):
                cc = c0_
                while cc < cend:
                    w = min(512, cend - cc)
                    jobs.append((srcT, dstT, tab, qi, h, cc, w))
                    cc += w
        NJ = len(jobs)
        with nc.Block() as ph:
            @ph.scalar
            def _(e):
                for ji in range(NJ + 2):
                    if ji < NJ:
                        srcT, dstT, tab, qi, h, c0_, w = jobs[ji]
                        if ji >= 2:
                            e.wait_ge(sNM.h, bNM + ji - 1)
                        sSQ.inc(e.activation(
                            sqb[:, ji % 2, 0:w],
                            srcT[h * 64:h * 64 + 64, c0_:c0_ + w],
                            AF.Square), 1)
                    if ji >= 2:
                        jj = ji - 2
                        srcT, dstT, tab, qi, h, c0_, w = jobs[jj]
                        e.wait_ge(sR0.h, bR0 + jj + 1)
                        if jj >= 2:
                            e.wait_ge(sBF.h, bBF + jj - 1)
                        e.activation(rows[0:1, 2, 0:w], rows[0:1, jj % 2, 0:w],
                                     AF.Ln, bias=1e-24)
                        sRS.inc(e.activation(rows[0:1, jj % 2, 0:w],
                                             rows[0:1, 2, 0:w], AF.Exp,
                                             scale=-0.5), 1)

            @ph.tensor
            def _(e):
                for ji in range(NJ + 4):
                    if ji < NJ:
                        srcT, dstT, tab, qi, h, c0_, w = jobs[ji]
                        e.wait_ge(sSQ.h, bSQ + ji + 1)
                        if ji >= 2:
                            e.wait_ge(sR0.h, bR0 + ji - 1)
                        sNM.inc(e.matmul(psn[0:1, ji % 2, 0:w], ones_col[0:64, :],
                                         sqb[:, ji % 2, 0:w],
                                         start=True, stop=True), 1)
                    if ji >= 4:
                        jj = ji - 4
                        srcT, dstT, tab, qi, h, c0_, w = jobs[jj]
                        e.wait_ge(sBF.h, bBF + jj + 1)
                        if jj >= 2:
                            e.wait_ge(sAP.h, bAP + jj - 1)
                        sBC.inc(e.matmul(psb2[:, jj % 2, 0:w], ones_row[0:1, 0:64],
                                         rowbf[0:1, jj % 2, 0:w],
                                         start=True, stop=True), 1)

            @ph.vector
            def _(e):
                for ji in range(NJ + 3):
                    if ji >= 2 and ji - 2 < NJ:
                        jj = ji - 2
                        srcT, dstT, tab, qi, h, c0_, w = jobs[jj]
                        e.wait_ge(sRS.h, bRS + jj + 1)
                        if jj >= 2:
                            e.wait_ge(sBC.h, bBC + jj - 1)
                        sBF.inc(e.tensor_copy(rowbf[0:1, jj % 2, 0:w],
                                              rows[0:1, jj % 2, 0:w]), 1)
                    if ji < NJ:
                        srcT, dstT, tab, qi, h, c0_, w = jobs[ji]
                        e.wait_ge(sNM.h, bNM + ji + 1)
                        if ji >= 2:
                            e.wait_ge(sRS.h, bRS + ji - 1)
                        sR0.inc(e.tensor_copy(rows[0:1, ji % 2, 0:w],
                                              psn[0:1, ji % 2, 0:w]), 1)
                    if ji >= 3:
                        jj = ji - 3
                        srcT, dstT, tab, qi, h, c0_, w = jobs[jj]
                        hs = slice(h * 64, h * 64 + 64)
                        sl = slice(c0_, c0_ + w)
                        e.wait_ge(sBC.h, bBC + jj + 1)
                        e.tensor_tensor(scrq2[:, 0, 0:w], srcT[hs, sl],
                                        psb2[:, jj % 2, 0:w], ALU.mult)
                        e.tensor_scalar(scrr[:, 1, 0:w], scrq2[:, 0, 0:w],
                                        qscol[:, qi:qi + 1], None, ALU.mult)
                        qn = scrr[:, 1, 0:w]
                        e.tensor_copy(scrr[0:32, 0, 0:w], qn[32:64, :])
                        e.tensor_copy(scrr[32:64, 0, 0:w], qn[0:32, :])
                        e.tensor_tensor(scrq2[:, 0, 0:w], qn, tab[:, 0, sl],
                                        ALU.mult)
                        e.tensor_tensor(scrq2[:, 1, 0:w], scrr[:, 0, 0:w],
                                        tab[:, 1, sl], ALU.mult)
                        sAP.inc(e.tensor_tensor(dstT[hs, sl], scrq2[:, 0, 0:w],
                                                scrq2[:, 1, 0:w], ALU.add), 1)
        pp.close()

        pp = PsumPool(nc)
        pstv = pp.get([128, 2, 1024], BF16)
        vs, ve = c.sem("f1_vts"), c.sem("f1_vte")
        bvs, bve = vs.n, ve.n
        with nc.Block() as ph:
            @ph.vector
            def _(e):
                if p == 0:
                    e.memset(VT[:, :, :, 64:65], 1.0)
                for tb in range(18):
                    e.wait_ge(vs.h, bvs + tb + 1)
                    e.tensor_copy(VT[:, tb, 0, 0:64], pstv[:, tb % 2, 0:64])
                    ve.inc(e.tensor_copy(VT[:, tb, 1, 0:64],
                                         pstv[:, tb % 2, 64:128]), 1)

            @ph.tensor
            def _(e):
                for tb in range(18):
                    if tb > 1:
                        e.wait_ge(ve.h, bve + tb - 1)
                    vs.inc(e.transpose(pstv[:, tb % 2, 0:128],
                                       VP[:, tb * 128:(tb + 1) * 128],
                                       identb[:, :]), 1)
        pp.close()

        pp = PsumPool(nc)
        psS = pp.get([128, 2, 512], F32)
        psO = pp.get([128, 2, 512], F32)
        psD = pp.get([128, 2, 512], F32)
        fS, fM, fE, fO = (c.sem(f"f1a{k}") for k in "abcd")
        fD, fB, fV = (c.sem(f"f1a{k}") for k in "efg")
        bfS, bfM, bfE, bfO = fS.n, fM.n, fE.n, fO.n
        bfD, bfB, bfV = fD.n, fB.n, fV.n
        NHW = 2 * NW

        def hw(it):
            return divmod(it, NW)

        with nc.Block() as ph:
            @ph.tensor
            def _(e):
                for it in range(NHW + 1):
                    if it >= 1:
                        jt = it - 1
                        e.wait_ge(fD.h, bfD + jt + 1)
                        if jt >= 2:
                            e.wait_ge(fV.h, bfV + jt - 1)
                        fB.inc(e.matmul(psD[0:64, jt % 2, 0:256],
                                        ones_row[0:1, 0:64], dbf[0:1, jt % 2, :],
                                        start=True, stop=True), 1)
                    if it < NHW:
                        h, w_ = hw(it)
                        hs = slice(h * 64, h * 64 + 64)
                        q0 = 256 + 256 * w_
                        k0 = 256 * w_
                        if it >= 2:
                            e.wait_ge(fV.h, bfV + it - 1)
                        for kc in range(4):
                            g = 4 * it + kc
                            if g >= 2:
                                e.wait_ge(fM.h, bfM + g - 1)
                            fS.inc(e.matmul(psS[:, g % 2, 0:256],
                                            KR[hs, k0 + kc * 128:k0 + kc * 128 + 128],
                                            QR[hs, q0:q0 + 256],
                                            start=True, stop=True), 1)
                            e.wait_ge(fE.h, bfE + g + 1)
                            fO.inc(e.matmul(psO[0:65, it % 2, 0:256],
                                            VT[:, 2 * w_ + kc, h, :],
                                            PN[:, g % 2, :],
                                            start=(kc == 0), stop=(kc == 3)), 1)

            @ph.vector
            def _(e):
                for it in range(NHW + 1):
                    if it < NHW:
                        h, w_ = hw(it)
                        v = 0 if w_ == 0 else 1
                        for kc in range(4):
                            g = 4 * it + kc
                            e.wait_ge(fS.h, bfS + g + 1)
                            if g >= 2:
                                e.wait_ge(fE.h, bfE + g - 1)
                            fM.inc(e.scalar_tensor_tensor(
                                scrS[:, g % 2, :], psS[:, g % 2, 0:256],
                                negM[:, 0:1], mask[:, v, kc, :],
                                ALU.add, ALU.add), 1)
                    if it >= 1:
                        jt = it - 1
                        h, w_ = hw(jt)
                        e.wait_ge(fB.h, bfB + jt + 1)
                        e.tensor_copy(obuf[:, jt % 2, :], psO[0:64, jt % 2, 0:256])
                        fV.inc(e.tensor_tensor(
                            AO[h * 64:h * 64 + 64, p, w_ * 256:w_ * 256 + 256],
                            obuf[:, jt % 2, :], psD[0:64, jt % 2, 0:256],
                            ALU.mult), 1)
                    if it < NHW:
                        e.wait_ge(fO.h, bfO + 4 * (it + 1))
                        if it >= 2:
                            e.wait_ge(fB.h, bfB + it - 1)
                        e.reciprocal(drow[0:1, it % 2, :],
                                     psO[64:65, it % 2, 0:256])
                        fD.inc(e.tensor_copy(dbf[0:1, it % 2, :],
                                             drow[0:1, it % 2, :]), 1)

            @ph.scalar
            def _(e):
                for it in range(NHW):
                    for kc in range(4):
                        g = 4 * it + kc
                        e.wait_ge(fM.h, bfM + g + 1)
                        if g >= 2:
                            e.wait_ge(fO.h, bfO + g - 1)
                        fE.inc(e.activation(PN[:, g % 2, :], scrS[:, g % 2, :],
                                            AF.Exp, scale=8.0), 1)
        pp.close()

    for m in range(NCH):
        emit_load_weight(c, None, 1024, 128, wpair,
                         dram_ap=io["woT"][:, m * 128:(m + 1) * 128])
        pp = PsumPool(nc)
        with nc.Block() as ph:
            def evac(e, m_, ns, w, ps, m=m):
                return xb.add_evac(e, m, slice(ns * 512, ns * 512 + w), ps)
            mm_loop(c, ph, pp, wpair, NCH, [0], lambda m_: (0, 128), TC,
                    lambda k, ns, w: AO[:, k, ns * 512:ns * 512 + w],
                    evac, sem_role="f1_wo")
        pp.close()


def emit_f1_ffn(c, xb, io):
    nc = c.nc
    XNE = c.sbuf("f1f_xne", [128, NCH, TC], BF16)
    XProw = c.sbuf("f1f_xp", [128, NCH], F32)
    XP0 = c.sbuf("f1f_x0", [128, NCH], F32)
    PRD = c.sbuf("f1f_prd", [128, 22, 512], BF16)
    wcol = c.sbuf("f1f_w", [128, 2, NCH, 256], BF16)
    w2c = c.sbuf("f1f_w2", [128, 22, 128], BF16)
    gcol = c.sbuf("f1f_g", [128, 2 * NCH], F32)
    b1col = c.sbuf("f1f_b1", [128, 44], F32)
    b2col = c.sbuf("f1f_b2", [128, NCH], F32)
    gel = c.sbuf("f1f_gel", [128, 2, 512], F32)
    ax = c.sbuf("f1f_ax", [128, 2, 512], F32)
    emit_load_small(c, io["lngb"], [128, 2 * NCH], "f1f_g")
    emit_load_small(c, io["b1col"], [128, 44], "f1f_b1")
    emit_load_small(c, io["b2col"], [128, NCH], "f1f_b2")
    emit_load_fm(c, [(XProw, j) for j in range(NCH)], io["xprev1"], 1)
    with nc.Block() as ph:
        @ph.vector
        def _(e):
            for j in range(NCH):
                if j < 4:
                    e.tensor_copy(XP0[:, j:j + 1], xb.cur(j)[:, 0:1])
                else:
                    e.tensor_copy(XP0[:, j:j + 1], XProw[:, j:j + 1])
    gam = lambda j: gcol[:, j:j + 1]
    bet = lambda j: gcol[:, NCH + j:NCH + j + 1]
    emit_ln1(c, lambda j: XP0[:, j:j + 1], XNE, 0, gamma=gam, beta=bet)
    emit_ln(c, [(xb.cur(j), 1) for j in range(NCH)], XNE, TC - 1, dst_off=1,
            gamma=gam, beta=bet, shift_chunks=(4, 5, 6, 7))

    for ns in range(TC // 512):
        pp = PsumPool(nc)
        psW = pp.get([128, 4, 512])
        wl = c.sem("f1f_wl")
        sPM = c.sem("f1f_pm")
        sGE = c.sem("f1f_ge")
        sPR = c.sem("f1f_pr")
        bwl, bPM, bGE, bPR = wl.n, sPM.n, sGE.n, sPR.n
        with nc.Block() as ph:
            @ph.gpsimd
            def _(e):
                for pr in range(22):
                    if pr >= 2:
                        e.wait_ge(sPM.h, bPM + 2 * (pr - 1))
                    for k in range(NCH):
                        wl.inc(e.dma_start(
                            out=wcol[0:128, pr % 2, k, 0:256],
                            in_=io["w1p"][k * 128:(k + 1) * 128,
                                          pr * 256:(pr + 1) * 256]), 16)

            @ph.tensor
            def _(e):
                for pr in range(22):
                    e.wait_ge(wl.h, bwl + 16 * NCH * (pr + 1))
                    for half in range(2):
                        g = 2 * pr + half
                        if g >= 4:
                            e.wait_ge(sPR.h, bPR + pr - 1)
                        for k in range(NCH):
                            mm = e.matmul(psW[:, g % 4, 0:512],
                                          wcol[:, pr % 2, k, half * 128:half * 128 + 128],
                                          XNE[:, k, ns * 512:ns * 512 + 512],
                                          start=(k == 0), stop=(k == NCH - 1))
                    sPM.inc(mm, 2)

            @ph.scalar
            def _(e):
                for pr in range(22):
                    e.wait_ge(sPM.h, bPM + 2 * (pr + 1))
                    if pr >= 2:
                        e.wait_ge(sPR.h, bPR + pr - 1)
                    sGE.inc(e.activation(gel[:, pr % 2, :],
                                         psW[:, (2 * pr + 1) % 4, 0:512],
                                         AF.Gelu, bias=b1col[:, 2 * pr + 1:2 * pr + 2]), 1)

            @ph.vector
            def _(e):
                for pr in range(22):
                    e.wait_ge(sGE.h, bGE + pr + 1)
                    e.tensor_scalar(ax[:, pr % 2, :], psW[:, (2 * pr) % 4, 0:512],
                                    b1col[:, 2 * pr:2 * pr + 1], None, ALU.add)
                    sPR.inc(e.tensor_tensor(PRD[:, pr, :], ax[:, pr % 2, :],
                                            gel[:, pr % 2, :], ALU.mult), 1)
        pp.close()

        for m in range(NCH):
            emit_load_weight(c, None, 2816, 128, w2c,
                             dram_ap=io["w2p"][:, m * 128:(m + 1) * 128])
            pp = PsumPool(nc)
            with nc.Block() as ph:
                def evac(e, m_, ns_, w, ps, m=m, ns=ns):
                    sl = slice(ns * 512, ns * 512 + 512)
                    e.tensor_scalar(ax[:, 0, :], ps, b2col[:, m:m + 1], None,
                                    ALU.add)
                    e.tensor_tensor(xb.scr[:, 0:512], ax[:, 0, :],
                                    xb.cur(m)[:, sl], ALU.add)
                    return e.tensor_copy(xb.cur(m)[:, sl], xb.scr[:, 0:512])
                mm_loop(c, ph, pp, w2c, 22, [0], lambda m_: (0, 128), 512,
                        lambda k, ns_, w: PRD[:, k, :],
                        evac, sem_role="f1f_w2")
            pp.close()


def build_f1_attn():
    nc = bass.Bass("TRN2", target_bir_lowering=False)
    c = Ctx(nc)
    dr = {}
    _din(nc, dr, "x", (TC, D))
    _din(nc, dr, "xprev", (F1H, D))
    _din(nc, dr, "wqkvp", (8, 1024, 384))
    _din(nc, dr, "woT", (1024, 1024))
    _din(nc, dr, "txq", (64, 2 * F1K))
    _din(nc, dr, "txk", (64, 2 * F1K))
    _din(nc, dr, "qs2", (64, 2))
    _din(nc, dr, "fmask", (128, 2 * 4 * 256))
    _din(nc, dr, "identf", (128, 128))
    y = nc.dram_tensor("y", [TC, D], F32, kind="ExternalOutput").ap()
    emit_setup(c, dr)
    io = {
        "xprev": dr["xprev"], "wqkvp": dr["wqkvp"], "woT": dr["woT"],
        "t_xq": emit_load_small(c, dr["txq"], [64, 2, F1K], "t_xq", BF16),
        "t_xk": emit_load_small(c, dr["txk"], [64, 2, F1K], "t_xk", BF16),
        "t_qs": emit_load_small(c, dr["qs2"], [64, 2], "t_qs"),
        "t_fmask": emit_load_small(c, dr["fmask"], [128, 2, 4, 256], "t_fmask", BF16),
    }
    xb = XBufs(c, TC)
    emit_load_fm(c, xb.chunks(), dr["x"], TC)
    emit_f1_attn(c, xb, io)
    emit_store_fm(c, xb.chunks(), y, TC)
    c.close()
    return nc


def build_f1_ffn():
    nc = bass.Bass("TRN2", target_bir_lowering=False)
    c = Ctx(nc)
    dr = {}
    _din(nc, dr, "x", (TC, D))
    _din(nc, dr, "xprev1", (1, D))
    _din(nc, dr, "w1p", (1024, 5632))
    _din(nc, dr, "w2p", (2816, 1024))
    _din(nc, dr, "lngb", (128, 2 * NCH))
    _din(nc, dr, "b1col", (128, 44))
    _din(nc, dr, "b2col", (128, NCH))
    _din(nc, dr, "identf", (128, 128))
    y = nc.dram_tensor("y", [TC, D], F32, kind="ExternalOutput").ap()
    emit_setup(c, dr)
    io = {"xprev1": dr["xprev1"], "w1p": dr["w1p"], "w2p": dr["w2p"],
          "lngb": dr["lngb"], "b1col": dr["b1col"], "b2col": dr["b2col"]}
    xb = XBufs(c, TC)
    emit_load_fm(c, xb.chunks(), dr["x"], TC)
    emit_f1_ffn(c, xb, io)
    emit_store_fm(c, xb.chunks(), y, TC)
    c.close()
    return nc


def prep_f1(inp):
    w = {}
    Wqkv = np.asarray(inp["L1_ffn_Wqkv"])
    w["f1_wqkvp"] = [f32c(np.stack(
        [np.concatenate([Wqkv[i][:, 128 * p:128 * p + 128],
                         Wqkv[i][:, 1024 + 128 * p:1024 + 128 * p + 128],
                         Wqkv[i][:, 2048 + 128 * p:2048 + 128 * p + 128]], axis=1)
         for p in range(8)])) for i in range(6)]
    w["f1_woT"] = [f32c(np.asarray(inp["L1_ffn_Wo"])[i]) for i in range(6)]
    w["f1_qs2"] = [f32c(np.stack([np.asarray(inp["L1_ffn_q_scale"])[i],
                                  np.asarray(inp["L1_ffn_k_scale"])[i]], axis=1))
                   for i in range(6)]
    W1 = np.asarray(inp["L1_ffn_W1"])
    W2 = np.asarray(inp["L1_ffn_W2"])
    b1 = np.asarray(inp["L1_ffn_b1"])
    w1p, w2p, b1c = [], [], []
    for i in range(6):
        a = np.concatenate([W1[i][:, :2730],
                            np.zeros((1024, 86), np.float32)], axis=1)
        g = np.concatenate([W1[i][:, 2730:],
                            np.zeros((1024, 86), np.float32)], axis=1)
        cols = []
        for pr in range(22):
            cols.append(a[:, 128 * pr:128 * pr + 128])
            cols.append(g[:, 128 * pr:128 * pr + 128])
        w1p.append(f32c(np.concatenate(cols, axis=1)))
        w2p.append(f32c(np.concatenate(
            [W2[i], np.zeros((86, 1024), np.float32)], axis=0)))
        b1a = np.concatenate([b1[i][:2730], np.zeros(86, np.float32)])
        b1g = np.concatenate([b1[i][2730:], np.zeros(86, np.float32)])
        bc = np.zeros((128, 44), np.float32)
        for pr in range(22):
            bc[:, 2 * pr] = b1a[128 * pr:128 * pr + 128]
            bc[:, 2 * pr + 1] = b1g[128 * pr:128 * pr + 128]
        b1c.append(f32c(bc))
    w["f1_w1p"], w["f1_w2p"], w["f1_b1c"] = w1p, w2p, b1c
    w["f1_lngb"] = [f32c(np.concatenate([col_layout(np.asarray(inp["L1_ffn_ln_g"])[i]),
                                         col_layout(np.asarray(inp["L1_ffn_ln_b"])[i])],
                                        axis=1)) for i in range(6)]
    w["f1_b2c"] = [col_layout(np.asarray(inp["L1_ffn_b2"])[i]) for i in range(6)]
    DIM_HEAD, WINDOW = 64, 256
    half = np.arange(0, DIM_HEAD, 2, dtype=np.float32)
    inv_freq = (1.0 / 10000.0 ** (half / DIM_HEAD)).astype(np.float32)
    t = np.arange(T, dtype=np.float32)
    freqs = t[:, None] * inv_freq[None, :]
    base_scale = ((half + 0.4 * DIM_HEAD) / (1.4 * DIM_HEAD)).astype(np.float32)
    power = ((t - T // 2) / (WINDOW // 2)).astype(np.float32)
    scale = (base_scale[None, :] ** power[:, None]).astype(np.float32)
    cosf, sinf = np.cos(freqs), np.sin(freqs)
    w["f1_tabs"] = (cosf, sinf, scale)
    m = np.zeros((128, 2, 4, 256), np.float32)
    q = np.arange(256)[None, :]
    for kc in range(4):
        j = (kc * 128 + np.arange(128))[:, None]
        valid = (q <= j) & (j <= q + 256)
        m[:, 1, kc, :] = np.where(valid, 0.0, NEG)
        m[:, 0, kc, :] = np.where(valid & (j >= 256), 0.0, NEG)
    w["f1_mask_first"] = f32c(m.reshape(128, 2048))
    mn = m.copy()
    mn[:, 0] = m[:, 1]
    w["f1_mask_mid"] = f32c(mn.reshape(128, 2048))
    return w


def f1_core_tables(w, cidx):
    cosf, sinf, scale = w["f1_tabs"]
    q0 = (cidx % CPS) * TC
    pos = np.arange(q0 - 256, q0 + TC)
    pos = np.clip(pos, 0, T - 1) if False else pos
    posc = np.clip(pos, -(T), T - 1)
    cf, sf, sc = cosf[posc], sinf[posc], scale[posc]

    def dup(a):
        return np.concatenate([a.T, a.T], axis=0)

    def dsgn(a):
        return np.concatenate([-a.T, a.T], axis=0)
    txq = f32c(np.concatenate([dup(cf * sc), dsgn(sf * sc)], axis=1))
    txk = f32c(np.concatenate([dup(cf / sc), dsgn(sf / sc)], axis=1))
    return txq, txk


def run_f1_attn(x_chunks, w, layer):
    nc = _nc_cache('f1_attn', build_f1_attn)
    prevs = prev_rows(x_chunks, F1H)
    maps = []
    for cidx in range(NCORES):
        txq, txk = f1_core_tables(w, cidx)
        m = {"x": x_chunks[cidx], "xprev": prevs[cidx],
             "wqkvp": w["f1_wqkvp"][layer], "woT": w["f1_woT"][layer],
             "txq": txq, "txk": txk, "qs2": w["f1_qs2"][layer],
             "fmask": w["f1_mask_first"] if cidx % CPS == 0 else w["f1_mask_mid"]}
        m.update(_CONSTS)
        maps.append(m)
    res = run_bass_kernel_spmd(nc, maps, core_ids=list(range(NCORES)))
    return [res.results[i]["y"] for i in range(NCORES)]


def run_f1_ffn(x_chunks, w, layer):
    nc = _nc_cache('f1_ffn', build_f1_ffn)
    prevs = prev_rows(x_chunks, 1)
    maps = []
    for cidx in range(NCORES):
        m = {"x": x_chunks[cidx], "xprev1": prevs[cidx],
             "w1p": w["f1_w1p"][layer], "w2p": w["f1_w2p"][layer],
             "lngb": w["f1_lngb"][layer], "b1col": w["f1_b1c"][layer],
             "b2col": w["f1_b2c"][layer]}
        m.update(_CONSTS)
        maps.append(m)
    res = run_bass_kernel_spmd(nc, maps, core_ids=list(range(NCORES)))
    return [res.results[i]["y"] for i in range(NCORES)]



def build_a1a_pre():
    nc = bass.Bass("TRN2", target_bir_lowering=False)
    c = Ctx(nc)
    dr = {}
    _din(nc, dr, "x", (TC, D))
    _din(nc, dr, "xh3", (3, D))
    _din(nc, dr, "winT", (1024, 4096))
    _din(nc, dr, "convw", (128, 4 * 16))
    _din(nc, dr, "convb", (128, 16))
    _din(nc, dr, "wxT", (2048, 33))
    _din(nc, dr, "wdtcol", (128, 16))
    _din(nc, dr, "bdtcol", (128, 16))
    _din(nc, dr, "identf", (128, 128))
    dout = nc.dram_tensor("delta", [2048, TC], F32, kind="ExternalOutput").ap()
    uout = nc.dram_tensor("u", [2048, TC], F32, kind="ExternalOutput").ap()
    szout = nc.dram_tensor("sz", [2048, TC], F32, kind="ExternalOutput").ap()
    bcout = nc.dram_tensor("bc", [32, TC], F32, kind="ExternalOutput").ap()
    emit_setup(c, dr)
    cw = emit_load_small(c, dr["convw"], [128, 4, 16], "a1_cw")
    cb = emit_load_small(c, dr["convb"], [128, 16], "a1_cb")
    wdt = emit_load_small(c, dr["wdtcol"], [128, 16], "a1_wdt")
    bdt = emit_load_small(c, dr["bdtcol"], [128, 16], "a1_bdt")
    wx = c.sbuf("a1_wx", [128, 16, 33], BF16)
    emit_load_weight(c, None, 2048, 33, wx, dram_ap=dr["wxT"])
    XB8 = c.sbuf("a1_x", [128, NCH, TC // 2], BF16)
    XN = c.sbuf("XN", [128, NCH, TC], BF16)
    for half in range(2):
        emit_load_fm(c, [(XB8.rearrange("p a b -> p (a b)"), j * (TC // 2))
                         for j in range(NCH)], dr["x"], TC // 2,
                     dram_row0=half * (TC // 2))
        emit_ln(c, [(XB8.rearrange("p a b -> p (a b)"), j * (TC // 2))
                    for j in range(NCH)], XN, TC // 2, dst_off=half * (TC // 2))
    XH = c.sbuf("a1_xh", [128, NCH], F32)
    XNH = c.sbuf("a1_xnh", [128, NCH, 3], BF16)
    for hrow in range(3):
        emit_load_fm(c, [(XH, j) for j in range(NCH)], dr["xh3"], 1, dram_row0=hrow)
        emit_ln1(c, lambda j: XH[:, j:j + 1], XNH, hrow)
    XD = c.sbuf("a1_xd", [128, 16, TC + 3], BF16)
    wpair = c.sbuf("a1_wp", [128, NCH, 128], BF16)
    for m in range(16):
        emit_load_weight(c, None, 1024, 128, wpair,
                         dram_ap=dr["winT"][:, m * 128:(m + 1) * 128])
        pp = PsumPool(nc)
        with nc.Block() as ph:
            mm_loop(c, ph, pp, wpair, NCH, [0], lambda m_: (0, 128), 3,
                    lambda k, ns, w: XNH[:, k, 0:3],
                    lambda e, m_, ns, w, ps, m=m: e.tensor_copy(XD[:, m, 0:3], ps),
                    sem_role="a1_xzh")
            mm_loop(c, ph, pp, wpair, NCH, [0], lambda m_: (0, 128), TC,
                    lambda k, ns, w: XN[:, k, ns * 512:ns * 512 + w],
                    lambda e, m_, ns, w, ps, m=m: e.tensor_copy(
                        XD[:, m, 3 + ns * 512:3 + ns * 512 + w], ps),
                    sem_role="a1_xz")
        pp.close()
    ZF = c.sbuf("a1_zf", [128, 512], F32)
    for m in range(16):
        emit_load_weight(c, None, 1024, 128, wpair,
                         dram_ap=dr["winT"][:, 2048 + m * 128:2048 + (m + 1) * 128])
        for ns in range(4):
            pp = PsumPool(nc)
            with nc.Block() as ph:
                mm_loop(c, ph, pp, wpair, NCH, [0], lambda m_: (0, 128), 512,
                        lambda k, ns_, w, ns=ns: XN[:, k, ns * 512:ns * 512 + 512],
                        lambda e, m_, ns_, w, ps: e.activation(ZF[:, :], ps, AF.Silu),
                        evac_eng="scalar", sem_role="a1_z")
                zgoal = c.sem("a1_z_e").n
                st = c.sem("a1_zst")

                def gp(e, m=m, ns=ns):
                    e.wait_ge(c.sem("a1_z_e").h, zgoal)
                    st.inc(e.dma_start(out=szout[m * 128:(m + 1) * 128,
                                                 ns * 512:(ns + 1) * 512],
                                       in_=ZF[:, :]), 16)
                    e.wait_ge(st.h, st.n)
                ph.gpsimd(gp)
            pp.close()
    PRE = c.sbuf("a1_pre", [128, 2, 512], F32)
    USL = c.sbuf("a1_usl", [128, 16, 512], BF16)
    CS = c.sbuf("a1_cs", [128, 2, 512], F32)
    DROW = c.sbuf("a1_drow", [1, 512], F32)
    DRB = c.sbuf("a1_drb", [1, 512], BF16)
    SPX = c.sbuf("a1_spx", [128, 2, 512], F32)
    LNO = c.sbuf("a1_lno", [128, 2, 512], F32)
    DPP = c.sbuf("a1_dpp", [128, 2, 512], F32)
    DSL = c.sbuf("a1_dsl", [128, 2, 512], F32)
    BCS = c.sbuf("a1_bcs", [16, 2, 512], F32)
    sCV = c.sem("a1_cv")
    sSI = c.sem("a1_si")
    sUD = c.sem("a1_ud")
    ones_row = c.sbuf("ones_row_bf", [1, 128], BF16)
    for ns in range(4):
        bCV, bSI = sCV.n, sSI.n
        with nc.Block() as ph:
            @ph.vector
            def _(e, ns=ns, bSI=bSI):
                for j in range(16):
                    base = 3 + ns * 512
                    if j >= 2:
                        e.wait_ge(sSI.h, bSI + j - 1)
                    e.tensor_scalar(CS[:, 0, :], XD[:, j, base - 3:base - 3 + 512],
                                    cw[:, 0, j:j + 1], None, ALU.mult)
                    e.scalar_tensor_tensor(CS[:, 1, :], XD[:, j, base - 2:base - 2 + 512],
                                           cw[:, 1, j:j + 1], CS[:, 0, :],
                                           ALU.mult, ALU.add)
                    e.scalar_tensor_tensor(CS[:, 0, :], XD[:, j, base - 1:base - 1 + 512],
                                           cw[:, 2, j:j + 1], CS[:, 1, :],
                                           ALU.mult, ALU.add)
                    e.scalar_tensor_tensor(CS[:, 1, :], XD[:, j, base:base + 512],
                                           cw[:, 3, j:j + 1], CS[:, 0, :],
                                           ALU.mult, ALU.add)
                    sCV.inc(e.tensor_scalar(PRE[:, j % 2, :], CS[:, 1, :],
                                            cb[:, j:j + 1], None, ALU.add), 1)

            @ph.scalar
            def _(e, bCV=bCV):
                for j in range(16):
                    e.wait_ge(sCV.h, bCV + j + 1)
                    sSI.inc(e.activation(USL[:, j, :], PRE[:, j % 2, :], AF.Silu), 1)
        with nc.Block() as ph:
            @ph.gpsimd
            def _(e, ns=ns):
                st = c.sem("a1_ust")
                b0 = st.n
                for j in range(16):
                    st.inc(e.dma_start(out=uout[j * 128:(j + 1) * 128,
                                                ns * 512:(ns + 1) * 512],
                                       in_=USL[:, j, :]), 16)
                e.wait_ge(st.h, b0 + 256)
        pp = PsumPool(nc)
        psdt = pp.get([1, 512])
        psB = pp.get([16, 512])
        psC = pp.get([16, 512])
        sWX = c.sem("a1_wxm")
        sWV = c.sem("a1_wxv")
        bWX, bWV = sWX.n, sWV.n
        with nc.Block() as ph:
            @ph.tensor
            def _(e):
                for k in range(16):
                    e.matmul(psdt[0:1, :], wx[:, k, 0:1], USL[:, k, :],
                             start=(k == 0), stop=(k == 15))
                for k in range(16):
                    e.matmul(psB[:, :], wx[:, k, 1:17], USL[:, k, :],
                             start=(k == 0), stop=(k == 15))
                for k in range(16):
                    mm = e.matmul(psC[:, :], wx[:, k, 17:33], USL[:, k, :],
                                  start=(k == 0), stop=(k == 15))
                sWX.inc(mm, 1)

            @ph.vector
            def _(e, bWX=bWX):
                e.wait_ge(sWX.h, bWX + 1)
                e.tensor_copy(BCS[:, 0, :], psB[:, :])
                e.tensor_copy(BCS[:, 1, :], psC[:, :])
                e.tensor_copy(DROW[0:1, :], psdt[0:1, :])
                sWV.inc(e.tensor_copy(DRB[0:1, :], DROW[0:1, :]), 1)

            @ph.gpsimd
            def _(e, ns=ns, bWV=bWV):
                e.wait_ge(sWV.h, bWV + 1)
                st = c.sem("a1_bcst")
                b0 = st.n
                st.inc(e.dma_start(out=bcout[0:16, ns * 512:(ns + 1) * 512],
                                   in_=BCS[:, 0, :]), 16)
                st.inc(e.dma_start(out=bcout[16:32, ns * 512:(ns + 1) * 512],
                                   in_=BCS[:, 1, :]), 16)
                e.wait_ge(st.h, b0 + 32)
        pp.close()
        pp = PsumPool(nc)
        psbc = pp.get([128, 512])
        sBC2 = c.sem("a1_bc2")
        sDP = c.sem("a1_dp")
        sLN = c.sem("a1_ln")
        sDD = c.sem("a1_dd")
        sDS = c.sem("a1_dst")
        bBC2, bDP, bLN, bDD, bDS = sBC2.n, sDP.n, sLN.n, sDD.n, sDS.n
        with nc.Block() as ph:
            @ph.tensor
            def _(e):
                sBC2.inc(e.matmul(psbc[:, :], ones_row[0:1, :], DRB[0:1, :],
                                  start=True, stop=True), 1)

            @ph.vector
            def _(e, ns=ns):
                for j in range(16):
                    e.wait_ge(sBC2.h, bBC2 + 1)
                    if j >= 2:
                        e.wait_ge(sLN.h, bLN + j - 1)
                    sDP.inc(e.tensor_scalar(DPP[:, j % 2, :], psbc[:, :],
                                            wdt[:, j:j + 1], bdt[:, j:j + 1],
                                            ALU.mult, ALU.add), 1)
                    e.wait_ge(sLN.h, bLN + j + 1)
                    if j >= 2:
                        e.wait_ge(sDS.h, bDS + 16 * (j - 1))
                    e.tensor_scalar(CS[:, 0, :], DPP[:, j % 2, :], 0.0, None,
                                    ALU.max)
                    sDD.inc(e.tensor_tensor(DSL[:, j % 2, :], CS[:, 0, :],
                                            LNO[:, j % 2, :], ALU.add), 1)

            @ph.scalar
            def _(e):
                for j in range(16):
                    e.wait_ge(sDP.h, bDP + j + 1)
                    if j >= 2:
                        e.wait_ge(sDD.h, bDD + j - 1)
                    e.activation(SPX[:, 0, :], DPP[:, j % 2, :], AF.Abs)
                    e.activation(SPX[:, 1, :], SPX[:, 0, :], AF.Exp, scale=-1.0)
                    sLN.inc(e.activation(LNO[:, j % 2, :], SPX[:, 1, :],
                                         AF.Ln, bias=1.0), 1)

            @ph.gpsimd
            def _(e, ns=ns):
                for j in range(16):
                    e.wait_ge(sDD.h, bDD + j + 1)
                    sDS.inc(e.dma_start(out=dout[j * 128:(j + 1) * 128,
                                                 ns * 512:(ns + 1) * 512],
                                        in_=DSL[:, j % 2, :]), 16)
                e.wait_ge(sDS.h, sDS.n)
        pp.close()
    c.close()
    return nc


def build_a1a_scan():
    nc = bass.Bass("TRN2", target_bir_lowering=False)
    c = Ctx(nc)
    dr = {}
    _din(nc, dr, "dsl", (512, T))
    _din(nc, dr, "usl", (512, T))
    _din(nc, dr, "bf", (16, T))
    _din(nc, dr, "cf", (16, T))
    _din(nc, dr, "alog", (128, 4 * 16))
    _din(nc, dr, "identf", (128, 128))
    yout = nc.dram_tensor("y", [512, T], F32, kind="ExternalOutput").ap()
    emit_setup(c, dr)
    AL = emit_load_small(c, dr["alog"], [128, 4, 16], "as_alog")
    AC = c.sbuf("as_a", [128, 4, 16], F32)
    with nc.Block() as ph:
        @ph.scalar
        def _(e):
            e.activation(AC.rearrange("p a b -> p (a b)"),
                         AL.rearrange("p a b -> p (a b)"), AF.Exp)
    with nc.Block() as ph:
        @ph.vector
        def _(e):
            e.tensor_scalar(AL.rearrange("p a b -> p (a b)"),
                            AC.rearrange("p a b -> p (a b)"), -1.0, None, ALU.mult)
    ACN = AL
    DSLB = c.sbuf("as_d", [128, 4, 2048], F32)
    USLB = c.sbuf("as_u", [128, 4, 2048], BF16)
    DU = c.sbuf("as_du", [128, 4, 2048], BF16)
    ROWB = c.sbuf("as_rb", [1, 2, 2048], BF16)
    ABUF = c.sbuf("as_ab", [128, 2048], BF16)
    WBUF = c.sbuf("as_wb", [128, 2048], BF16)
    HBUF = c.sbuf("as_hb", [128, 2048], BF16)
    YPP = c.sbuf("as_y", [128, 2, 2048], F32)
    TBUF = c.sbuf("as_t", [128, 2048], BF16)
    STC = c.sbuf("as_st", [128, 4, 16], F32)
    ones_row = c.sbuf("ones_row_bf", [1, 128], BF16)
    with nc.Block() as ph:
        @ph.vector
        def _(e):
            e.memset(STC.rearrange("p a b -> p (a b)"), 0.0)
    ld = c.sem("as_ld")
    for ns in range(4):
        tsl = slice(ns * 2048, (ns + 1) * 2048)
        b0 = ld.n
        with nc.Block() as ph:
            @ph.gpsimd
            def _(e, tsl=tsl, b0=b0):
                for jd in range(4):
                    ld.inc(e.dma_start(out=DSLB[:, jd, :],
                                       in_=dr["dsl"][jd * 128:(jd + 1) * 128, tsl]), 16)
                    ld.inc(e.dma_start(out=USLB[:, jd, :],
                                       in_=dr["usl"][jd * 128:(jd + 1) * 128, tsl]), 16)
                e.wait_ge(ld.h, b0 + 128)
        with nc.Block() as ph:
            @ph.vector
            def _(e):
                for jd in range(4):
                    e.tensor_tensor(DU[:, jd, :], DSLB[:, jd, :], USLB[:, jd, :],
                                    ALU.mult)
        for jd in range(4):
            ycur = 0
            for s in range(16):
                b0 = ld.n
                with nc.Block() as ph:
                    @ph.gpsimd
                    def _(e, s=s, tsl=tsl, b0=b0):
                        ld.inc(e.dma_start(out=ROWB[0:1, 0, :],
                                           in_=dr["bf"][s:s + 1, tsl]), 16)
                        ld.inc(e.dma_start(out=ROWB[0:1, 1, :],
                                           in_=dr["cf"][s:s + 1, tsl]), 16)
                        e.wait_ge(ld.h, b0 + 32)
                pp = PsumPool(nc)
                psb = pp.get([128, 4, 512])
                psc = pp.get([128, 4, 512])
                with nc.Block() as ph:
                    @ph.tensor
                    def _(e):
                        for q4 in range(4):
                            e.matmul(psb[:, q4, :], ones_row[0:1, :],
                                     ROWB[0:1, 0, q4 * 512:(q4 + 1) * 512],
                                     start=True, stop=True)
                            e.matmul(psc[:, q4, :], ones_row[0:1, :],
                                     ROWB[0:1, 1, q4 * 512:(q4 + 1) * 512],
                                     start=True, stop=True)

                    @ph.scalar
                    def _(e, jd=jd, s=s):
                        e.activation(ABUF[:, :], DSLB[:, jd, :], AF.Exp,
                                     scale=ACN[:, jd, s:s + 1])
                with nc.Block() as ph:
                    @ph.vector
                    def _(e, jd=jd, s=s, ycur=ycur):
                        e.tensor_tensor(WBUF[:, :], DU[:, jd, :],
                                        psb.rearrange("p a b -> p (a b)"), ALU.mult)
                        e.tensor_tensor_scan(HBUF[:, :], ABUF[:, :], WBUF[:, :],
                                             STC[:, jd, s:s + 1], ALU.mult, ALU.add)
                        e.tensor_copy(STC[:, jd, s:s + 1], HBUF[:, 2047:2048])
                        e.tensor_tensor(TBUF[:, :], HBUF[:, :],
                                        psc.rearrange("p a b -> p (a b)"), ALU.mult)
                        if s == 0:
                            e.tensor_copy(YPP[:, 0, :], TBUF[:, :])
                        else:
                            e.tensor_tensor(YPP[:, (s % 2)], YPP[:, 1 - (s % 2)],
                                            TBUF[:, :], ALU.add)
                pp.close()
            with nc.Block() as ph:
                @ph.gpsimd
                def _(e, jd=jd, tsl=tsl):
                    st = c.sem("as_st2")
                    b0 = st.n
                    st.inc(e.dma_start(out=yout[jd * 128:(jd + 1) * 128, tsl],
                                       in_=YPP[:, 15 % 2, :]), 16)
                    e.wait_ge(st.h, b0 + 16)
    c.close()
    return nc


def build_a1a_post():
    nc = bass.Bass("TRN2", target_bir_lowering=False)
    c = Ctx(nc)
    dr = {}
    _din(nc, dr, "x", (TC, D))
    _din(nc, dr, "yfm", (2048, TC))
    _din(nc, dr, "ufm", (2048, TC))
    _din(nc, dr, "szfm", (2048, TC))
    _din(nc, dr, "dcol", (128, 16))
    _din(nc, dr, "woutT", (2048, 1024))
    _din(nc, dr, "identf", (128, 128))
    y = nc.dram_tensor("y", [TC, D], F32, kind="ExternalOutput").ap()
    emit_setup(c, dr)
    dcol = emit_load_small(c, dr["dcol"], [128, 16], "ap_dcol")
    xb = XBufs(c, TC)
    emit_load_fm(c, xb.chunks(), dr["x"], TC)
    G = c.sbuf("ap_g", [128, 16, TC], BF16)
    SLB = c.sbuf("ap_slb", [128, 3, 512], F32)
    ld = c.sem("ap_ld")
    for j in range(16):
        for ns in range(4):
            b0 = ld.n
            with nc.Block() as ph:
                @ph.gpsimd
                def _(e, j=j, ns=ns, b0=b0):
                    rs = slice(j * 128, (j + 1) * 128)
                    cs = slice(ns * 512, (ns + 1) * 512)
                    ld.inc(e.dma_start(out=SLB[:, 0, :], in_=dr["yfm"][rs, cs]), 16)
                    ld.inc(e.dma_start(out=SLB[:, 1, :], in_=dr["ufm"][rs, cs]), 16)
                    ld.inc(e.dma_start(out=SLB[:, 2, :], in_=dr["szfm"][rs, cs]), 16)
                    e.wait_ge(ld.h, b0 + 48)
            with nc.Block() as ph:
                def vv(e, j=j, ns=ns):
                    e.scalar_tensor_tensor(G[:, j, ns * 512:(ns + 1) * 512],
                                           SLB[:, 1, :], dcol[:, j:j + 1],
                                           SLB[:, 0, :], ALU.mult, ALU.add)
                    e.tensor_tensor(SLB[:, 0, :],
                                    G[:, j, ns * 512:(ns + 1) * 512],
                                    SLB[:, 2, :], ALU.mult)
                    e.tensor_copy(G[:, j, ns * 512:(ns + 1) * 512], SLB[:, 0, :])
                ph.vector(vv)
    wpair = c.sbuf("ap_wp", [128, 16, 128], BF16)
    for m in range(NCH):
        emit_load_weight(c, None, 2048, 128, wpair,
                         dram_ap=dr["woutT"][:, m * 128:(m + 1) * 128])
        pp = PsumPool(nc)
        with nc.Block() as ph:
            def evac(e, m_, ns, w, ps, m=m):
                return xb.add_evac(e, m, slice(ns * 512, ns * 512 + w), ps)
            mm_loop(c, ph, pp, wpair, 16, [0], lambda m_: (0, 128), TC,
                    lambda k, ns, w: G[:, k, ns * 512:ns * 512 + w],
                    evac, sem_role="ap_wo")
        pp.close()
    emit_store_fm(c, xb.chunks(), y, TC)
    c.close()
    return nc


def prep_a1a(inp):
    w = {}
    w["a1_winT"] = f32c(np.asarray(inp["L1_attn_a_W_in"]).T)
    cwv = np.asarray(inp["L1_attn_a_conv_w"])[:, 0, :]
    w["a1_convw"] = f32c(np.concatenate(
        [col_layout(cwv[:, t]) for t in range(4)], axis=1))
    w["a1_convb"] = col_layout(inp["L1_attn_a_conv_b"])
    Wx = np.asarray(inp["L1_attn_a_W_x"])
    w["a1_wxT"] = f32c(Wx.T)
    w["a1_wdt"] = col_layout(np.asarray(inp["L1_attn_a_W_dt"])[:, 0])
    w["a1_bdt"] = col_layout(inp["L1_attn_a_b_dt"])
    al = np.asarray(inp["L1_attn_a_A_log"])
    w["a1_alog"] = [f32c(np.concatenate(
        [al[512 * q + 128 * jj:512 * q + 128 * (jj + 1)] for jj in range(4)],
        axis=1)) for q in range(4)]
    w["a1_dcol"] = col_layout(inp["L1_attn_a_D_param"])
    w["a1_woutT"] = f32c(np.asarray(inp["L1_attn_a_W_out"]).T)
    return w


def run_a1a(x_chunks, w):
    nc = _nc_cache('a1a_pre', build_a1a_pre)
    prevs = prev_rows(x_chunks, 3)
    maps = []
    for cidx in range(NCORES):
        m = {"x": x_chunks[cidx], "xh3": prevs[cidx], "winT": w["a1_winT"],
             "convw": w["a1_convw"], "convb": w["a1_convb"], "wxT": w["a1_wxT"],
             "wdtcol": w["a1_wdt"], "bdtcol": w["a1_bdt"]}
        m.update(_CONSTS)
        maps.append(m)
    res = run_bass_kernel_spmd(nc, maps, core_ids=list(range(NCORES)))
    pre = [dict(delta=res.results[i]["delta"], u=res.results[i]["u"],
                sz=res.results[i]["sz"], bc=res.results[i]["bc"])
           for i in range(NCORES)]
    nc = _nc_cache('a1a_scan', build_a1a_scan)
    maps = []
    for cidx in range(NCORES):
        s, q = divmod(cidx, CPS)
        dfull = np.concatenate([pre[4 * s + qq]["delta"] for qq in range(4)], axis=1)
        ufull = np.concatenate([pre[4 * s + qq]["u"] for qq in range(4)], axis=1)
        bcf = np.concatenate([pre[4 * s + qq]["bc"] for qq in range(4)], axis=1)
        rs = slice(512 * q, 512 * (q + 1))
        m = {"dsl": f32c(dfull[rs]), "usl": f32c(ufull[rs]),
             "bf": f32c(bcf[0:16]), "cf": f32c(bcf[16:32]),
             "alog": w["a1_alog"][q]}
        m.update(_CONSTS)
        maps.append(m)
    res = run_bass_kernel_spmd(nc, maps, core_ids=list(range(NCORES)))
    ys = [res.results[i]["y"] for i in range(NCORES)]
    nc = _nc_cache('a1a_post', build_a1a_post)
    maps = []
    for cidx in range(NCORES):
        s, q = divmod(cidx, CPS)
        yfull = np.concatenate([ys[4 * s + qq] for qq in range(4)], axis=0)
        cs = slice(TC * q, TC * (q + 1))
        m = {"x": x_chunks[cidx], "yfm": f32c(yfull[:, cs]),
             "ufm": f32c(pre[cidx]["u"]), "szfm": f32c(pre[cidx]["sz"]),
             "dcol": w["a1_dcol"], "woutT": w["a1_woutT"]}
        m.update(_CONSTS)
        maps.append(m)
    res = run_bass_kernel_spmd(nc, maps, core_ids=list(range(NCORES)))
    return [res.results[i]["y"] for i in range(NCORES)]



def build_a0b_kv():
    nc = bass.Bass("TRN2", target_bir_lowering=False)
    c = Ctx(nc)
    dr = {}
    _din(nc, dr, "x", (TC, D))
    _din(nc, dr, "wkT", (1024, 1024))
    _din(nc, dr, "wvT", (1024, 1024))
    _din(nc, dr, "bkv", (128, 2 * NCH))
    _din(nc, dr, "identf", (128, 128))
    kout = nc.dram_tensor("kfm", [1024, TC], F32, kind="ExternalOutput").ap()
    vout = nc.dram_tensor("vtok", [TC, 1024], F32, kind="ExternalOutput").ap()
    emit_setup(c, dr)
    bkv = emit_load_small(c, dr["bkv"], [128, 2 * NCH], "b_bkv")
    xb = XBufs(c, TC)
    emit_load_fm(c, xb.chunks(), dr["x"], TC)
    XN = c.sbuf("XN", [128, NCH, TC], BF16)
    emit_ln(c, xb.chunks(), XN, TC)
    KF = c.sbuf("b_kf", [128, NCH, TC], BF16)
    VF = c.sbuf("b_vf", [128, NCH, TC], BF16)
    wpair = c.sbuf("b_wp", [128, NCH, 128], BF16)
    for tag, wname, dst, bo in (("k", "wkT", KF, 0), ("v", "wvT", VF, NCH)):
        for m in range(NCH):
            emit_load_weight(c, None, 1024, 128, wpair,
                             dram_ap=dr[wname][:, m * 128:(m + 1) * 128])
            pp = PsumPool(nc)
            with nc.Block() as ph:
                mm_loop(c, ph, pp, wpair, NCH, [0], lambda m_: (0, 128), TC,
                        lambda k, ns, w: XN[:, k, ns * 512:ns * 512 + w],
                        lambda e, m_, ns, w, ps, dst=dst, m=m, bo=bo:
                            e.tensor_scalar(dst[:, m, ns * 512:ns * 512 + w], ps,
                                            bkv[:, bo + m:bo + m + 1], None,
                                            ALU.add),
                        sem_role="b_kv")
            pp.close()
    with nc.Block() as ph:
        @ph.gpsimd
        def _(e):
            st = c.sem("b_kst")
            b0 = st.n
            for m in range(NCH):
                st.inc(e.dma_start(out=kout[m * 128:(m + 1) * 128, :],
                                   in_=KF[:, m, :]), 16)
            e.wait_ge(st.h, b0 + 16 * NCH)
    emit_store_fm(c, [(VF.rearrange("p a b -> p (a b)"), j * TC)
                      for j in range(NCH)], vout, TC, dt=BF16)
    c.close()
    return nc


def build_a0b_main():
    nc = bass.Bass("TRN2", target_bir_lowering=False)
    c = Ctx(nc)
    dr = {}
    _din(nc, dr, "x", (TC, D))
    _din(nc, dr, "kfm", (1024, T))
    _din(nc, dr, "vtok", (T, 1024))
    _din(nc, dr, "wqT", (1024, 1024))
    _din(nc, dr, "bq", (128, NCH))
    _din(nc, dr, "woT", (1024, 1024))
    _din(nc, dr, "bogb", (128, 3 * NCH))
    _din(nc, dr, "identf", (128, 128))
    y = nc.dram_tensor("y", [TC, D], F32, kind="ExternalOutput").ap()
    emit_setup(c, dr)
    bogb = emit_load_small(c, dr["bogb"], [128, 3 * NCH], "b_bogb")
    xb = XBufs(c, TC)
    emit_load_fm(c, xb.chunks(), dr["x"], TC)
    XN = c.sbuf("XN", [128, NCH, TC], BF16)
    emit_ln(c, xb.chunks(), XN, TC)
    QF = c.sbuf("b_qf", [128, NCH, TC], BF16)
    wpair = c.sbuf("b_wp", [128, NCH, 128], BF16)
    bq = emit_load_small(c, dr["bq"], [128, NCH], "b_bq")
    for m in range(NCH):
        emit_load_weight(c, None, 1024, 128, wpair,
                         dram_ap=dr["wqT"][:, m * 128:(m + 1) * 128])
        pp = PsumPool(nc)
        with nc.Block() as ph:
            mm_loop(c, ph, pp, wpair, NCH, [0], lambda m_: (0, 128), TC,
                    lambda k, ns, w: XN[:, k, ns * 512:ns * 512 + w],
                    lambda e, m_, ns, w, ps, m=m:
                        e.tensor_scalar(QF[:, m, ns * 512:ns * 512 + w], ps,
                                        bq[:, m:m + 1], None, ALU.add),
                    sem_role="b_q")
        pp.close()
    ones_col = c.sbuf("ones_col_bf", [128, 1], BF16)
    ones_row = c.sbuf("ones_row_bf", [1, 128], BF16)
    stk = ExitStack()
    negM = stk.enter_context(nc.sbuf_tensor("b_negm", [128, 1], F32))
    MROWb = stk.enter_context(nc.sbuf_tensor("b_mrow", [1, 16], BF16))
    stk2 = ExitStack()
    SQ = stk2.enter_context(nc.sbuf_tensor("b_sq", [128, NCH, 512], BF16))
    KB8 = stk2.enter_context(nc.sbuf_tensor("b_kb8", [128, NCH, 512], BF16))
    NRM = stk2.enter_context(nc.sbuf_tensor("b_nrm", [1, 512], F32))
    MX = stk2.enter_context(nc.sbuf_tensor("b_mx", [1, 24], F32))
    MS = stk2.enter_context(nc.sbuf_tensor("b_ms", [1, 8], F32))
    mbf = stk2.enter_context(nc.sbuf_tensor("b_mbf", [1, 1], BF16))
    AO = QF
    hoc = stk2.enter_context(nc.sbuf_tensor("b_hoc", [128, 2], BF16))
    MXQ = stk2.enter_context(nc.sbuf_tensor("b_mxq", [2, NCH, 4], F32))
    MXK = stk2.enter_context(nc.sbuf_tensor("b_mxk", [2, NCH, 16], F32))
    QMK = stk2.enter_context(nc.sbuf_tensor("b_qmk", [2, NCH], F32))
    KMK = stk2.enter_context(nc.sbuf_tensor("b_kmk", [2, NCH], F32))
    PRD2 = stk2.enter_context(nc.sbuf_tensor("b_prd2", [2, NCH], F32))
    PRDb = stk2.enter_context(nc.sbuf_tensor("b_prdb", [2, NCH], BF16))
    NPS = stk2.enter_context(nc.sbuf_tensor("b_nps", [NCH, 2], BF16))
    identbf = c.sbuf("identbf", [128, 128], BF16)
    with nc.Block() as ph:
        @ph.vector
        def _(e):
            e.memset(hoc[0:64, 0:1], 1.0)
            e.memset(hoc[64:128, 0:1], 0.0)
            e.memset(hoc[0:64, 1:2], 0.0)
            e.memset(hoc[64:128, 1:2], 1.0)
    ppn = PsumPool(nc)
    psn8 = [ppn.get([2, 512]) for _ in range(NCH)]
    for ns in range(4):
        with nc.Block() as ph:
            @ph.scalar
            def _(e, ns=ns):
                for k in range(NCH):
                    e.activation(SQ[:, k, :], QF[:, k, ns * 512:(ns + 1) * 512],
                                 AF.Square)
        with nc.Block() as ph:
            @ph.tensor
            def _(e):
                for k in range(NCH):
                    e.matmul(psn8[k][0:2, :], hoc[:, :], SQ[:, k, :],
                             start=True, stop=True)
        with nc.Block() as ph:
            @ph.vector
            def _(e, ns=ns):
                for k in range(NCH):
                    e.reduce_max(MXQ[0:2, k, ns:ns + 1], psn8[k][0:2, :],
                                 axis=mybir.AxisListType.X)
    for ns in range(16):
        with nc.Block() as ph:
            @ph.gpsimd
            def _(e, ns=ns):
                st = c.sem("b_kld")
                b0 = st.n
                for k in range(NCH):
                    st.inc(e.dma_start(out=KB8[:, k, :],
                                       in_=dr["kfm"][k * 128:(k + 1) * 128,
                                                     ns * 512:(ns + 1) * 512]), 16)
                e.wait_ge(st.h, b0 + 16 * NCH)
        with nc.Block() as ph:
            @ph.scalar
            def _(e):
                for k in range(NCH):
                    e.activation(SQ[:, k, :], KB8[:, k, :], AF.Square)
        with nc.Block() as ph:
            @ph.tensor
            def _(e):
                for k in range(NCH):
                    e.matmul(psn8[k][0:2, :], hoc[:, :], SQ[:, k, :],
                             start=True, stop=True)
        with nc.Block() as ph:
            @ph.vector
            def _(e, ns=ns):
                for k in range(NCH):
                    e.reduce_max(MXK[0:2, k, ns:ns + 1], psn8[k][0:2, :],
                                 axis=mybir.AxisListType.X)
    ppn.close()
    with nc.Block() as ph:
        @ph.vector
        def _(e):
            for k in range(NCH):
                e.reduce_max(QMK[0:2, k:k + 1], MXQ[0:2, k, :],
                             axis=mybir.AxisListType.X)
            for k in range(NCH):
                e.reduce_max(KMK[0:2, k:k + 1], MXK[0:2, k, :],
                             axis=mybir.AxisListType.X)
    with nc.Block() as ph:
        @ph.vector
        def _(e):
            e.tensor_tensor(PRD2[:, :], QMK[:, :], KMK[:, :], ALU.mult)
    with nc.Block() as ph:
        @ph.scalar
        def _(e):
            e.activation(QMK[:, :], PRD2[:, :], AF.Ln)
    with nc.Block() as ph:
        @ph.scalar
        def _(e):
            e.activation(KMK[:, :], QMK[:, :], AF.Exp, scale=0.5)
    with nc.Block() as ph:
        @ph.vector
        def _(e):
            e.tensor_scalar(PRDb[:, :], KMK[:, :], -1.0, None, ALU.mult)
    ppt = PsumPool(nc)
    psT = ppt.get([NCH, 2])
    with nc.Block() as ph:
        @ph.tensor
        def _(e):
            e.matmul(psT[0:NCH, 0:2], PRDb[0:2, 0:NCH], identbf[0:2, 0:2],
                     start=True, stop=True)
    with nc.Block() as ph:
        @ph.vector
        def _(e):
            e.tensor_copy(NPS[:, :], psT[0:NCH, 0:2])
    ppt.close()
    with nc.Block() as ph:
        @ph.gpsimd
        def _(e):
            st = c.sem("b_mrd")
            b0 = st.n
            for k in range(NCH):
                st.inc(e.dma_start(out=MROWb[0:1, 2 * k:2 * k + 2],
                                   in_=NPS[k:k + 1, 0:2]), 16)
            e.wait_ge(st.h, b0 + 16 * NCH)
    stk2.close()

    KH = stk.enter_context(nc.sbuf_tensor("b_kh", [64, T], BF16))
    VT = stk.enter_context(nc.sbuf_tensor("b_vt", [128, 64, 65], BF16))
    scrS = stk.enter_context(nc.sbuf_tensor("b_scrs", [128, 2, 512], F32))
    PN = stk.enter_context(nc.sbuf_tensor("b_pn", [128, 2, 512], BF16))
    obuf = stk.enter_context(nc.sbuf_tensor("b_obuf", [64, 512], BF16))
    drow = stk.enter_context(nc.sbuf_tensor("b_drow", [1, 2, 512], F32))
    dbf = stk.enter_context(nc.sbuf_tensor("b_dbf", [1, 2, 512], BF16))
    QH = stk.enter_context(nc.sbuf_tensor("b_qh", [64, TC], BF16))
    AOH = stk.enter_context(nc.sbuf_tensor("b_aoh", [64, TC], BF16))
    with nc.Block() as ph:
        @ph.vector
        def _(e):
            e.memset(VT[:, :, 64:65], 1.0)
    for h in range(16):
        with nc.Block() as ph:
            @ph.gpsimd
            def _(e, h=h):
                st = c.sem("b_hld")
                b0 = st.n
                st.inc(e.dma_start(out=KH[:, :],
                                   in_=dr["kfm"][h * 64:(h + 1) * 64, :]), 16)
                st.inc(e.dma_start(out=QH[:, :],
                                   in_=QF[(h % 2) * 64:(h % 2) * 64 + 64,
                                          h // 2, :]), 16)
                for kc in range(64):
                    st.inc(e.dma_start(
                        out=VT[:, kc, 0:64],
                        in_=dr["vtok"][kc * 128:(kc + 1) * 128,
                                       h * 64:(h + 1) * 64]), 16)
                e.wait_ge(st.h, b0 + 16 * 66)
        fS, fM, fE, fO = (c.sem(f"b_a{k}") for k in "abcd")
        fD, fB, fV = (c.sem(f"b_a{k}") for k in "efg")
        bfS, bfM, bfE, bfO = fS.n, fM.n, fE.n, fO.n
        bfD, bfB, bfV = fD.n, fB.n, fV.n
        pp = PsumPool(nc)
        psS = pp.get([128, 2, 512], F32)
        psO = pp.get([128, 2, 512], F32)
        psD = pp.get([128, 2, 512], F32)
        psbb = pp.get([128, 512], F32)
        qpart = slice((h % 2) * 64, (h % 2) * 64 + 64)
        qch = h // 2
        with nc.Block() as ph:
            @ph.tensor
            def _(e, h=h):
                e.matmul(psbb[:, 0:1], ones_row[0:1, :], MROWb[0:1, h:h + 1],
                         start=True, stop=True)
        with nc.Block() as ph:
            @ph.vector
            def _(e):
                e.tensor_copy(negM[:, :], psbb[:, 0:1])
        with nc.Block() as ph:
            @ph.tensor
            def _(e, h=h):
                for it in range(4 + 1):
                    if it >= 1:
                        jt = it - 1
                        e.wait_ge(fD.h, bfD + jt + 1)
                        if jt >= 2:
                            e.wait_ge(fV.h, bfV + jt - 1)
                        fB.inc(e.matmul(psD[0:64, jt % 2, 0:512],
                                        ones_row[0:1, 0:64], dbf[0:1, jt % 2, :],
                                        start=True, stop=True), 1)
                    if it < 4:
                        if it >= 2:
                            e.wait_ge(fV.h, bfV + it - 1)
                        for kc in range(64):
                            g = 64 * it + kc
                            if g >= 2:
                                e.wait_ge(fM.h, bfM + g - 1)
                            e.matmul(psS[:, g % 2, 0:512],
                                     KH[:, kc * 128:(kc + 1) * 128],
                                     QH[:, it * 512:(it + 1) * 512],
                                     start=True, stop=True).then_inc(fS.h, 1)
                            fS.n += 1
                            e.wait_ge(fE.h, bfE + g + 1)
                            fO.inc(e.matmul(psO[0:65, it % 2, 0:512],
                                            VT[:, kc, :], PN[:, g % 2, :],
                                            start=(kc == 0), stop=(kc == 63)), 1)

            @ph.vector
            def _(e, h=h):
                for it in range(4 + 1):
                    if it < 4:
                        for kc in range(64):
                            g = 64 * it + kc
                            e.wait_ge(fS.h, bfS + g + 1)
                            if g >= 2:
                                e.wait_ge(fE.h, bfE + g - 1)
                            fM.inc(e.tensor_scalar(scrS[:, g % 2, :],
                                                   psS[:, g % 2, 0:512],
                                                   negM[:, 0:1], None, ALU.add), 1)
                    if it >= 1:
                        jt = it - 1
                        e.wait_ge(fB.h, bfB + jt + 1)
                        e.tensor_copy(obuf[:, :], psO[0:64, jt % 2, 0:512])
                        fV.inc(e.tensor_tensor(
                            AOH[:, jt * 512:(jt + 1) * 512],
                            obuf[:, :], psD[0:64, jt % 2, 0:512], ALU.mult), 1)
                    if it < 4:
                        e.wait_ge(fO.h, bfO + 64 * (it + 1))
                        if it >= 2:
                            e.wait_ge(fB.h, bfB + it - 1)
                        e.reciprocal(drow[0:1, it % 2, :],
                                     psO[64:65, it % 2, 0:512])
                        fD.inc(e.tensor_copy(dbf[0:1, it % 2, :],
                                             drow[0:1, it % 2, :]), 1)

            @ph.scalar
            def _(e, h=h):
                for it in range(4):
                    for kc in range(64):
                        g = 64 * it + kc
                        e.wait_ge(fM.h, bfM + g + 1)
                        if g >= 2:
                            e.wait_ge(fO.h, bfO + g - 1)
                        fE.inc(e.activation(PN[:, g % 2, :], scrS[:, g % 2, :],
                                            AF.Exp, scale=0.125), 1)
        with nc.Block() as ph:
            @ph.gpsimd
            def _(e, h=h, qpart=qpart, qch=qch):
                st = c.sem("b_hst")
                b0 = st.n
                st.inc(e.dma_start(out=AO[qpart, qch, :], in_=AOH[:, :]), 16)
                e.wait_ge(st.h, b0 + 16)
        pp.close()
    stk.close()
    T1 = c.sbuf("b_t1", [128, NCH, TC], BF16)
    for m in range(NCH):
        emit_load_weight(c, None, 1024, 128, wpair,
                         dram_ap=dr["woT"][:, m * 128:(m + 1) * 128])
        pp = PsumPool(nc)
        with nc.Block() as ph:
            def evac(e, m_, ns, w, ps, m=m):
                sl = slice(ns * 512, ns * 512 + w)
                e.tensor_scalar(xb.scr[:, 0:w], ps, bogb[:, m:m + 1], None,
                                ALU.add)
                return e.tensor_tensor(T1[:, m, sl], xb.scr[:, 0:w],
                                       XN[:, m, sl], ALU.add)
            mm_loop(c, ph, pp, wpair, NCH, [0], lambda m_: (0, 128), TC,
                    lambda k, ns, w: AO[:, k, ns * 512:ns * 512 + w],
                    evac, sem_role="b_wo")
        pp.close()
    XN2 = XN
    emit_ln(c, [(T1.rearrange("p a b -> p (a b)"), j * TC) for j in range(NCH)],
            XN2, TC,
            gamma=lambda j: bogb[:, NCH + j:NCH + j + 1],
            beta=lambda j: bogb[:, 2 * NCH + j:2 * NCH + j + 1])
    with nc.Block() as ph:
        @ph.vector
        def _(e):
            for j in range(NCH):
                for ns in range(4):
                    sl = slice(ns * 512, (ns + 1) * 512)
                    e.tensor_tensor(xb.scr[:, 0:512], XN2[:, j, sl],
                                    xb.cur(j)[:, sl], ALU.add)
                    e.tensor_copy(xb.cur(j)[:, sl], xb.scr[:, 0:512])
    emit_store_fm(c, xb.chunks(), y, TC)
    c.close()
    return nc


def run_a0b(x_chunks, w):
    nc = _nc_cache('a0b_kv', build_a0b_kv)
    maps = []
    for cidx in range(NCORES):
        m = {"x": x_chunks[cidx], "wkT": w["b_wkT"], "wvT": w["b_wvT"],
             "bkv": w["b_bkv"]}
        m.update(_CONSTS)
        maps.append(m)
    res = run_bass_kernel_spmd(nc, maps, core_ids=list(range(NCORES)))
    kv = [(res.results[i]["kfm"], res.results[i]["vtok"]) for i in range(NCORES)]
    nc = _nc_cache('a0b_main', build_a0b_main)
    maps = []
    for cidx in range(NCORES):
        s = cidx // CPS
        kf = np.concatenate([kv[4 * s + q][0] for q in range(4)], axis=1)
        vt = np.concatenate([kv[4 * s + q][1] for q in range(4)], axis=0)
        m = {"x": x_chunks[cidx], "kfm": f32c(kf), "vtok": f32c(vt),
             "wqT": w["b_wqT"], "bq": w["b_bq"], "woT": w["b_woT"],
             "bogb": w["b_bogb"]}
        m.update(_CONSTS)
        maps.append(m)
    res = run_bass_kernel_spmd(nc, maps, core_ids=list(range(NCORES)))
    return [res.results[i]["y"] for i in range(NCORES)]



_NCC = {}


def _nc_cache(key, fn):
    if key not in _NCC:
        _NCC[key] = fn()
    return _NCC[key]


def prep_a1b(inp):
    w = {}
    w["1b_wqkv"] = f32c(inp["L1_attn_b_W_qkv"])
    w["1b_wug"] = f32c(inp["L1_attn_b_W_ug"])
    w["1b_bw"] = f32c(inp["L1_attn_b_B_w"])
    w["1b_bug"] = col_layout(inp["L1_attn_b_b_ug"])
    w["1b_g1b1"] = f32c(np.concatenate(
        [col_layout(inp["L1_attn_b_ln1_g"]),
         col_layout(inp["L1_attn_b_ln1_b"])], axis=1))
    w["1b_acol"] = f32c(np.asarray(inp["L1_attn_b_A"]).reshape(128, 1))
    w["1b_wo"] = f32c(inp["L1_attn_b_W_O"])
    w["1b_boc"] = col_layout(inp["L1_attn_b_b_O"])
    w["1b_cw"] = f32c(inp["L1_attn_b_C_w"])
    w["1b_g2b2"] = f32c(np.concatenate(
        [col_layout(inp["L1_attn_b_ln2_g"]),
         col_layout(inp["L1_attn_b_ln2_b"])], axis=1))
    w["1b_w1"] = f32c(inp["L1_attn_b_mlp_W1"])
    w["1b_b1c"] = col_layout(inp["L1_attn_b_mlp_b1"])
    w["1b_w2"] = f32c(inp["L1_attn_b_mlp_W2"])
    w["1b_b2c"] = col_layout(inp["L1_attn_b_mlp_b2"])
    q = np.arange(512)[None, :]
    m = np.zeros((128, 12, 512), np.float32)
    for kc in range(6):
        krel = (np.arange(kc * 128, kc * 128 + 128) - 256)[:, None]
        mid = (krel > q - 256) & (krel <= q)
        first = mid & (krel >= 0)
        m[:, 6 + kc, :] = np.where(mid, 0.0, -30000.0)
        m[:, kc, :] = np.where(first, 0.0, -30000.0)
    w["1b_masks"] = f32c(m.reshape(128, 12 * 512))
    mm = m.copy()
    mm[:, 0:6, :] = m[:, 6:12, :]
    w["1b_masks_mid"] = f32c(mm.reshape(128, 12 * 512))
    return w


def build_a1b_p1():
    nc = bass.Bass("TRN2", target_bir_lowering=False)
    c = Ctx(nc)
    dr = {}
    _din(nc, dr, "x", (TC, D))
    _din(nc, dr, "wqkv", (1024, 3072))
    _din(nc, dr, "wug", (1024, 1152))
    _din(nc, dr, "bw", (1024, 128))
    _din(nc, dr, "bug", (128, 9))
    _din(nc, dr, "g1b1", (128, 16))
    _din(nc, dr, "identf", (128, 128))
    qkvo = nc.dram_tensor("qkvfm", [3072, TC], F32, kind="ExternalOutput").ap()
    gout = nc.dram_tensor("gfm", [1024, TC], F32, kind="ExternalOutput").ap()
    dout = nc.dram_tensor("drv", [128, TC], F32, kind="ExternalOutput").ap()
    emit_setup(c, dr)
    bug = emit_load_small(c, dr["bug"], [128, 9], "1b_bug")
    g1b1 = emit_load_small(c, dr["g1b1"], [128, 16], "1b_g1b1")
    xb = XBufs(c, TC)
    emit_load_fm(c, xb.chunks(), dr["x"], TC)
    XNI = c.sbuf("XN", [128, NCH, TC], BF16)
    emit_ln(c, xb.chunks(), XNI, TC)
    XN1 = c.sbuf("1b_xn1", [128, NCH, TC], BF16)
    emit_ln(c, [(XNI.rearrange("p a b -> p (a b)"), j * TC) for j in range(NCH)],
            XN1, TC, gamma=lambda j: g1b1[:, j:j + 1],
            beta=lambda j: g1b1[:, NCH + j:NCH + j + 1])
    wpair = c.sbuf("1b_wp", [128, NCH, 128], BF16)
    wpair2 = c.sbuf("1b_wp2", [128, NCH, 128], BF16)
    OB = c.sbuf("1b_ob", [128, TC], BF16)
    st = c.sem("1b_st")
    jobs = [(dr["wqkv"], m, qkvo, m, None) for m in range(24)] + \
           [(dr["wug"], m, gout, m - 1, m) for m in range(1, 9)]
    for (wsrc, m, dst, mo, bci) in jobs:
        emit_load_weight(c, None, 1024, 128, wpair,
                         dram_ap=wsrc[:, m * 128:(m + 1) * 128])
        pp = PsumPool(nc)
        with nc.Block() as ph:
            if bci is None:
                def evac(e, m_, ns, w, ps):
                    return e.tensor_copy(OB[:, ns * 512:ns * 512 + w], ps)
            else:
                def evac(e, m_, ns, w, ps, bci=bci):
                    return e.activation(OB[:, ns * 512:ns * 512 + w], ps,
                                        AF.Sigmoid, bias=bug[:, bci:bci + 1])
            mm_loop(c, ph, pp, wpair, NCH, [0], lambda m_: (0, 128), TC,
                    lambda k, ns, w: XN1[:, k, ns * 512:ns * 512 + w],
                    evac, evac_eng=("vector" if bci is None else "scalar"),
                    sem_role="1b_mm")
        pp.close()
        with nc.Block() as ph:
            @ph.gpsimd
            def _(e, dst=dst, mo=mo):
                b0 = st.n
                st.inc(e.dma_start(out=dst[mo * 128:(mo + 1) * 128, :],
                                   in_=OB[:, :]), 16)
                e.wait_ge(st.h, b0 + 16)
    emit_load_weight(c, None, 1024, 128, wpair, dram_ap=dr["wug"][:, 0:128])
    emit_load_weight(c, None, 1024, 128, wpair2, dram_ap=dr["bw"])
    DRV = c.sbuf("1b_drv", [128, TC], F32)
    pp = PsumPool(nc)

    def wb2(k, m):
        return wpair[:, k, 0:128] if k < NCH else wpair2[:, k - NCH, 0:128]
    with nc.Block() as ph:
        mm_loop(c, ph, pp, wb2, 2 * NCH, [0], lambda m_: (0, 128), TC,
                lambda k, ns, w: XN1[:, k % NCH, ns * 512:ns * 512 + w],
                lambda e, m_, ns, w, ps: e.tensor_scalar(
                    DRV[:, ns * 512:ns * 512 + w], ps, bug[:, 0:1], None,
                    ALU.add),
                sem_role="1b_dv")
    pp.close()
    with nc.Block() as ph:
        @ph.gpsimd
        def _(e):
            b0 = st.n
            st.inc(e.dma_start(out=dout, in_=DRV[:, :]), 16)
            e.wait_ge(st.h, b0 + 16)
    c.close()
    return nc


def build_a1b_p2():
    nc = bass.Bass("TRN2", target_bir_lowering=False)
    c = Ctx(nc)
    dr = {}
    _din(nc, dr, "drv", (128, T))
    _din(nc, dr, "acol", (128, 1))
    y = nc.dram_tensor("sfm", [128, T], F32, kind="ExternalOutput").ap()
    DB = c.sbuf("2_db", [128, T], F32)
    HB = c.sbuf("2_hb", [128, T], F32)
    AB = c.sbuf("2_ab", [128, TC], F32)
    AC = c.sbuf("2_ac", [128, 1], F32)
    ON = c.sbuf("2_on", [128, TC], F32)
    STC = c.sbuf("2_stc", [128, 2], F32)
    ld = c.sem("2_ld")
    st = c.sem("2_sto")
    with nc.Block() as ph:
        @ph.gpsimd
        def _(e):
            ld.inc(e.dma_start(out=DB[:, :], in_=dr["drv"]), 16)
            ld.inc(e.dma_start(out=AC[:, :], in_=dr["acol"]), 16)
            e.wait_ge(ld.h, ld.n)
    with nc.Block() as ph:
        @ph.vector
        def _(e):
            e.memset(ON[:, :], 1.0)
            e.memset(STC[:, :], 0.0)
    with nc.Block() as ph:
        @ph.vector
        def _(e):
            e.tensor_scalar(AB[:, :], ON[:, :], AC[:, 0:1], None, ALU.mult)
    for s in range(4):
        with nc.Block() as ph:
            @ph.vector
            def _(e, s=s):
                sl = slice(s * TC, (s + 1) * TC)
                e.tensor_tensor_scan(HB[:, sl], AB[:, :], DB[:, sl],
                                     STC[:, s % 2:s % 2 + 1], ALU.mult, ALU.add)
                e.tensor_copy(STC[:, (s + 1) % 2:(s + 1) % 2 + 1],
                              HB[:, (s + 1) * TC - 1:(s + 1) * TC])
    with nc.Block() as ph:
        @ph.gpsimd
        def _(e):
            st.inc(e.dma_start(out=y, in_=HB[:, :]), 16)
            e.wait_ge(st.h, st.n)
    c.close()
    return nc


def build_a1b_p3():
    nc = bass.Bass("TRN2", target_bir_lowering=False)
    c = Ctx(nc)
    dr = {}
    _din(nc, dr, "x", (TC, D))
    _din(nc, dr, "qfm", (1024, TC))
    _din(nc, dr, "kext", (1024, TC + 256))
    _din(nc, dr, "vext", (TC + 256, 1024))
    _din(nc, dr, "gfm", (1024, TC))
    _din(nc, dr, "sfm", (128, TC))
    _din(nc, dr, "wo", (1024, 1024))
    _din(nc, dr, "boc", (128, NCH))
    _din(nc, dr, "cw", (128, 1024))
    _din(nc, dr, "g2b2", (128, 16))
    _din(nc, dr, "w1", (1024, 4096))
    _din(nc, dr, "b1c", (128, 32))
    _din(nc, dr, "w2", (4096, 1024))
    _din(nc, dr, "b2c", (128, NCH))
    _din(nc, dr, "masks", (128, 12 * 512))
    _din(nc, dr, "identf", (128, 128))
    y = nc.dram_tensor("y", [TC, D], F32, kind="ExternalOutput").ap()
    emit_setup(c, dr)
    boc = emit_load_small(c, dr["boc"], [128, NCH], "3_boc")
    g2b2 = emit_load_small(c, dr["g2b2"], [128, 16], "3_g2b2")
    b1c = emit_load_small(c, dr["b1c"], [128, 32], "3_b1c")
    b2c = emit_load_small(c, dr["b2c"], [128, NCH], "3_b2c")
    ones_col = c.sbuf("ones_col_bf", [128, 1], BF16)
    ones_row = c.sbuf("ones_row_bf", [1, 128], BF16)
    identbf = c.sbuf("identbf", [128, 128], BF16)
    c.sbuf("ld_stage", [128, 1024], F32)
    c.sbuf("ln_row", [1, 2, 2, 512], F32)
    c.sbuf("ln_var", [1, 2, 512], F32)
    c.sbuf("ln_scrB", [128, 512], F32)
    c.sbuf("ln_rbf", [1, 2, 2, 512], BF16)
    c.sbuf("ln_scr2", [1, 512], F32)
    c.sbuf("ln_sq", [128, 2, 512], BF16)
    c.sbuf("ln_scr", [128, 512], F32)
    XNI = c.sbuf("XN", [128, NCH, TC], BF16)
    wpair = c.sbuf("3_wp", [128, NCH, 128], BF16)
    scrA = c.sbuf("3_scra", [128, 512], F32)
    scrB = c.sbuf("3_scrb", [128, 512], F32)
    scrC = c.sbuf("3_scrc", [128, 512], F32)
    stkX = ExitStack()
    xb = XBufs.__new__(XBufs)
    xb.tcols = TC
    xb.bufs = [stkX.enter_context(nc.sbuf_tensor(f"3x{i}", [128, TC], F32))
               for i in range(NCH)]
    xb.scr = stkX.enter_context(nc.sbuf_tensor("3xscr", [128, 512], F32))
    emit_load_fm(c, xb.chunks(), dr["x"], TC)
    emit_ln(c, xb.chunks(), XNI, TC)
    stkX.close()
    stk = ExitStack()
    AO = stk.enter_context(nc.sbuf_tensor("3_ao", [128, NCH, TC], BF16))
    MSKS = stk.enter_context(nc.sbuf_tensor("3_msk", [128, 12, 512], F32))
    negM = stk.enter_context(nc.sbuf_tensor("3_negm", [128, 1], F32))
    MROWb = stk.enter_context(nc.sbuf_tensor("3_mrow", [1, 16], BF16))
    with nc.Block() as ph:
        @ph.gpsimd
        def _(e):
            st = c.sem("3_mld")
            b0 = st.n
            st.inc(e.dma_start(out=MSKS.rearrange("p a b -> p (a b)"),
                               in_=dr["masks"]), 16)
            e.wait_ge(st.h, b0 + 16)
    stk2 = ExitStack()
    SQ = stk2.enter_context(nc.sbuf_tensor("3_sq", [128, NCH, 512], BF16))
    KB8 = stk2.enter_context(nc.sbuf_tensor("3_kb8", [128, NCH, 512], BF16))
    hoc = stk2.enter_context(nc.sbuf_tensor("3_hoc", [128, 2], BF16))
    MXQ = stk2.enter_context(nc.sbuf_tensor("3_mxq", [2, NCH, 4], F32))
    MXK = stk2.enter_context(nc.sbuf_tensor("3_mxk", [2, NCH, 5], F32))
    QMK = stk2.enter_context(nc.sbuf_tensor("3_qmk", [2, NCH], F32))
    KMK = stk2.enter_context(nc.sbuf_tensor("3_kmk", [2, NCH], F32))
    PRD2 = stk2.enter_context(nc.sbuf_tensor("3_prd2", [2, NCH], F32))
    PRDb = stk2.enter_context(nc.sbuf_tensor("3_prdb", [2, NCH], BF16))
    NPS = stk2.enter_context(nc.sbuf_tensor("3_nps", [NCH, 2], BF16))
    with nc.Block() as ph:
        @ph.vector
        def _(e):
            e.memset(hoc[0:64, 0:1], 1.0)
            e.memset(hoc[64:128, 0:1], 0.0)
            e.memset(hoc[0:64, 1:2], 0.0)
            e.memset(hoc[64:128, 1:2], 1.0)
    ppn = PsumPool(nc)
    psn8 = [ppn.get([2, 512]) for _ in range(NCH)]
    for src_name, n_sl, MXT in (("qfm", 4, MXQ), ("kext", 5, MXK)):
        wtot = TC if src_name == "qfm" else TC + 256
        for ns in range(n_sl):
            w_ = min(512, wtot - ns * 512)
            with nc.Block() as ph:
                @ph.gpsimd
                def _(e, ns=ns, w_=w_, src_name=src_name):
                    st = c.sem("3_kld")
                    b0 = st.n
                    for k in range(NCH):
                        st.inc(e.dma_start(
                            out=KB8[:, k, 0:w_],
                            in_=dr[src_name][k * 128:(k + 1) * 128,
                                             ns * 512:ns * 512 + w_]), 16)
                    e.wait_ge(st.h, b0 + 16 * NCH)
            with nc.Block() as ph:
                @ph.scalar
                def _(e, w_=w_):
                    for k in range(NCH):
                        e.activation(SQ[:, k, 0:w_], KB8[:, k, 0:w_], AF.Square)
            with nc.Block() as ph:
                @ph.tensor
                def _(e, w_=w_):
                    for k in range(NCH):
                        e.matmul(psn8[k][0:2, 0:w_], hoc[:, :], SQ[:, k, 0:w_],
                                 start=True, stop=True)
            with nc.Block() as ph:
                @ph.vector
                def _(e, ns=ns, w_=w_, MXT=MXT):
                    for k in range(NCH):
                        e.reduce_max(MXT[0:2, k, ns:ns + 1], psn8[k][0:2, 0:w_],
                                     axis=mybir.AxisListType.X)
    ppn.close()
    with nc.Block() as ph:
        @ph.vector
        def _(e):
            for k in range(NCH):
                e.reduce_max(QMK[0:2, k:k + 1], MXQ[0:2, k, :],
                             axis=mybir.AxisListType.X)
            for k in range(NCH):
                e.reduce_max(KMK[0:2, k:k + 1], MXK[0:2, k, :],
                             axis=mybir.AxisListType.X)
    with nc.Block() as ph:
        @ph.vector
        def _(e):
            e.tensor_tensor(PRD2[:, :], QMK[:, :], KMK[:, :], ALU.mult)
    with nc.Block() as ph:
        @ph.scalar
        def _(e):
            e.activation(QMK[:, :], PRD2[:, :], AF.Ln)
    with nc.Block() as ph:
        @ph.scalar
        def _(e):
            e.activation(KMK[:, :], QMK[:, :], AF.Exp, scale=0.5)
    with nc.Block() as ph:
        @ph.vector
        def _(e):
            e.tensor_scalar(PRDb[:, :], KMK[:, :], -1.0, None, ALU.mult)
    ppt = PsumPool(nc)
    psT = ppt.get([NCH, 2])
    with nc.Block() as ph:
        @ph.tensor
        def _(e):
            e.matmul(psT[0:NCH, 0:2], PRDb[0:2, 0:NCH], identbf[0:2, 0:2],
                     start=True, stop=True)
    with nc.Block() as ph:
        @ph.vector
        def _(e):
            e.tensor_copy(NPS[:, :], psT[0:NCH, 0:2])
    ppt.close()
    with nc.Block() as ph:
        @ph.gpsimd
        def _(e):
            st = c.sem("3_mrd")
            b0 = st.n
            for k in range(NCH):
                st.inc(e.dma_start(out=MROWb[0:1, 2 * k:2 * k + 2],
                                   in_=NPS[k:k + 1, 0:2]), 16)
            e.wait_ge(st.h, b0 + 16 * NCH)
    stk2.close()
    stkA = ExitStack()
    KH = stkA.enter_context(nc.sbuf_tensor("3_kh", [64, TC + 256], BF16))
    VT = stkA.enter_context(nc.sbuf_tensor("3_vt", [128, 18, 65], BF16))
    QH = stkA.enter_context(nc.sbuf_tensor("3_qh", [64, TC], BF16))
    AOH = stkA.enter_context(nc.sbuf_tensor("3_aoh", [64, TC], BF16))
    scrS = stkA.enter_context(nc.sbuf_tensor("3_scrs", [128, 2, 512], F32))
    PN = stkA.enter_context(nc.sbuf_tensor("3_pn", [128, 2, 512], BF16))
    obuf = stkA.enter_context(nc.sbuf_tensor("3_obuf", [64, 512], BF16))
    drow = stkA.enter_context(nc.sbuf_tensor("3_drow", [1, 2, 512], F32))
    dbf = stkA.enter_context(nc.sbuf_tensor("3_dbf", [1, 2, 512], BF16))
    with nc.Block() as ph:
        @ph.vector
        def _(e):
            e.memset(VT[:, :, 64:65], 1.0)
    for h in range(16):
        qpart = slice((h % 2) * 64, (h % 2) * 64 + 64)
        qch = h // 2
        with nc.Block() as ph:
            @ph.gpsimd
            def _(e, h=h):
                st = c.sem("3_hld")
                b0 = st.n
                st.inc(e.dma_start(out=KH[:, :],
                                   in_=dr["kext"][h * 64:(h + 1) * 64, :]), 16)
                st.inc(e.dma_start(out=QH[:, :],
                                   in_=dr["qfm"][h * 64:(h + 1) * 64, :]), 16)
                for tc_ in range(18):
                    st.inc(e.dma_start(
                        out=VT[:, tc_, 0:64],
                        in_=dr["vext"][tc_ * 128:(tc_ + 1) * 128,
                                       h * 64:(h + 1) * 64]), 16)
                e.wait_ge(st.h, b0 + 16 * 20)
        fS, fM, fE, fO = (c.sem(f"3a{k}") for k in "abcd")
        fD, fB, fV = (c.sem(f"3a{k}") for k in "efg")
        bfS, bfM, bfE, bfO = fS.n, fM.n, fE.n, fO.n
        bfD, bfB, bfV = fD.n, fB.n, fV.n
        pp = PsumPool(nc)
        psS = pp.get([128, 2, 512], F32)
        psO = pp.get([128, 2, 512], F32)
        psD = pp.get([128, 2, 512], F32)
        psbb = pp.get([128, 512], F32)
        with nc.Block() as ph:
            @ph.tensor
            def _(e, h=h):
                e.matmul(psbb[:, 0:1], ones_row[0:1, :], MROWb[0:1, h:h + 1],
                         start=True, stop=True)
        with nc.Block() as ph:
            @ph.vector
            def _(e):
                e.tensor_copy(negM[:, :], psbb[:, 0:1])
        with nc.Block() as ph:
            @ph.tensor
            def _(e, h=h):
                for it in range(4 + 1):
                    if it >= 1:
                        jt = it - 1
                        e.wait_ge(fD.h, bfD + jt + 1)
                        if jt >= 2:
                            e.wait_ge(fV.h, bfV + jt - 1)
                        fB.inc(e.matmul(psD[0:64, jt % 2, 0:512],
                                        ones_row[0:1, 0:64], dbf[0:1, jt % 2, :],
                                        start=True, stop=True), 1)
                    if it < 4:
                        if it >= 2:
                            e.wait_ge(fV.h, bfV + it - 1)
                        for kc in range(6):
                            g = 6 * it + kc
                            if g >= 2:
                                e.wait_ge(fM.h, bfM + g - 1)
                            k0 = it * 512 + kc * 128
                            e.matmul(psS[:, g % 2, 0:512],
                                     KH[:, k0:k0 + 128],
                                     QH[:, it * 512:(it + 1) * 512],
                                     start=True, stop=True).then_inc(fS.h, 1)
                            fS.n += 1
                            e.wait_ge(fE.h, bfE + g + 1)
                            fO.inc(e.matmul(psO[0:65, it % 2, 0:512],
                                            VT[:, 4 * it + kc, :],
                                            PN[:, g % 2, :],
                                            start=(kc == 0), stop=(kc == 5)), 1)

            @ph.vector
            def _(e, h=h):
                for it in range(4 + 1):
                    if it < 4:
                        for kc in range(6):
                            g = 6 * it + kc
                            slot = kc if it == 0 else 6 + kc
                            e.wait_ge(fS.h, bfS + g + 1)
                            if g >= 2:
                                e.wait_ge(fE.h, bfE + g - 1)
                            fM.inc(e.scalar_tensor_tensor(
                                scrS[:, g % 2, :], psS[:, g % 2, 0:512],
                                negM[:, 0:1], MSKS[:, slot, :],
                                ALU.add, ALU.add), 1)
                    if it >= 1:
                        jt = it - 1
                        e.wait_ge(fB.h, bfB + jt + 1)
                        e.tensor_copy(obuf[:, :], psO[0:64, jt % 2, 0:512])
                        fV.inc(e.tensor_tensor(
                            AOH[:, jt * 512:(jt + 1) * 512],
                            obuf[:, :], psD[0:64, jt % 2, 0:512], ALU.mult), 1)
                    if it < 4:
                        e.wait_ge(fO.h, bfO + 6 * (it + 1))
                        if it >= 2:
                            e.wait_ge(fB.h, bfB + it - 1)
                        e.reciprocal(drow[0:1, it % 2, :],
                                     psO[64:65, it % 2, 0:512])
                        fD.inc(e.tensor_copy(dbf[0:1, it % 2, :],
                                             drow[0:1, it % 2, :]), 1)

            @ph.scalar
            def _(e, h=h):
                for it in range(4):
                    for kc in range(6):
                        g = 6 * it + kc
                        e.wait_ge(fM.h, bfM + g + 1)
                        if g >= 2:
                            e.wait_ge(fO.h, bfO + g - 1)
                        fE.inc(e.activation(PN[:, g % 2, :], scrS[:, g % 2, :],
                                            AF.Exp, scale=0.125), 1)
        with nc.Block() as ph:
            @ph.gpsimd
            def _(e, h=h, qpart=qpart, qch=qch):
                st = c.sem("3_hst")
                b0 = st.n
                st.inc(e.dma_start(out=AO[qpart, qch, :], in_=AOH[:, :]), 16)
                e.wait_ge(st.h, b0 + 16)
        pp.close()
    stkA.close()
    stkC = ExitStack()
    SFMB = stkC.enter_context(nc.sbuf_tensor("3_sfmb", [128, TC], BF16))
    CWB = stkC.enter_context(nc.sbuf_tensor("3_cwb", [128, 1024], BF16))
    GSm = stkC.enter_context(nc.sbuf_tensor("3_gsm", [128, TC], F32))
    YSm = stkC.enter_context(nc.sbuf_tensor("3_ysm", [128, TC], F32))
    with nc.Block() as ph:
        @ph.gpsimd
        def _(e):
            st = c.sem("3_cl")
            b0 = st.n
            st.inc(e.dma_start(out=SFMB[:, :], in_=dr["sfm"]), 16)
            st.inc(e.dma_start(out=CWB[:, :], in_=dr["cw"]), 16)
            e.wait_ge(st.h, b0 + 32)
    for m in range(NCH):
        emit_load_weight(c, None, 1024, 128, wpair,
                         dram_ap=dr["wo"][:, m * 128:(m + 1) * 128])
        with nc.Block() as ph:
            @ph.gpsimd
            def _(e, m=m):
                st = c.sem("3_gl")
                b0 = st.n
                st.inc(e.dma_start(out=GSm[:, :],
                                   in_=dr["gfm"][m * 128:(m + 1) * 128, :]), 16)
                e.wait_ge(st.h, b0 + 16)
        pp = PsumPool(nc)
        with nc.Block() as ph:
            mm_loop(c, ph, pp,
                    lambda k, m_, m=m: CWB[:, m * 128:(m + 1) * 128],
                    1, [0], lambda m_: (0, 128), TC,
                    lambda k, ns, w: SFMB[:, ns * 512:ns * 512 + w],
                    lambda e, m_, ns, w, ps:
                        e.tensor_copy(YSm[:, ns * 512:ns * 512 + w], ps),
                    sem_role="3_ys")
        pp.close()
        pp = PsumPool(nc)
        with nc.Block() as ph:
            def evac(e, m_, ns, w, ps, m=m):
                sl = slice(ns * 512, ns * 512 + w)
                e.tensor_scalar(scrA[:, 0:w], ps, boc[:, m:m + 1], None,
                                ALU.add)
                e.tensor_tensor(scrB[:, 0:w], scrA[:, 0:w], YSm[:, sl],
                                ALU.subtract)
                e.tensor_tensor(scrA[:, 0:w], GSm[:, sl], scrB[:, 0:w],
                                ALU.mult)
                e.tensor_tensor(scrB[:, 0:w], scrA[:, 0:w], YSm[:, sl],
                                ALU.add)
                e.tensor_copy(scrC[:, 0:w], XNI[:, m, sl])
                e.tensor_tensor(scrA[:, 0:w], scrB[:, 0:w], scrC[:, 0:w],
                                ALU.add)
                return e.tensor_copy(XNI[:, m, sl], scrA[:, 0:w])
            mm_loop(c, ph, pp, wpair, NCH, [0], lambda m_: (0, 128), TC,
                    lambda k, ns, w: AO[:, k, ns * 512:ns * 512 + w],
                    evac, sem_role="3_wo")
        pp.close()
    stkC.close()
    stk.close()
    H = c.sbuf("3_h", [128, NCH, TC], BF16)
    PRDm = c.sbuf("3_prdm", [128, 32, 512], BF16)
    w2b = c.sbuf("3_w2b", [128, 32, 128], BF16)
    emit_ln(c, [(XNI.rearrange("p a b -> p (a b)"), j * TC) for j in range(NCH)],
            H, TC, gamma=lambda j: g2b2[:, j:j + 1],
            beta=lambda j: g2b2[:, NCH + j:NCH + j + 1])
    xb2 = XBufs(c, TC)
    emit_load_fm(c, xb2.chunks(), dr["x"], TC)
    for ns in range(4):
        for mi in range(32):
            emit_load_weight(c, None, 1024, 128, wpair,
                             dram_ap=dr["w1"][:, mi * 128:(mi + 1) * 128])
            pp = PsumPool(nc)
            with nc.Block() as ph:
                mm_loop(c, ph, pp, wpair, NCH, [0], lambda m_: (0, 128), 512,
                        lambda k, ns_, w, ns=ns:
                            H[:, k, ns * 512:ns * 512 + w],
                        lambda e, m_, ns_, w, ps, mi=mi:
                            e.activation(PRDm[:, mi, 0:w], ps, AF.Gelu,
                                         bias=b1c[:, mi:mi + 1]),
                        evac_eng="scalar", sem_role="3_w1")
            pp.close()
        for mo in range(NCH):
            emit_load_weight(c, None, 4096, 128, w2b,
                             dram_ap=dr["w2"][:, mo * 128:(mo + 1) * 128])
            pp = PsumPool(nc)
            with nc.Block() as ph:
                def evac(e, m_, ns_, w, ps, mo=mo, ns=ns):
                    sl = slice(ns * 512, ns * 512 + w)
                    e.tensor_scalar(scrA[:, 0:w], ps, b2c[:, mo:mo + 1], None,
                                    ALU.add)
                    e.tensor_copy(scrC[:, 0:w], XNI[:, mo, sl])
                    e.tensor_tensor(scrB[:, 0:w], scrA[:, 0:w], scrC[:, 0:w],
                                    ALU.add)
                    return xb2.add_evac(e, mo, sl, scrB[:, 0:w])
                mm_loop(c, ph, pp, w2b, 32, [0], lambda m_: (0, 128), 512,
                        lambda k, ns_, w: PRDm[:, k, 0:w],
                        evac, sem_role="3_w2")
            pp.close()
    emit_store_fm(c, xb2.chunks(), y, TC)
    c.close()
    return nc


def run_a1b(x_chunks, w):
    nc = _nc_cache("a1b1", build_a1b_p1)
    maps = []
    for i in range(NCORES):
        m = {"x": x_chunks[i], "wqkv": w["1b_wqkv"], "wug": w["1b_wug"],
             "bw": w["1b_bw"], "bug": w["1b_bug"], "g1b1": w["1b_g1b1"]}
        m.update(_CONSTS)
        maps.append(m)
    r1 = run_bass_kernel_spmd(nc, maps, core_ids=list(range(NCORES)))
    qkv = [r1.results[i]["qkvfm"] for i in range(NCORES)]
    gf = [r1.results[i]["gfm"] for i in range(NCORES)]
    dv = [r1.results[i]["drv"] for i in range(NCORES)]
    drs = [f32c(np.concatenate([dv[CPS * s + q] for q in range(CPS)], axis=1))
           for s in range(2)]
    nc = _nc_cache("a1b2", build_a1b_p2)
    maps = [{"drv": drs[i // CPS], "acol": w["1b_acol"]}
            for i in range(NCORES)]
    r2 = run_bass_kernel_spmd(nc, maps, core_ids=list(range(NCORES)))
    nc = _nc_cache("a1b3", build_a1b_p3)
    maps = []
    for i in range(NCORES):
        kf = qkv[i][1024:2048]
        vf = qkv[i][2048:3072]
        if i % CPS == 0:
            kh = np.zeros((1024, 256), np.float32)
            vh = np.zeros((256, 1024), np.float32)
            msk = w["1b_masks"]
        else:
            kh = qkv[i - 1][1024:2048][:, -256:]
            vh = qkv[i - 1][2048:3072][:, -256:].T
            msk = w["1b_masks_mid"]
        sfm = r2.results[i]["sfm"][:, (i % CPS) * TC:(i % CPS + 1) * TC]
        m = {"x": x_chunks[i], "qfm": f32c(qkv[i][0:1024]),
             "kext": f32c(np.concatenate([kh, kf], axis=1)),
             "vext": f32c(np.concatenate([vh, vf.T], axis=0)),
             "gfm": f32c(gf[i]), "sfm": f32c(sfm),
             "wo": w["1b_wo"], "boc": w["1b_boc"], "cw": w["1b_cw"],
             "g2b2": w["1b_g2b2"], "w1": w["1b_w1"], "b1c": w["1b_b1c"],
             "w2": w["1b_w2"], "b2c": w["1b_b2c"], "masks": msk}
        m.update(_CONSTS)
        maps.append(m)
    r3 = run_bass_kernel_spmd(nc, maps, core_ids=list(range(NCORES)))
    return [r3.results[i]["y"] for i in range(NCORES)]



def build_ln_only():
    nc = bass.Bass("TRN2", target_bir_lowering=False)
    c = Ctx(nc)
    dr = {}
    _din(nc, dr, "x", (TC, D))
    _din(nc, dr, "identf", (128, 128))
    y = nc.dram_tensor("y", [TC, D], F32, kind="ExternalOutput").ap()
    emit_setup(c, dr)
    xb = XBufs(c, TC)
    emit_load_fm(c, xb.chunks(), dr["x"], TC)
    XF = c.sbuf("lo_xf", [128, NCH, TC], BF16)
    emit_ln(c, xb.chunks(), XF, TC)
    emit_store_fm(c, [(XF.rearrange("p a b -> p (a b)"), j * TC)
                      for j in range(NCH)], y, TC, dt=BF16)
    c.close()
    return nc


def build_add():
    nc = bass.Bass("TRN2", target_bir_lowering=False)
    c = Ctx(nc)
    dr = {}
    _din(nc, dr, "a", (TC, D))
    _din(nc, dr, "b", (TC, D))
    y = nc.dram_tensor("y", [TC, D], F32, kind="ExternalOutput").ap()
    TA = c.sbuf("ad_a", [128, 1024], F32)
    TB = c.sbuf("ad_b", [128, 1024], F32)
    TO = c.sbuf("ad_o", [128, 1024], F32)
    ld = c.sem("ad_ld")
    st = c.sem("ad_st")
    for t in range(TC // 128):
        sl = slice(t * 128, (t + 1) * 128)
        with nc.Block() as ph:
            @ph.gpsimd
            def _(e, sl=sl):
                b0 = ld.n
                ld.inc(e.dma_start(out=TA[:, :], in_=dr["a"][sl, :]), 16)
                ld.inc(e.dma_start(out=TB[:, :], in_=dr["b"][sl, :]), 16)
                e.wait_ge(ld.h, b0 + 32)
        with nc.Block() as ph:
            @ph.vector
            def _(e):
                e.tensor_tensor(TO[:, :], TA[:, :], TB[:, :], ALU.add)
        with nc.Block() as ph:
            @ph.gpsimd
            def _(e, sl=sl):
                b0 = st.n
                st.inc(e.dma_start(out=y[sl, :], in_=TO[:, :]), 16)
                e.wait_ge(st.h, b0 + 16)
    c.close()
    return nc


def run_ln(x_chunks):
    nc = _nc_cache("lnon", build_ln_only)
    maps = []
    for i in range(NCORES):
        m = {"x": x_chunks[i]}
        m.update(_CONSTS)
        maps.append(m)
    res = run_bass_kernel_spmd(nc, maps, core_ids=list(range(NCORES)))
    return [res.results[i]["y"] for i in range(NCORES)]


def run_add(a_chunks, b_chunks):
    nc = _nc_cache("add", build_add)
    maps = [{"a": a_chunks[i], "b": b_chunks[i]} for i in range(NCORES)]
    res = run_bass_kernel_spmd(nc, maps, core_ids=list(range(NCORES)))
    return [res.results[i]["y"] for i in range(NCORES)]


# version 04 (final)
# speedup vs baseline: 1.2358x; 1.2358x over previous
import numpy as np
from contextlib import ExitStack
import ml_dtypes
import concourse.bass as bass
import concourse.mybir as mybir
from concourse.bass_utils import run_bass_kernel_spmd

F32 = mybir.dt.float32
BF16 = mybir.dt.bfloat16
AF = mybir.ActivationFunctionType
ALU = mybir.AluOpType

NCORES = 8
B, T, D = 2, 8192, 1024
CPS = 4
TC = T // CPS
NCH = D // 128
NEG = -1.0e9
LN_EPS = 1e-5
BF = ml_dtypes.bfloat16


def f32c(a):
    return np.ascontiguousarray(a, dtype=np.float32)


def col_layout(v):
    return f32c(np.asarray(v).reshape(-1, 128).T)


class SemW:
    def __init__(self, h):
        self.h = h
        self.n = 0

    def inc(self, instr, k=1):
        instr.then_inc(self.h, k)
        self.n += k
        return instr


class Ctx:
    def __init__(self, nc):
        self.nc = nc
        self._sems = {}
        self._stack = []
        self._tiles = {}

    def sem(self, name):
        if name not in self._sems:
            cm = self.nc.semaphore(name)
            self._stack.append(cm)
            self._sems[name] = SemW(cm.__enter__())
        return self._sems[name]

    def sbuf(self, name, shape, dtype):
        if name not in self._tiles:
            cm = self.nc.sbuf_tensor(name, list(shape), dtype)
            self._stack.append(cm)
            self._tiles[name] = cm.__enter__()
        return self._tiles[name]

    def close(self):
        for cm in reversed(self._stack):
            cm.__exit__(None, None, None)
        self._stack = []


class PsumPool:

    SERIAL = [0]

    def __init__(self, nc):
        self.nc = nc
        self._cms = []

    def get(self, shape, dtype=F32):
        PsumPool.SERIAL[0] += 1
        cm = self.nc.psum_tensor(f"pp_{PsumPool.SERIAL[0]}", list(shape), dtype)
        self._cms.append(cm)
        return cm.__enter__()

    def close(self):
        for cm in reversed(self._cms):
            cm.__exit__(None, None, None)
        self._cms = []


class XBufs:

    def __init__(self, c, tcols, tag="X"):
        self.tcols = tcols
        self.bufs = [c.sbuf(f"{tag}{i}", [128, tcols], F32) for i in range(NCH)]
        self.scr = c.sbuf(f"{tag}scr", [128, 512], F32)

    def cur(self, j):
        return self.bufs[j]

    def commit(self, j):
        pass

    def chunks(self, off=0):
        return [(self.bufs[j], off) for j in range(NCH)]

    def add_evac(self, e, j, sl, ps):
        w = sl.stop - sl.start
        e.tensor_tensor(self.scr[:, 0:w], ps, self.bufs[j][:, sl], ALU.add)
        return e.tensor_copy(self.bufs[j][:, sl], self.scr[:, 0:w])



def emit_load_fm(c, dst_chunks, x_dram, rows, col_off=0, nch=NCH, dram_row0=0):
    nc = c.nc
    pp = PsumPool(nc)
    stage = c.sbuf("ld_stage", [128, 1024], F32)
    pst = pp.get([128, 128], F32)
    ident = c.sbuf("identf32", [128, 128], F32)
    ld, tr, ev = c.sem("ld_a"), c.sem("ld_b"), c.sem("ld_c")
    l0, t0, e0 = ld.n, tr.n, ev.n
    ntile = (rows + 127) // 128
    with nc.Block() as ph:
        @ph.sync
        def _(e):
            for tt in range(ntile):
                r = min(128, rows - tt * 128)
                if tt > 0:
                    e.wait_ge(ev.h, e0 + tt * nch)
                ld.inc(e.dma_start(
                    out=stage[0:r, 0:nch * 128],
                    in_=x_dram[dram_row0 + tt * 128: dram_row0 + tt * 128 + r, :]), 16)

        @ph.tensor
        def _(e):
            cnt = 0
            for tt in range(ntile):
                r = min(128, rows - tt * 128)
                e.wait_ge(ld.h, l0 + 16 * (tt + 1))
                for j in range(nch):
                    if cnt > 0:
                        e.wait_ge(ev.h, e0 + cnt)
                    tr.inc(e.transpose(pst[0:128, 0:r],
                                       stage[0:r, j * 128:(j + 1) * 128],
                                       ident[0:r, 0:r]), 1)
                    cnt += 1

        @ph.vector
        def _(e):
            cnt = 0
            for tt in range(ntile):
                r = min(128, rows - tt * 128)
                for j in range(nch):
                    cnt += 1
                    e.wait_ge(tr.h, t0 + cnt)
                    tile, base = dst_chunks[j]
                    ev.inc(e.tensor_copy(
                        tile[:, base + col_off + tt * 128: base + col_off + tt * 128 + r],
                        pst[0:128, 0:r]), 1)
    pp.close()


def emit_store_fm(c, src_chunks, x_dram, rows, col_off=0, final_wait=True,
                  nch=NCH, dt=F32):
    nc = c.nc
    pp = PsumPool(nc)
    stage = c.sbuf("ld_stage" if dt == F32 else "st_stageb",
                   [128, 1024], dt)
    pst = pp.get([128, 128], dt)
    ident = c.sbuf("identf32" if dt == F32 else "identbf", [128, 128], dt)
    tr, ev, st = c.sem("st_a"), c.sem("st_b"), c.sem("st_c")
    t0, e0, s0 = tr.n, ev.n, st.n
    ntile = (rows + 127) // 128
    with nc.Block() as ph:
        @ph.tensor
        def _(e):
            cnt = 0
            for tt in range(ntile):
                r = min(128, rows - tt * 128)
                if tt > 0:
                    e.wait_ge(st.h, s0 + 16 * tt)
                for j in range(nch):
                    if cnt > 0:
                        e.wait_ge(ev.h, e0 + cnt)
                    tile, base = src_chunks[j]
                    tr.inc(e.transpose(
                        pst[0:r, 0:128],
                        tile[:, base + col_off + tt * 128: base + col_off + tt * 128 + r],
                        ident[:, :]), 1)
                    cnt += 1

        @ph.vector
        def _(e):
            cnt = 0
            for tt in range(ntile):
                r = min(128, rows - tt * 128)
                for j in range(nch):
                    cnt += 1
                    e.wait_ge(tr.h, t0 + cnt)
                    ev.inc(e.tensor_copy(stage[0:r, j * 128:(j + 1) * 128],
                                         pst[0:r, 0:128]), 1)

        @ph.gpsimd
        def _(e):
            for tt in range(ntile):
                r = min(128, rows - tt * 128)
                e.wait_ge(ev.h, e0 + (tt + 1) * nch)
                st.inc(e.dma_start(out=x_dram[tt * 128: tt * 128 + r, :],
                                   in_=stage[0:r, 0:nch * 128]), 16)
            if final_wait:
                e.wait_ge(st.h, st.n)
    pp.close()


def emit_load_weight(c, w_dram, K, M, wb, wf_name=None, k0=0, m0=0,
                     dram_ap=None):
    nc = c.nc
    kch = (K + 127) // 128
    ld = c.sem("wl_a")
    src_ = dram_ap if dram_ap is not None else w_dram
    with nc.Block() as ph:
        @ph.gpsimd
        def _(e):
            for k in range(kch):
                r = min(128, K - k * 128)
                ld.inc(e.dma_start(out=wb[0:r, k0 + k, m0:m0 + M],
                                   in_=src_[k * 128:k * 128 + r, :]), 16)
            e.wait_ge(ld.h, ld.n)


def emit_load_small(c, dram_ap, shape, name, dtype=F32):
    nc = c.nc
    tile = c.sbuf(name, list(shape), dtype)
    ld = c.sem("small_ld")
    with nc.Block() as ph:
        @ph.gpsimd
        def _(e):
            ld.inc(e.dma_start(out=tile[tuple(slice(0, s) for s in shape)],
                               in_=dram_ap), 16)
            e.wait_ge(ld.h, ld.n)
    return tile



def emit_ln(c, src_chunks, XN, cols, gamma=None, beta=None, dst_off=0,
            shift_chunks=(), tagname="ln"):
    nc = c.nc
    pp = PsumPool(nc)
    ns_tot = (cols + 511) // 512
    rowbuf = c.sbuf("ln_row", [1, 2, 2, 512], F32)
    lnvar = c.sbuf("ln_var", [1, 2, 512], F32)
    scrB = c.sbuf("ln_scrB", [128, 512], F32)
    rowbf = c.sbuf("ln_rbf", [1, 2, 2, 512], BF16)
    scr2 = c.sbuf("ln_scr2", [1, 512], F32)
    sq = c.sbuf("ln_sq", [128, 2, 512], BF16)
    scr = c.sbuf("ln_scr", [128, 512], F32)
    ps_mu = pp.get([1, 512])
    ps_sq = pp.get([1, 512])
    ps_b1 = pp.get([128, 2, 512])
    ps_b2 = pp.get([128, 2, 512])
    ones_col = c.sbuf("ones_col_bf", [128, 1], BF16)
    ones_row = c.sbuf("ones_row_bf", [1, 128], BF16)
    sCast = c.sem("ln_cast")
    sSq = c.sem("ln_sq_s")
    sSt = c.sem("ln_stat")
    sV = c.sem("ln_rows")
    sA = c.sem("ln_rstd")
    sNu = c.sem("ln_nu")
    sBc = c.sem("ln_bc")
    sAp = c.sem("ln_apply")
    bCast, bSq, bSt = sCast.n, sSq.n, sSt.n
    bV, bA, bNu, bBc, bAp = sV.n, sA.n, sNu.n, sBc.n, sAp.n
    shift_chunks = set(shift_chunks)

    def src(j, lo, hi):
        tile, base = src_chunks[j]
        o = base - (1 if j in shift_chunks else 0)
        return tile[:, o + lo:o + hi]

    with nc.Block() as ph:
        @ph.scalar
        def _(e):
            for j in range(NCH):
                sCast.inc(e.activation(XN[:, j, dst_off:dst_off + cols],
                                       src(j, 0, cols), AF.Copy), 1)
            for ns in range(ns_tot):
                w = min(512, cols - ns * 512)
                e.wait_ge(sV.h, bV + ns + 1)
                e.activation(scr2[0:1, 0:w], lnvar[0:1, ns % 2, 0:w],
                             AF.Ln, bias=LN_EPS)
                sA.inc(e.activation(rowbuf[0:1, ns % 2, 1, 0:w], scr2[0:1, 0:w],
                                    AF.Exp, scale=-0.5), 1)

        @ph.gpsimd
        def _(e):
            e.wait_ge(sCast.h, bCast + NCH)
            for ns in range(ns_tot):
                w = min(512, cols - ns * 512)
                for j in range(NCH):
                    fi = ns * NCH + j
                    if fi >= 2:
                        e.wait_ge(sSt.h, bSt + fi - 1)
                    xsl = XN[:, j, dst_off + ns * 512:dst_off + ns * 512 + w]
                    sSq.inc(e.tensor_tensor(sq[:, fi % 2, 0:w], xsl, xsl,
                                            ALU.mult), 1)

        @ph.tensor
        def _(e):
            e.wait_ge(sCast.h, bCast + NCH)
            for ns in range(ns_tot):
                w = min(512, cols - ns * 512)
                if ns > 0:
                    e.wait_ge(sV.h, bV + ns)
                for j in range(NCH):
                    e.matmul(ps_mu[0:1, 0:w], ones_col[:, :],
                             XN[:, j, dst_off + ns * 512:dst_off + ns * 512 + w],
                             start=(j == 0), stop=(j == NCH - 1))
                for j in range(NCH):
                    fi = ns * NCH + j
                    e.wait_ge(sSq.h, bSq + fi + 1)
                    sSt.inc(e.matmul(ps_sq[0:1, 0:w], ones_col[:, :],
                                     sq[:, fi % 2, 0:w],
                                     start=(j == 0), stop=(j == NCH - 1)), 1)
            for ns in range(ns_tot):
                w = min(512, cols - ns * 512)
                e.wait_ge(sNu.h, bNu + ns + 1)
                if ns > 1:
                    e.wait_ge(sAp.h, bAp + (ns - 1) * NCH)
                e.matmul(ps_b1[:, ns % 2, 0:w], ones_row[:, :],
                         rowbf[0:1, ns % 2, 0, 0:w], start=True, stop=True)
                sBc.inc(e.matmul(ps_b2[:, ns % 2, 0:w], ones_row[:, :],
                                 rowbf[0:1, ns % 2, 1, 0:w],
                                 start=True, stop=True), 1)

        @ph.vector
        def _(e):
            for ns in range(ns_tot + 1):
                if ns < ns_tot:
                    w = min(512, cols - ns * 512)
                    e.wait_ge(sSt.h, bSt + (ns + 1) * NCH)
                    e.tensor_scalar(rowbuf[0:1, ns % 2, 0, 0:w], ps_mu[0:1, 0:w],
                                    1.0 / D, None, ALU.mult)
                    e.tensor_scalar(rowbuf[0:1, ns % 2, 1, 0:w], ps_sq[0:1, 0:w],
                                    1.0 / D, None, ALU.mult)
                    e.tensor_tensor(scr[0:1, 0:w], rowbuf[0:1, ns % 2, 0, 0:w],
                                    rowbuf[0:1, ns % 2, 0, 0:w], ALU.mult)
                    sV.inc(e.tensor_tensor(lnvar[0:1, ns % 2, 0:w],
                                           rowbuf[0:1, ns % 2, 1, 0:w],
                                           scr[0:1, 0:w], ALU.subtract), 1)
                if ns >= 1:
                    m = ns - 1
                    w = min(512, cols - m * 512)
                    e.wait_ge(sA.h, bA + m + 1)
                    if m >= 2:
                        e.wait_ge(sBc.h, bBc + m - 1)
                    e.tensor_tensor(scr[0:1, 0:w],
                                    rowbuf[0:1, m % 2, 0, 0:w],
                                    rowbuf[0:1, m % 2, 1, 0:w], ALU.mult)
                    e.tensor_copy(rowbf[0:1, m % 2, 0, 0:w],
                                  rowbuf[0:1, m % 2, 1, 0:w])
                    sNu.inc(e.tensor_copy(rowbf[0:1, m % 2, 1, 0:w],
                                          scr[0:1, 0:w]), 1)
            for ns in range(ns_tot):
                w = min(512, cols - ns * 512)
                e.wait_ge(sBc.h, bBc + ns + 1)
                for j in range(NCH):
                    dsl = XN[:, j, dst_off + ns * 512:dst_off + ns * 512 + w]
                    e.tensor_tensor(scr[:, 0:w], src(j, ns * 512, ns * 512 + w),
                                    ps_b1[:, ns % 2, 0:w], ALU.mult)
                    if gamma is None:
                        sAp.inc(e.tensor_tensor(dsl, scr[:, 0:w],
                                                ps_b2[:, ns % 2, 0:w],
                                                ALU.subtract), 1)
                    else:
                        e.tensor_tensor(scrB[:, 0:w], scr[:, 0:w],
                                        ps_b2[:, ns % 2, 0:w], ALU.subtract)
                        sAp.inc(e.tensor_scalar(dsl, scrB[:, 0:w], gamma(j), beta(j),
                                                ALU.mult, ALU.add), 1)
    pp.close()


def emit_ln1(c, src_col_fn, XN, dst_col, gamma=None, beta=None):
    nc = c.nc
    XP0 = c.sbuf("ln1_src", [128, NCH, 64], F32)
    XN0 = c.sbuf("ln1_dst", [128, NCH, 64], BF16)
    with nc.Block() as ph:
        @ph.vector
        def _(e):
            e.memset(XP0.rearrange("p a b -> p (a b)"), 1.0)
    with nc.Block() as ph:
        @ph.vector
        def _(e):
            for j in range(NCH):
                e.tensor_copy(XP0[:, j, 0:1], src_col_fn(j))
    emit_ln(c, [(XP0.rearrange("p a b -> p (a b)"), j * 64) for j in range(NCH)],
            XN0, 64, gamma=gamma, beta=beta)
    with nc.Block() as ph:
        @ph.vector
        def _(e):
            for j in range(NCH):
                e.tensor_copy(XN[:, j, dst_col:dst_col + 1], XN0[:, j, 0:1])



def mm_loop(c, ph, pp, wb, kch, m_list, mslice, cols, rhs_fn, evac, nbank=2,
            evac_eng="vector", sem_role="mm"):
    ns_tot = (cols + 511) // 512
    ps = pp.get([128, nbank, 512], F32)
    sm, se = c.sem(sem_role + "_m"), c.sem(sem_role + "_e")
    bm, be = sm.n, se.n

    def pe(e):
        g = 0
        for m in m_list:
            c0, mw = mslice(m)
            for ns in range(ns_tot):
                w = min(512, cols - ns * 512)
                if g >= nbank:
                    e.wait_ge(se.h, be + g - nbank + 1)
                for k in range(kch):
                    lhs = wb(k, m) if callable(wb) else wb[:, k, c0:c0 + mw]
                    mm = e.matmul(ps[0:mw, g % nbank, 0:w], lhs, rhs_fn(k, ns, w),
                                  start=(k == 0), stop=(k == kch - 1))
                sm.inc(mm, 1)
                g += 1

    def ve(e):
        g = 0
        for m in m_list:
            c0, mw = mslice(m)
            for ns in range(ns_tot):
                w = min(512, cols - ns * 512)
                e.wait_ge(sm.h, bm + g + 1)
                se.inc(evac(e, m, ns, w, ps[0:mw, g % nbank, 0:w]), 1)
                g += 1

    ph.tensor(pe)
    getattr(ph, evac_eng)(ve)



def emit_a0a(c, xb, io, stage=99, npairs=NCH, do_qkv=True, do_attn=True, do_wout=True, attn_pairs=None, attn_iters=32, attn_sub=9, attn_serial=False, attn_plain=False, attn_nomask=False, attn_nobias=False, attn_noden=False):
    nc = c.nc
    XN = c.sbuf("XN", [128, NCH, TC], BF16)
    VP = c.sbuf("a0a_v", [128, TC], BF16)
    QR = c.sbuf("a0a_qr", [128, TC], BF16)
    KR = c.sbuf("a0a_kr", [128, TC], BF16)
    VT = c.sbuf("a0a_vt", [128, 16, 128], BF16)
    AO = c.sbuf("a0a_ao", [128, NCH, TC], BF16)
    rtmp = c.sbuf("a0a_rt", [32, 2, 512], F32)
    scrS = c.sbuf("a0a_scrs", [128, 2, 256], F32)
    scrS2 = c.sbuf("a0a_scrs2", [128, 2, 256], F32)
    PN = c.sbuf("a0a_pn", [128, 2, 256], BF16)
    PN2 = c.sbuf("a0a_pn2", [128, 2, 256], BF16)
    PT = c.sbuf("a0a_pt", [128, 2, 2, 128], BF16)
    colbuf = c.sbuf("a0a_cols", [128, 2, 4], F32)
    den = c.sbuf("a0a_den", [128, 2, 1], F32)
    wpair = c.sbuf("a0a_wp", [128, NCH, 384], BF16)
    identb = c.sbuf("identbf", [128, 128], BF16)
    rc = io["t_ropec"]
    rs = io["t_ropes"]
    mask = io["t_mask"]

    emit_ln(c, xb.chunks(), XN, TC)
    if stage <= 1:
        return
    for p in range(npairs):
        emit_load_weight(c, None, 1024, 384, wpair, dram_ap=io["wqkvp"][p])
        if not do_qkv:
            continue
        pp = PsumPool(nc)
        with nc.Block() as ph:
            def evac(e, m, ns, w, ps):
                sl = slice(ns * 512, ns * 512 + w)
                if m == 2:
                    return e.tensor_copy(VP[:, sl], ps)
                out = QR if m == 0 else KR
                rcs = rc[:, sl]
                rss = rs[:, sl]
                for base in (0, 64):
                    evs = slice(base, base + 32)
                    ods = slice(base + 32, base + 64)
                    e.tensor_tensor(rtmp[:, 0, 0:w], ps[evs, :], rcs, ALU.mult)
                    e.tensor_tensor(rtmp[:, 1, 0:w], ps[ods, :], rss, ALU.mult)
                    e.tensor_tensor(out[evs, sl], rtmp[:, 0, 0:w], rtmp[:, 1, 0:w],
                                    ALU.subtract)
                    e.tensor_tensor(rtmp[:, 0, 0:w], ps[ods, :], rcs, ALU.mult)
                    e.tensor_tensor(rtmp[:, 1, 0:w], ps[evs, :], rss, ALU.mult)
                    last = e.tensor_tensor(out[ods, sl], rtmp[:, 0, 0:w],
                                           rtmp[:, 1, 0:w], ALU.add)
                return last
            mm_loop(c, ph, pp, wpair, NCH, [0, 1, 2],
                    lambda m: (m * 128, 128), TC,
                    lambda k, ns, w: XN[:, k, ns * 512:ns * 512 + w],
                    evac, sem_role="a0a_qkv")
        pp.close()
        if not do_attn or (attn_pairs is not None and p not in attn_pairs):
            continue

        pp = PsumPool(nc)
        pstv = pp.get([128, 2, 1024], BF16)
        psS = pp.get([128, 2, 512], F32)
        pstp = pp.get([128, 2, 1024], BF16)
        psO = pp.get([128, 2, 512], F32)
        vt_s, vt_e = c.sem("a0a_vts"), c.sem("a0a_vte")
        sS, sM, sE, sN = c.sem("a0a_sS"), c.sem("a0a_sM"), c.sem("a0a_sE"), c.sem("a0a_sN")
        sPT, sPTe, sO, sOe = c.sem("a0a_sPT"), c.sem("a0a_sPTe"), c.sem("a0a_sO"), c.sem("a0a_sOe")
        bvt, bve = vt_s.n, vt_e.n
        bS, bM, bE, bN = sS.n, sM.n, sE.n, sN.n
        bPT, bPTe, bO, bOe = sPT.n, sPTe.n, sO.n, sOe.n
        NIT = attn_iters

        def it(i):
            h, r = divmod(i, 16)
            b, qc = divmod(r, 2)
            return h, b, qc

        with nc.Block() as ph:
            @ph.tensor
            def _(e):
                for tb in range(16):
                    if tb > 1:
                        e.wait_ge(vt_e.h, bve + tb - 1)
                    vt_s.inc(e.transpose(pstv[:, tb % 2, 0:128],
                                         VP[:, tb * 128:(tb + 1) * 128],
                                         identb[:, :]), 1)
                for i in range(NIT + 1):
                    if i < NIT:
                        h, b, qc = it(i)
                        hs = slice(h * 64, h * 64 + 64)
                        q0 = b * 256 + qc * 128
                        if i >= 2:
                            e.wait_ge(sM.h, bM + i - 1)
                        if i == 0:
                            qe = c.sem("a0a_qkv_e")
                            e.wait_ge(qe.h, qe.n)
                        sS.inc(e.matmul(psS[:, i % 2, 0:256], QR[hs, q0:q0 + 128],
                                        KR[hs, b * 256:b * 256 + 256],
                                        start=True, stop=True), 1)
                    if i >= 1:
                        j = i - 1
                        h, b, qc = it(j)
                        e.wait_ge(sN.h, bN + j + 1)
                        if j >= 1:
                            e.wait_ge(sPTe.h, bPTe + 2 * j)
                        for kc in range(2):
                            sPT.inc(e.transpose(pstp[:, kc, 0:128],
                                                PN2[:, j % 2, kc * 128:kc * 128 + 128],
                                                identb[:, :]), 1)
                        e.wait_ge(sPTe.h, bPTe + 2 * j + 2)
                        if j >= 2:
                            e.wait_ge(sOe.h, bOe + j - 1)
                        for kc in range(2):
                            mm = e.matmul(psO[0:64, j % 2, 0:128],
                                          VT[:, 2 * b + kc, h * 64:h * 64 + 64],
                                          PT[:, j % 2, kc, :],
                                          start=(kc == 0), stop=(kc == 1))
                        sO.inc(mm, 1)

            @ph.vector
            def _(e):
                for tb in range(16):
                    e.wait_ge(vt_s.h, bvt + tb + 1)
                    vt_e.inc(e.tensor_copy(VT[:, tb, :], pstv[:, tb % 2, 0:128]), 1)
                for i in range(NIT + 1):
                    if i < NIT:
                        h, b, qc = it(i)
                        e.wait_ge(sS.h, bS + i + 1)
                        if i >= 2:
                            e.wait_ge(sE.h, bE + i - 1)
                        e.tensor_tensor(scrS[:, 0, :], psS[:, i % 2, 0:256],
                                        mask[:, qc, :], ALU.add)
                        e.reduce_max(colbuf[:, i % 2, 0:1], scrS[:, 0, :],
                                     axis=mybir.AxisListType.X)
                    if i >= 1:
                        e.wait_ge(sE.h, bE + i)
                        e.reciprocal(colbuf[:, (i - 1) % 2, 2:3],
                                     den[:, (i - 1) % 2, :])
                    else:
                        e.tensor_copy(scrS[:, 1, 0:64], scrS[:, 0, 0:64])
                    if i < NIT:
                        sM.inc(e.tensor_scalar(scrS2[:, i % 2, :], scrS[:, 0, :],
                                               colbuf[:, i % 2, 0:1], None,
                                               ALU.subtract), 1)
                    else:
                        e.tensor_copy(scrS[:, 1, 0:64], scrS[:, 0, 0:64])
                    if i >= 1:
                        j = i - 1
                        h, b, qc = it(j)
                        sN.inc(e.tensor_scalar(PN2[:, j % 2, :], PN[:, j % 2, :],
                                               colbuf[:, j % 2, 2:3], None,
                                               ALU.mult), 1)
                        if j >= 2:
                            e.wait_ge(sO.h, bO + j - 1)
                        for kc in range(2):
                            e.wait_ge(sPT.h, bPT + 2 * j + kc + 1)
                            sPTe.inc(e.tensor_copy(PT[:, j % 2, kc, :],
                                                   pstp[:, kc, 0:128]), 1)
                        e.wait_ge(sO.h, bO + j + 1)
                        sOe.inc(e.tensor_copy(
                            AO[h * 64:h * 64 + 64, p,
                               b * 256 + qc * 128:b * 256 + qc * 128 + 128],
                            psO[0:64, j % 2, 0:128]), 1)

            @ph.scalar
            def _(e):
                for i in range(NIT):
                    e.wait_ge(sM.h, bM + i + 1)
                    if i >= 2:
                        e.wait_ge(sPT.h, bPT + 2 * (i - 1))
                        e.wait_ge(sN.h, bN + i - 1)
                    sE.inc(e.activation(PN[:, i % 2, :], scrS2[:, i % 2, :],
                                        AF.Exp, scale=0.125,
                                        accum_out=den[:, i % 2, :]), 1)
        pp.close()

    if not do_wout:
        return
    for m in range(NCH):
        emit_load_weight(c, None, 1024, 128, wpair,
                         dram_ap=io["woutT"][:, m * 128:(m + 1) * 128])
        pp = PsumPool(nc)
        with nc.Block() as ph:
            def evac(e, m_, ns, w, ps, m=m):
                return xb.add_evac(e, m, slice(ns * 512, ns * 512 + w), ps)
            mm_loop(c, ph, pp, wpair, NCH, [0], lambda m_: (0, 128), TC,
                    lambda k, ns, w: AO[:, k, ns * 512:ns * 512 + w],
                    evac, sem_role="a0a_wo")
        pp.close()



def emit_f0(c, xb, io, xprev_dram):
    nc = c.nc
    XNE = c.sbuf("f0_A", [128, NCH, TC + 1], BF16)
    XR = c.sbuf("f0_B", [128, NCH, TC], BF16)
    XK = c.sbuf("f0_C", [128, NCH, TC], BF16)
    RS = XNE
    KA = XR
    sscr = c.sbuf("f0_scr", [128, 512], F32)
    prevX = c.sbuf("f0_px", [128, NCH], F32)
    mucols = emit_load_small(c, io["mu_rk"], [128, 2 * NCH], "f0_mu")
    om = c.sbuf("f0_om", [128, 2 * NCH], F32)
    wbig = c.sbuf("f0_w", [128, NCH, 128], BF16)

    with nc.Block() as ph:
        @ph.vector
        def _(e):
            e.tensor_scalar(om[:, :], mucols[:, :], -1.0, 1.0, ALU.mult, ALU.add)

    emit_load_fm(c, [(prevX, j) for j in range(NCH)], xprev_dram, 1)
    emit_ln1(c, lambda j: prevX[:, j:j + 1], XNE, 0)
    emit_ln(c, xb.chunks(), XNE, TC, dst_off=1)

    with nc.Block() as ph:
        @ph.vector
        def _(e):
            for j in range(NCH):
                for half, (dst, mc, oc) in enumerate(
                        ((XR, mucols[:, j:j + 1], om[:, j:j + 1]),
                         (XK, mucols[:, NCH + j:NCH + j + 1], om[:, NCH + j:NCH + j + 1]))):
                    for ns in range((TC + 511) // 512):
                        w = min(512, TC - ns * 512)
                        sl = slice(ns * 512, ns * 512 + w)
                        e.tensor_scalar(sscr[:, 0:w], XNE[:, j, 1 + ns * 512:1 + ns * 512 + w],
                                        mc, None, ALU.mult)
                        e.scalar_tensor_tensor(dst[:, j, sl], XNE[:, j, ns * 512:ns * 512 + w],
                                               oc, sscr[:, 0:w], ALU.mult, ALU.add)

    for m in range(NCH):
        emit_load_weight(c, None, 1024, 128, wbig,
                         dram_ap=io["wrT"][:, m * 128:(m + 1) * 128])
        pp = PsumPool(nc)
        with nc.Block() as ph:
            mm_loop(c, ph, pp, wbig, NCH, [m], lambda m_: (0, 128),
                    TC, lambda k, ns, w: XR[:, k, ns * 512:ns * 512 + w],
                    lambda e, m_, ns, w, ps, m=m: e.activation(
                        RS[:, m, ns * 512:ns * 512 + w], ps, AF.Sigmoid),
                    evac_eng="scalar", sem_role="f0_r")
        pp.close()
    for m in range(NCH):
        emit_load_weight(c, None, 1024, 128, wbig,
                         dram_ap=io["wkT"][:, m * 128:(m + 1) * 128])
        pp = PsumPool(nc)
        with nc.Block() as ph:
            def evac(e, m_, ns, w, ps, m=m):
                e.activation(sscr[:, 0:w], ps, AF.Relu)
                return e.activation(KA[:, m, ns * 512:ns * 512 + w], sscr[:, 0:w],
                                    AF.Square)
            mm_loop(c, ph, pp, wbig, NCH, [m], lambda m_: (0, 128),
                    TC, lambda k, ns, w: XK[:, k, ns * 512:ns * 512 + w],
                    evac, evac_eng="scalar", sem_role="f0_k")
        pp.close()
    for m in range(NCH):
        emit_load_weight(c, None, 1024, 128, wbig,
                         dram_ap=io["wvT"][:, m * 128:(m + 1) * 128])
        pp = PsumPool(nc)
        with nc.Block() as ph:
            def evac(e, m_, ns, w, ps, m=m):
                sl = slice(ns * 512, ns * 512 + w)
                e.tensor_tensor(sscr[:, 0:w], ps, RS[:, m, sl], ALU.mult)
                e.tensor_tensor(xb.scr[:, 0:w], sscr[:, 0:w], xb.cur(m)[:, sl],
                                ALU.add)
                return e.tensor_copy(xb.cur(m)[:, sl], xb.scr[:, 0:w])
            mm_loop(c, ph, pp, wbig, NCH, [m], lambda m_: (0, 128),
                    TC, lambda k, ns, w: KA[:, k, ns * 512:ns * 512 + w],
                    evac, sem_role="f0_v")
        pp.close()



def _host_consts():
    return {"identf": np.eye(128, dtype=np.float32)}


def _rope_tables_a0a():
    inv = (1.0 / 10000.0 ** (np.arange(0, 64, 2, dtype=np.float32) / 64.0)
           ).astype(np.float32)
    t = np.arange(T, dtype=np.float32)
    ang = t[:, None] * inv[None, :]
    return f32c(np.cos(ang).T), f32c(np.sin(ang).T)


def _mask_a0a():
    m = np.zeros((128, 2, 256), np.float32)
    for qc in range(2):
        q = qc * 128 + np.arange(128)[:, None]
        k = np.arange(256)[None, :]
        m[:, qc, :] = np.where(k <= q, 0.0, NEG)
    return f32c(m.reshape(128, 512))


def _perm_evenodd(n=64):
    return np.concatenate([np.arange(0, n, 2), np.arange(1, n, 2)])


def prep_weights(inp):
    w = {}
    Wqkv = np.asarray(inp["L0_attn_a_Wqkv"])
    perm = _perm_evenodd()
    rows = []
    for sect in range(3):
        s = Wqkv[sect * 1024:(sect + 1) * 1024]
        if sect < 2:
            s = s.reshape(16, 64, 1024)[:, perm, :].reshape(1024, 1024)
        rows.append(s)
    WqkvP = np.concatenate(rows, axis=0)
    WqkvT = WqkvP.T
    pairs = [np.concatenate([WqkvT[:, 128 * p:128 * p + 128],
                             WqkvT[:, 1024 + 128 * p:1024 + 128 * p + 128],
                             WqkvT[:, 2048 + 128 * p:2048 + 128 * p + 128]], axis=1)
             for p in range(8)]
    w["a0a_wqkvp"] = f32c(np.stack(pairs))
    w["a0a_woutT"] = f32c(np.asarray(inp["L0_attn_a_Wout"]).T)
    rc, rs = _rope_tables_a0a()
    w["a0a_ropec"], w["a0a_ropes"] = rc, rs
    w["a0a_mask"] = _mask_a0a()
    w["f0_wrT"] = f32c(np.asarray(inp["L0_ffn_W_r"]).T)
    w["f0_wkT"] = f32c(np.asarray(inp["L0_ffn_W_k"]).T)
    w["f0_wvT"] = f32c(np.asarray(inp["L0_ffn_W_v"]).T)
    w["f0_mu"] = f32c(np.concatenate([col_layout(inp["L0_ffn_mu_r"]),
                                      col_layout(inp["L0_ffn_mu_k"])], axis=1))
    w["b_wqT"] = f32c(np.asarray(inp["L0_attn_b_Wq"]).T)
    w["b_wkT"] = f32c(np.asarray(inp["L0_attn_b_Wk"]).T)
    w["b_wvT"] = f32c(np.asarray(inp["L0_attn_b_Wv"]).T)
    w["b_woT"] = f32c(np.asarray(inp["L0_attn_b_Wo"]).T)
    w["b_bq"] = col_layout(inp["L0_attn_b_bq"])
    w["b_bkv"] = f32c(np.concatenate([col_layout(inp["L0_attn_b_bk"]),
                                      col_layout(inp["L0_attn_b_bv"])], axis=1))
    w["b_bogb"] = f32c(np.concatenate([col_layout(inp["L0_attn_b_bo"]),
                                       col_layout(inp["L0_attn_b_ln_gamma"]),
                                       col_layout(inp["L0_attn_b_ln_beta"])],
                                      axis=1))
    return w


def _din(nc, dr, name, shape, dtype=F32):
    t = nc.dram_tensor(name, list(shape), dtype, kind="ExternalInput").ap()
    dr[name] = t
    return t


FLOAT_CONSTS = [LN_EPS, 1.0, 1e-24]


def emit_setup(c, dr):
    nc = c.nc
    idf = emit_load_small(c, dr["identf"], [128, 128], "identf32")
    idb = c.sbuf("identbf", [128, 128], BF16)
    oc = c.sbuf("ones_col_bf", [128, 1], BF16)
    orow = c.sbuf("ones_row_bf", [1, 128], BF16)
    cpool = c.sbuf("constpool", [128, len(FLOAT_CONSTS)], F32)
    with nc.Block() as ph:
        @ph.vector
        def _(e):
            e.tensor_copy(idb[:, :], idf[:, :])
            e.memset(oc[:, :], 1.0)
            e.memset(orow[:, :], 1.0)
            for i, v in enumerate(FLOAT_CONSTS):
                e.memset(cpool[:, i:i + 1], v)
    for i, v in enumerate(FLOAT_CONSTS):
        nc.const_aps.register(F32, v, cpool[:, i:i + 1])


def emit_debug_dump(c, tiles):
    nc = c.nc
    st = c.sem("dbg_st")
    with nc.Block() as ph:
        @ph.gpsimd
        def _(e):
            for dram_ap, sb_ap in tiles:
                st.inc(e.dma_start(out=dram_ap, in_=sb_ap), 16)
            e.wait_ge(st.h, st.n)


def build_a0a(stage=99, debug=False, **kw):
    nc = bass.Bass("TRN2", target_bir_lowering=False)
    c = Ctx(nc)
    dr = {}
    _din(nc, dr, "x", (TC, D))
    _din(nc, dr, "wqkvp", (8, 1024, 384))
    _din(nc, dr, "woutT", (1024, 1024))
    _din(nc, dr, "ropec", (32, TC))
    _din(nc, dr, "ropes", (32, TC))
    _din(nc, dr, "mask", (128, 512))
    _din(nc, dr, "identf", (128, 128))
    y = nc.dram_tensor("y", [TC, D], F32, kind="ExternalOutput").ap()
    emit_setup(c, dr)
    tc_ = emit_load_small(c, dr["ropec"], [32, TC], "t_ropec", BF16)
    ts_ = emit_load_small(c, dr["ropes"], [32, TC], "t_ropes", BF16)
    tm = emit_load_small(c, dr["mask"], [128, 2, 256], "t_mask")
    xb = XBufs(c, TC)
    emit_load_fm(c, xb.chunks(), dr["x"], TC)
    io = {"wqkvp": dr["wqkvp"], "woutT": dr["woutT"],
          "t_ropec": tc_[:, :], "t_ropes": ts_[:, :], "t_mask": tm}
    emit_a0a(c, xb, io, stage=stage, **kw)
    if debug:
        dbg = {}
        for nm, shape in (("d_xn", (128, NCH * TC)), ("d_qr", (128, TC)),
                          ("d_kr", (128, TC)), ("d_vp", (128, TC)),
                          ("d_ao", (128, NCH * TC))):
            dbg[nm] = nc.dram_tensor(nm, list(shape), F32, kind="ExternalOutput").ap()
        t = c._tiles
        emit_debug_dump(c, [
            (dbg["d_xn"], t["XN"].rearrange("p a b -> p (a b)")),
            (dbg["d_qr"], t["a0a_qr"][:, :]),
            (dbg["d_kr"], t["a0a_kr"][:, :]),
            (dbg["d_vp"], t["a0a_v"][:, :]),
            (dbg["d_ao"], t["a0a_ao"].rearrange("p a b -> p (a b)")),
        ])
    emit_store_fm(c, xb.chunks(), y, TC)
    c.close()
    return nc


def build_f0():
    nc = bass.Bass("TRN2", target_bir_lowering=False)
    c = Ctx(nc)
    dr = {}
    _din(nc, dr, "x", (TC, D))
    _din(nc, dr, "xprev", (1, D))
    _din(nc, dr, "wrT", (1024, 1024))
    _din(nc, dr, "wkT", (1024, 1024))
    _din(nc, dr, "wvT", (1024, 1024))
    _din(nc, dr, "mu_rk", (128, 2 * NCH))
    _din(nc, dr, "identf", (128, 128))
    y = nc.dram_tensor("y", [TC, D], F32, kind="ExternalOutput").ap()
    emit_setup(c, dr)
    xb = XBufs(c, TC)
    emit_load_fm(c, xb.chunks(), dr["x"], TC)
    io = {"wrT": dr["wrT"], "wkT": dr["wkT"], "wvT": dr["wvT"],
          "mu_rk": dr["mu_rk"]}
    emit_f0(c, xb, io, dr["xprev"])
    emit_store_fm(c, xb.chunks(), y, TC)
    c.close()
    return nc


def shard_x(x_full):
    out = []
    for cidx in range(NCORES):
        s, q = divmod(cidx, CPS)
        out.append(f32c(x_full[s, q * TC:(q + 1) * TC]))
    return out


def unshard_x(chunks):
    x = np.empty((B, T, D), np.float32)
    for cidx in range(NCORES):
        s, q = divmod(cidx, CPS)
        x[s, q * TC:(q + 1) * TC] = chunks[cidx]
    return x


def prev_rows(x_chunks, nrows=1):
    outs = []
    for cidx in range(NCORES):
        s, q = divmod(cidx, CPS)
        if q == 0:
            outs.append(np.zeros((nrows, D), np.float32))
        else:
            outs.append(f32c(x_chunks[cidx - 1][-nrows:]))
    return outs


_CONSTS = _host_consts()


def run_a0a(x_chunks, w):
    nc = _nc_cache('a0a', build_a0a)
    maps = []
    for cidx in range(NCORES):
        q = cidx % CPS
        m = {"x": x_chunks[cidx], "wqkvp": w["a0a_wqkvp"],
             "woutT": w["a0a_woutT"],
             "ropec": f32c(w["a0a_ropec"][:, q * TC:(q + 1) * TC]),
             "ropes": f32c(w["a0a_ropes"][:, q * TC:(q + 1) * TC]),
             "mask": w["a0a_mask"]}
        m.update(_CONSTS)
        maps.append(m)
    res = run_bass_kernel_spmd(nc, maps, core_ids=list(range(NCORES)))
    return [res.results[i]["y"] for i in range(NCORES)]


def run_f0(x_chunks, w):
    nc = _nc_cache('f0', build_f0)
    prevs = prev_rows(x_chunks, 1)
    maps = []
    for cidx in range(NCORES):
        m = {"x": x_chunks[cidx], "xprev": prevs[cidx], "wrT": w["f0_wrT"],
             "wkT": w["f0_wkT"], "wvT": w["f0_wvT"], "mu_rk": w["f0_mu"]}
        m.update(_CONSTS)
        maps.append(m)
    res = run_bass_kernel_spmd(nc, maps, core_ids=list(range(NCORES)))
    return [res.results[i]["y"] for i in range(NCORES)]


def kernel(**inputs):
    inputs = {k: np.asarray(v) for k, v in inputs.items()}
    w = prep_weights(inputs)
    w.update(prep_f1(inputs))
    w.update(prep_a1a(inputs))
    w.update(prep_a1b(inputs))
    xc = shard_x(inputs["L0_attn_a_x"])
    xc = run_a0a(xc, w)
    xc = run_f0(xc, w)
    xc = run_a0b(xc, w)
    xc = run_f0(xc, w)
    xc = run_a1a(xc, w)

    def f1_block(xc):
        zc = run_ln(xc)
        for layer in range(6):
            zc = run_f1_attn(zc, w, layer)
            zc = run_f1_ffn(zc, w, layer)
        return run_add(xc, zc)
    xc = f1_block(xc)
    xc = run_a1b(xc, w)
    xc = f1_block(xc)
    return unshard_x(xc)


F1H = 257
F1E = TC + F1H
F1K = TC + 256


def emit_f1_attn(c, xb, io):
    nc = c.nc
    XBE = c.sbuf("f1_xbe", [128, NCH, F1E], BF16)
    QP = c.sbuf("f1_qp", [128, F1K], BF16)
    KP = c.sbuf("f1_kp", [128, F1K], BF16)
    QR = QP
    KR = KP
    VP = c.sbuf("f1_vp", [128, F1K], BF16)
    VT = c.sbuf("f1_vt", [128, 18, 2, 65], BF16)
    AO = c.sbuf("f1_ao", [128, NCH, TC], BF16)
    wpair = c.sbuf("f1_wp", [128, NCH, 384], BF16)
    sqb = c.sbuf("f1_sqb", [64, 2, 512], BF16)
    scrr = c.sbuf("f1_scrr", [64, 2, 512], BF16)
    scrq2 = c.sbuf("f1_scrq2", [64, 2, 512], BF16)
    rows = c.sbuf("f1_rows", [1, 3, 512], F32)
    rowbf = c.sbuf("f1_rbf", [1, 2, 512], BF16)
    drow = c.sbuf("f1_drow", [1, 2, 256], F32)
    dbf = c.sbuf("f1_dbf", [1, 2, 256], BF16)
    scrS = c.sbuf("f1_scrs", [128, 2, 256], F32)
    PN = c.sbuf("f1_pn", [128, 2, 256], BF16)
    obuf = c.sbuf("f1_obuf", [64, 2, 256], BF16)
    negM = c.sbuf("f1_negm", [128, 1], F32)
    mrow = c.sbuf("f1_mrow", [1, 8], F32)
    mrowW = c.sbuf("f1_mroww", [1, 64], F32)
    mrow2 = c.sbuf("f1_mrow2", [64, 2], F32)
    identf = c.sbuf("identf32", [128, 128], F32)
    mbf = c.sbuf("f1_mbf", [1, 1], BF16)
    tq = io["t_xq"]
    tk = io["t_xk"]
    qscol = io["t_qs"]
    mask = io["t_fmask"]
    identb = c.sbuf("identbf", [128, 128], BF16)
    ones_row = c.sbuf("ones_row_bf", [1, 128], BF16)
    ones_col = c.sbuf("ones_col_bf", [128, 1], BF16)

    pp = PsumPool(nc)
    psm = pp.get([1, 512])
    psb = pp.get([128, 512])
    with nc.Block() as ph:
        @ph.vector
        def _(e):
            e.tensor_tensor(mrow2[:, 0:1], qscol[:, 0:1], qscol[:, 1:2], ALU.mult)
    with nc.Block() as ph:
        @ph.scalar
        def _(e):
            e.activation(mrow2[:, 1:2], mrow2[:, 0:1], AF.Abs)
    with nc.Block() as ph:
        @ph.tensor
        def _(e):
            e.transpose(psm[0:1, 0:64], mrow2[0:64, 1:2], identf[0:64, 0:64])
    with nc.Block() as ph:
        @ph.vector
        def _(e):
            e.tensor_copy(mrowW[0:1, 0:64], psm[0:1, 0:64])
    with nc.Block() as ph:
        @ph.vector
        def _(e):
            e.reduce_max(mrow[0:1, 0:1], mrowW[0:1, 0:64],
                         axis=mybir.AxisListType.X)
    with nc.Block() as ph:
        @ph.vector
        def _(e):
            e.tensor_scalar(mrow[0:1, 1:2], mrow[0:1, 0:1], -1.0, None, ALU.mult)
    with nc.Block() as ph:
        @ph.vector
        def _(e):
            e.tensor_copy(mbf[0:1, 0:1], mrow[0:1, 1:2])
    with nc.Block() as ph:
        @ph.tensor
        def _(e):
            e.matmul(psb[:, 0:1], ones_row[0:1, :], mbf[0:1, 0:1],
                     start=True, stop=True)
    with nc.Block() as ph:
        @ph.vector
        def _(e):
            e.tensor_copy(negM[:, :], psb[:, 0:1])
    pp.close()

    emit_load_fm(c, [(XBE.rearrange("p a b -> p (a b)"), j * F1E)
                     for j in range(NCH)], io["xprev"], F1H)
    with nc.Block() as ph:
        @ph.scalar
        def _(e):
            for j in range(NCH):
                e.activation(XBE[:, j, F1H:F1E], xb.cur(j)[:, :], AF.Copy)

    NW = TC // 256
    for p in range(NCH):
        emit_load_weight(c, None, 1024, 384, wpair, dram_ap=io["wqkvp"][p])
        pp = PsumPool(nc)
        with nc.Block() as ph:
            specs = [(0, QP, F1H, TC, 256), (1, KP, 1, F1K, 0), (2, VP, 1, F1K, 0)]
            for sect, dst, off, cols, dof in specs:
                def rhs(k, ns, w, off=off):
                    sh = -1 if k >= 4 else 0
                    return XBE[:, k, off + sh + ns * 512: off + sh + ns * 512 + w]
                mm_loop(c, ph, pp, wpair, NCH, [sect],
                        lambda m, sect=sect: (sect * 128, 128), cols, rhs,
                        lambda e, m, ns, w, ps, dst=dst, dof=dof:
                            e.tensor_copy(dst[:, dof + ns * 512:dof + ns * 512 + w], ps),
                        sem_role="f1_qkv")
        pp.close()

        pp = PsumPool(nc)
        psn = pp.get([1, 2, 512])
        psb2 = pp.get([64, 2, 512])
        sSQ, sNM, sR0, sRS = (c.sem(f"f1l{k}") for k in "abcd")
        sBF, sBC, sAP = (c.sem(f"f1l{k}") for k in "efg")
        bSQ, bNM, bR0, bRS = sSQ.n, sNM.n, sR0.n, sRS.n
        bBF, bBC, bAP = sBF.n, sBC.n, sAP.n
        jobs = []
        for h in range(2):
            for srcT, dstT, tab, qi, c0_, cend in (
                    (QP, QR, tq, 0, 256, F1K), (KP, KR, tk, 1, 0, F1K)):
                cc = c0_
                while cc < cend:
                    w = min(512, cend - cc)
                    jobs.append((srcT, dstT, tab, qi, h, cc, w))
                    cc += w
        NJ = len(jobs)
        with nc.Block() as ph:
            @ph.scalar
            def _(e):
                for ji in range(NJ + 2):
                    if ji < NJ:
                        srcT, dstT, tab, qi, h, c0_, w = jobs[ji]
                        if ji >= 2:
                            e.wait_ge(sNM.h, bNM + ji - 1)
                        sSQ.inc(e.activation(
                            sqb[:, ji % 2, 0:w],
                            srcT[h * 64:h * 64 + 64, c0_:c0_ + w],
                            AF.Square), 1)
                    if ji >= 2:
                        jj = ji - 2
                        srcT, dstT, tab, qi, h, c0_, w = jobs[jj]
                        e.wait_ge(sR0.h, bR0 + jj + 1)
                        if jj >= 2:
                            e.wait_ge(sBF.h, bBF + jj - 1)
                        e.activation(rows[0:1, 2, 0:w], rows[0:1, jj % 2, 0:w],
                                     AF.Ln, bias=1e-24)
                        sRS.inc(e.activation(rows[0:1, jj % 2, 0:w],
                                             rows[0:1, 2, 0:w], AF.Exp,
                                             scale=-0.5), 1)

            @ph.tensor
            def _(e):
                for ji in range(NJ + 4):
                    if ji < NJ:
                        srcT, dstT, tab, qi, h, c0_, w = jobs[ji]
                        e.wait_ge(sSQ.h, bSQ + ji + 1)
                        if ji >= 2:
                            e.wait_ge(sR0.h, bR0 + ji - 1)
                        sNM.inc(e.matmul(psn[0:1, ji % 2, 0:w], ones_col[0:64, :],
                                         sqb[:, ji % 2, 0:w],
                                         start=True, stop=True), 1)
                    if ji >= 4:
                        jj = ji - 4
                        srcT, dstT, tab, qi, h, c0_, w = jobs[jj]
                        e.wait_ge(sBF.h, bBF + jj + 1)
                        if jj >= 2:
                            e.wait_ge(sAP.h, bAP + jj - 1)
                        sBC.inc(e.matmul(psb2[:, jj % 2, 0:w], ones_row[0:1, 0:64],
                                         rowbf[0:1, jj % 2, 0:w],
                                         start=True, stop=True), 1)

            @ph.vector
            def _(e):
                for ji in range(NJ + 3):
                    if ji >= 2 and ji - 2 < NJ:
                        jj = ji - 2
                        srcT, dstT, tab, qi, h, c0_, w = jobs[jj]
                        e.wait_ge(sRS.h, bRS + jj + 1)
                        if jj >= 2:
                            e.wait_ge(sBC.h, bBC + jj - 1)
                        sBF.inc(e.tensor_copy(rowbf[0:1, jj % 2, 0:w],
                                              rows[0:1, jj % 2, 0:w]), 1)
                    if ji < NJ:
                        srcT, dstT, tab, qi, h, c0_, w = jobs[ji]
                        e.wait_ge(sNM.h, bNM + ji + 1)
                        if ji >= 2:
                            e.wait_ge(sRS.h, bRS + ji - 1)
                        sR0.inc(e.tensor_copy(rows[0:1, ji % 2, 0:w],
                                              psn[0:1, ji % 2, 0:w]), 1)
                    if ji >= 3:
                        jj = ji - 3
                        srcT, dstT, tab, qi, h, c0_, w = jobs[jj]
                        hs = slice(h * 64, h * 64 + 64)
                        sl = slice(c0_, c0_ + w)
                        e.wait_ge(sBC.h, bBC + jj + 1)
                        e.tensor_tensor(scrq2[:, 0, 0:w], srcT[hs, sl],
                                        psb2[:, jj % 2, 0:w], ALU.mult)
                        e.tensor_scalar(scrr[:, 1, 0:w], scrq2[:, 0, 0:w],
                                        qscol[:, qi:qi + 1], None, ALU.mult)
                        qn = scrr[:, 1, 0:w]
                        e.tensor_copy(scrr[0:32, 0, 0:w], qn[32:64, :])
                        e.tensor_copy(scrr[32:64, 0, 0:w], qn[0:32, :])
                        e.tensor_tensor(scrq2[:, 0, 0:w], qn, tab[:, 0, sl],
                                        ALU.mult)
                        e.tensor_tensor(scrq2[:, 1, 0:w], scrr[:, 0, 0:w],
                                        tab[:, 1, sl], ALU.mult)
                        sAP.inc(e.tensor_tensor(dstT[hs, sl], scrq2[:, 0, 0:w],
                                                scrq2[:, 1, 0:w], ALU.add), 1)
        pp.close()

        pp = PsumPool(nc)
        pstv = pp.get([128, 2, 1024], BF16)
        vs, ve = c.sem("f1_vts"), c.sem("f1_vte")
        bvs, bve = vs.n, ve.n
        with nc.Block() as ph:
            @ph.vector
            def _(e):
                if p == 0:
                    e.memset(VT[:, :, :, 64:65], 1.0)
                for tb in range(18):
                    e.wait_ge(vs.h, bvs + tb + 1)
                    e.tensor_copy(VT[:, tb, 0, 0:64], pstv[:, tb % 2, 0:64])
                    ve.inc(e.tensor_copy(VT[:, tb, 1, 0:64],
                                         pstv[:, tb % 2, 64:128]), 1)

            @ph.tensor
            def _(e):
                for tb in range(18):
                    if tb > 1:
                        e.wait_ge(ve.h, bve + tb - 1)
                    vs.inc(e.transpose(pstv[:, tb % 2, 0:128],
                                       VP[:, tb * 128:(tb + 1) * 128],
                                       identb[:, :]), 1)
        pp.close()

        pp = PsumPool(nc)
        psS = pp.get([128, 2, 512], F32)
        psO = pp.get([128, 2, 512], F32)
        psD = pp.get([128, 2, 512], F32)
        fS, fM, fE, fO = (c.sem(f"f1a{k}") for k in "abcd")
        fD, fB, fV = (c.sem(f"f1a{k}") for k in "efg")
        bfS, bfM, bfE, bfO = fS.n, fM.n, fE.n, fO.n
        bfD, bfB, bfV = fD.n, fB.n, fV.n
        NHW = 2 * NW

        def hw(it):
            return divmod(it, NW)

        with nc.Block() as ph:
            @ph.tensor
            def _(e):
                def emit_O(gp):
                    itp, kcp = divmod(gp, 4)
                    hp, wp = hw(itp)
                    if kcp == 0 and itp >= 2:
                        e.wait_ge(fV.h, bfV + itp - 1)
                    e.wait_ge(fE.h, bfE + gp + 1)
                    fO.inc(e.matmul(psO[0:65, itp % 2, 0:256],
                                    VT[:, 2 * wp + kcp, hp, :],
                                    PN[:, gp % 2, :],
                                    start=(kcp == 0), stop=(kcp == 3)), 1)
                    if kcp == 3:
                        e.wait_ge(fD.h, bfD + itp + 1)
                        if itp >= 2:
                            e.wait_ge(fV.h, bfV + itp - 1)
                        fB.inc(e.matmul(psD[0:64, itp % 2, 0:256],
                                        ones_row[0:1, 0:64],
                                        dbf[0:1, itp % 2, :],
                                        start=True, stop=True), 1)
                for it in range(NHW):
                    h, w_ = hw(it)
                    hs = slice(h * 64, h * 64 + 64)
                    q0 = 256 + 256 * w_
                    k0 = 256 * w_
                    for kc in range(4):
                        g = 4 * it + kc
                        if g >= 2:
                            e.wait_ge(fM.h, bfM + g - 1)
                        fS.inc(e.matmul(psS[:, g % 2, 0:256],
                                        KR[hs, k0 + kc * 128:k0 + kc * 128 + 128],
                                        QR[hs, q0:q0 + 256],
                                        start=True, stop=True), 1)
                        if g >= 1:
                            emit_O(g - 1)
                emit_O(4 * NHW - 1)

            @ph.vector
            def _(e):
                for it in range(NHW + 1):
                    if it < NHW:
                        h, w_ = hw(it)
                        v = 0 if w_ == 0 else 1
                        for kc in range(4):
                            g = 4 * it + kc
                            e.wait_ge(fS.h, bfS + g + 1)
                            if g >= 2:
                                e.wait_ge(fE.h, bfE + g - 1)
                            fM.inc(e.scalar_tensor_tensor(
                                scrS[:, g % 2, :], psS[:, g % 2, 0:256],
                                negM[:, 0:1], mask[:, v, kc, :],
                                ALU.add, ALU.add), 1)
                    if it >= 1:
                        jt = it - 1
                        h, w_ = hw(jt)
                        e.wait_ge(fB.h, bfB + jt + 1)
                        e.tensor_copy(obuf[:, jt % 2, :], psO[0:64, jt % 2, 0:256])
                        fV.inc(e.tensor_tensor(
                            AO[h * 64:h * 64 + 64, p, w_ * 256:w_ * 256 + 256],
                            obuf[:, jt % 2, :], psD[0:64, jt % 2, 0:256],
                            ALU.mult), 1)
                    if it < NHW:
                        e.wait_ge(fO.h, bfO + 4 * (it + 1))
                        if it >= 2:
                            e.wait_ge(fB.h, bfB + it - 1)
                        e.reciprocal(drow[0:1, it % 2, :],
                                     psO[64:65, it % 2, 0:256])
                        fD.inc(e.tensor_copy(dbf[0:1, it % 2, :],
                                             drow[0:1, it % 2, :]), 1)

            @ph.scalar
            def _(e):
                for it in range(NHW):
                    for kc in range(4):
                        g = 4 * it + kc
                        e.wait_ge(fM.h, bfM + g + 1)
                        if g >= 2:
                            e.wait_ge(fO.h, bfO + g - 1)
                        fE.inc(e.activation(PN[:, g % 2, :], scrS[:, g % 2, :],
                                            AF.Exp, scale=8.0), 1)
        pp.close()

    for m in range(NCH):
        emit_load_weight(c, None, 1024, 128, wpair,
                         dram_ap=io["woT"][:, m * 128:(m + 1) * 128])
        pp = PsumPool(nc)
        with nc.Block() as ph:
            def evac(e, m_, ns, w, ps, m=m):
                return xb.add_evac(e, m, slice(ns * 512, ns * 512 + w), ps)
            mm_loop(c, ph, pp, wpair, NCH, [0], lambda m_: (0, 128), TC,
                    lambda k, ns, w: AO[:, k, ns * 512:ns * 512 + w],
                    evac, sem_role="f1_wo")
        pp.close()


def emit_f1_ffn(c, xb, io):
    nc = c.nc
    XNE = c.sbuf("f1f_xne", [128, NCH, TC], BF16)
    XProw = c.sbuf("f1f_xp", [128, NCH], F32)
    XP0 = c.sbuf("f1f_x0", [128, NCH], F32)
    PRD = c.sbuf("f1f_prd", [128, 22, 512], BF16)
    wcol = c.sbuf("f1f_w", [128, 2, NCH, 256], BF16)
    w2c = c.sbuf("f1f_w2", [128, 2, 22, 128], BF16)
    gcol = c.sbuf("f1f_g", [128, 2 * NCH], F32)
    b1col = c.sbuf("f1f_b1", [128, 44], F32)
    b2col = c.sbuf("f1f_b2", [128, NCH], F32)
    gel = c.sbuf("f1f_gel", [128, 2, 512], F32)
    ax = c.sbuf("f1f_ax", [128, 2, 512], F32)
    emit_load_small(c, io["lngb"], [128, 2 * NCH], "f1f_g")
    emit_load_small(c, io["b1col"], [128, 44], "f1f_b1")
    emit_load_small(c, io["b2col"], [128, NCH], "f1f_b2")
    emit_load_fm(c, [(XProw, j) for j in range(NCH)], io["xprev1"], 1)
    with nc.Block() as ph:
        @ph.vector
        def _(e):
            for j in range(NCH):
                if j < 4:
                    e.tensor_copy(XP0[:, j:j + 1], xb.cur(j)[:, 0:1])
                else:
                    e.tensor_copy(XP0[:, j:j + 1], XProw[:, j:j + 1])
    gam = lambda j: gcol[:, j:j + 1]
    bet = lambda j: gcol[:, NCH + j:NCH + j + 1]
    emit_ln1(c, lambda j: XP0[:, j:j + 1], XNE, 0, gamma=gam, beta=bet)
    emit_ln(c, [(xb.cur(j), 1) for j in range(NCH)], XNE, TC - 1, dst_off=1,
            gamma=gam, beta=bet, shift_chunks=(4, 5, 6, 7))

    for ns in range(TC // 512):
        pp = PsumPool(nc)
        psW = pp.get([128, 4, 512])
        wl = c.sem("f1f_wl")
        sPM = c.sem("f1f_pm")
        sGE = c.sem("f1f_ge")
        sPR = c.sem("f1f_pr")
        bwl, bPM, bGE, bPR = wl.n, sPM.n, sGE.n, sPR.n
        with nc.Block() as ph:
            @ph.gpsimd
            def _(e):
                for pr in range(22):
                    if pr >= 2:
                        e.wait_ge(sPM.h, bPM + 2 * (pr - 1))
                    for k in range(NCH):
                        wl.inc(e.dma_start(
                            out=wcol[0:128, pr % 2, k, 0:256],
                            in_=io["w1p"][k * 128:(k + 1) * 128,
                                          pr * 256:(pr + 1) * 256]), 16)

            @ph.tensor
            def _(e):
                for pr in range(22):
                    e.wait_ge(wl.h, bwl + 16 * NCH * (pr + 1))
                    for half in range(2):
                        g = 2 * pr + half
                        if g >= 4:
                            e.wait_ge(sPR.h, bPR + pr - 1)
                        for k in range(NCH):
                            mm = e.matmul(psW[:, g % 4, 0:512],
                                          wcol[:, pr % 2, k, half * 128:half * 128 + 128],
                                          XNE[:, k, ns * 512:ns * 512 + 512],
                                          start=(k == 0), stop=(k == NCH - 1))
                    sPM.inc(mm, 2)

            @ph.scalar
            def _(e):
                for pr in range(22):
                    e.wait_ge(sPM.h, bPM + 2 * (pr + 1))
                    if pr >= 2:
                        e.wait_ge(sPR.h, bPR + pr - 1)
                    sGE.inc(e.activation(gel[:, pr % 2, :],
                                         psW[:, (2 * pr + 1) % 4, 0:512],
                                         AF.Gelu, bias=b1col[:, 2 * pr + 1:2 * pr + 2]), 1)

            @ph.vector
            def _(e):
                for pr in range(22):
                    e.wait_ge(sGE.h, bGE + pr + 1)
                    e.tensor_scalar(ax[:, pr % 2, :], psW[:, (2 * pr) % 4, 0:512],
                                    b1col[:, 2 * pr:2 * pr + 1], None, ALU.add)
                    sPR.inc(e.tensor_tensor(PRD[:, pr, :], ax[:, pr % 2, :],
                                            gel[:, pr % 2, :], ALU.mult), 1)
        pp.close()

        emit_load_weight(c, None, 2816, 128, w2c[:, 0],
                         dram_ap=io["w2p"][:, 0:128])
        for m in range(NCH):
            pp = PsumPool(nc)
            with nc.Block() as ph:
                if m + 1 < NCH:
                    @ph.gpsimd
                    def _(e, m=m):
                        w2l = c.sem("f1f_w2l")
                        b0 = w2l.n
                        for k in range(22):
                            w2l.inc(e.dma_start(
                                out=w2c[:, (m + 1) % 2, k, 0:128],
                                in_=io["w2p"][k * 128:(k + 1) * 128,
                                              (m + 1) * 128:(m + 2) * 128]), 16)
                        e.wait_ge(w2l.h, b0 + 16 * 22)

                def evac(e, m_, ns_, w, ps, m=m, ns=ns):
                    sl = slice(ns * 512, ns * 512 + 512)
                    e.tensor_scalar(ax[:, 0, :], ps, b2col[:, m:m + 1], None,
                                    ALU.add)
                    e.tensor_tensor(xb.scr[:, 0:512], ax[:, 0, :],
                                    xb.cur(m)[:, sl], ALU.add)
                    return e.tensor_copy(xb.cur(m)[:, sl], xb.scr[:, 0:512])
                mm_loop(c, ph, pp,
                        lambda k, m_, m=m: w2c[:, m % 2, k, 0:128],
                        22, [0], lambda m_: (0, 128), 512,
                        lambda k, ns_, w: PRD[:, k, :],
                        evac, sem_role="f1f_w2")
            pp.close()


def build_f1_attn():
    nc = bass.Bass("TRN2", target_bir_lowering=False)
    c = Ctx(nc)
    dr = {}
    _din(nc, dr, "x", (TC, D))
    _din(nc, dr, "xprev", (F1H, D))
    _din(nc, dr, "wqkvp", (8, 1024, 384))
    _din(nc, dr, "woT", (1024, 1024))
    _din(nc, dr, "txq", (64, 2 * F1K))
    _din(nc, dr, "txk", (64, 2 * F1K))
    _din(nc, dr, "qs2", (64, 2))
    _din(nc, dr, "fmask", (128, 2 * 4 * 256))
    _din(nc, dr, "identf", (128, 128))
    y = nc.dram_tensor("y", [TC, D], F32, kind="ExternalOutput").ap()
    emit_setup(c, dr)
    io = {
        "xprev": dr["xprev"], "wqkvp": dr["wqkvp"], "woT": dr["woT"],
        "t_xq": emit_load_small(c, dr["txq"], [64, 2, F1K], "t_xq", BF16),
        "t_xk": emit_load_small(c, dr["txk"], [64, 2, F1K], "t_xk", BF16),
        "t_qs": emit_load_small(c, dr["qs2"], [64, 2], "t_qs"),
        "t_fmask": emit_load_small(c, dr["fmask"], [128, 2, 4, 256], "t_fmask", BF16),
    }
    xb = XBufs(c, TC)
    emit_load_fm(c, xb.chunks(), dr["x"], TC)
    emit_f1_attn(c, xb, io)
    emit_store_fm(c, xb.chunks(), y, TC)
    c.close()
    return nc


def build_f1_ffn():
    nc = bass.Bass("TRN2", target_bir_lowering=False)
    c = Ctx(nc)
    dr = {}
    _din(nc, dr, "x", (TC, D))
    _din(nc, dr, "xprev1", (1, D))
    _din(nc, dr, "w1p", (1024, 5632))
    _din(nc, dr, "w2p", (2816, 1024))
    _din(nc, dr, "lngb", (128, 2 * NCH))
    _din(nc, dr, "b1col", (128, 44))
    _din(nc, dr, "b2col", (128, NCH))
    _din(nc, dr, "identf", (128, 128))
    y = nc.dram_tensor("y", [TC, D], F32, kind="ExternalOutput").ap()
    emit_setup(c, dr)
    io = {"xprev1": dr["xprev1"], "w1p": dr["w1p"], "w2p": dr["w2p"],
          "lngb": dr["lngb"], "b1col": dr["b1col"], "b2col": dr["b2col"]}
    xb = XBufs(c, TC)
    emit_load_fm(c, xb.chunks(), dr["x"], TC)
    emit_f1_ffn(c, xb, io)
    emit_store_fm(c, xb.chunks(), y, TC)
    c.close()
    return nc


def prep_f1(inp):
    w = {}
    Wqkv = np.asarray(inp["L1_ffn_Wqkv"])
    w["f1_wqkvp"] = [f32c(np.stack(
        [np.concatenate([Wqkv[i][:, 128 * p:128 * p + 128],
                         Wqkv[i][:, 1024 + 128 * p:1024 + 128 * p + 128],
                         Wqkv[i][:, 2048 + 128 * p:2048 + 128 * p + 128]], axis=1)
         for p in range(8)])) for i in range(6)]
    w["f1_woT"] = [f32c(np.asarray(inp["L1_ffn_Wo"])[i]) for i in range(6)]
    w["f1_qs2"] = [f32c(np.stack([np.asarray(inp["L1_ffn_q_scale"])[i],
                                  np.asarray(inp["L1_ffn_k_scale"])[i]], axis=1))
                   for i in range(6)]
    W1 = np.asarray(inp["L1_ffn_W1"])
    W2 = np.asarray(inp["L1_ffn_W2"])
    b1 = np.asarray(inp["L1_ffn_b1"])
    w1p, w2p, b1c = [], [], []
    for i in range(6):
        a = np.concatenate([W1[i][:, :2730],
                            np.zeros((1024, 86), np.float32)], axis=1)
        g = np.concatenate([W1[i][:, 2730:],
                            np.zeros((1024, 86), np.float32)], axis=1)
        cols = []
        for pr in range(22):
            cols.append(a[:, 128 * pr:128 * pr + 128])
            cols.append(g[:, 128 * pr:128 * pr + 128])
        w1p.append(f32c(np.concatenate(cols, axis=1)))
        w2p.append(f32c(np.concatenate(
            [W2[i], np.zeros((86, 1024), np.float32)], axis=0)))
        b1a = np.concatenate([b1[i][:2730], np.zeros(86, np.float32)])
        b1g = np.concatenate([b1[i][2730:], np.zeros(86, np.float32)])
        bc = np.zeros((128, 44), np.float32)
        for pr in range(22):
            bc[:, 2 * pr] = b1a[128 * pr:128 * pr + 128]
            bc[:, 2 * pr + 1] = b1g[128 * pr:128 * pr + 128]
        b1c.append(f32c(bc))
    w["f1_w1p"], w["f1_w2p"], w["f1_b1c"] = w1p, w2p, b1c
    w["f1_lngb"] = [f32c(np.concatenate([col_layout(np.asarray(inp["L1_ffn_ln_g"])[i]),
                                         col_layout(np.asarray(inp["L1_ffn_ln_b"])[i])],
                                        axis=1)) for i in range(6)]
    w["f1_b2c"] = [col_layout(np.asarray(inp["L1_ffn_b2"])[i]) for i in range(6)]
    DIM_HEAD, WINDOW = 64, 256
    half = np.arange(0, DIM_HEAD, 2, dtype=np.float32)
    inv_freq = (1.0 / 10000.0 ** (half / DIM_HEAD)).astype(np.float32)
    t = np.arange(T, dtype=np.float32)
    freqs = t[:, None] * inv_freq[None, :]
    base_scale = ((half + 0.4 * DIM_HEAD) / (1.4 * DIM_HEAD)).astype(np.float32)
    power = ((t - T // 2) / (WINDOW // 2)).astype(np.float32)
    scale = (base_scale[None, :] ** power[:, None]).astype(np.float32)
    cosf, sinf = np.cos(freqs), np.sin(freqs)
    w["f1_tabs"] = (cosf, sinf, scale)
    m = np.zeros((128, 2, 4, 256), np.float32)
    q = np.arange(256)[None, :]
    for kc in range(4):
        j = (kc * 128 + np.arange(128))[:, None]
        valid = (q <= j) & (j <= q + 256)
        m[:, 1, kc, :] = np.where(valid, 0.0, NEG)
        m[:, 0, kc, :] = np.where(valid & (j >= 256), 0.0, NEG)
    w["f1_mask_first"] = f32c(m.reshape(128, 2048))
    mn = m.copy()
    mn[:, 0] = m[:, 1]
    w["f1_mask_mid"] = f32c(mn.reshape(128, 2048))
    return w


def f1_core_tables(w, cidx):
    cosf, sinf, scale = w["f1_tabs"]
    q0 = (cidx % CPS) * TC
    pos = np.arange(q0 - 256, q0 + TC)
    pos = np.clip(pos, 0, T - 1) if False else pos
    posc = np.clip(pos, -(T), T - 1)
    cf, sf, sc = cosf[posc], sinf[posc], scale[posc]

    def dup(a):
        return np.concatenate([a.T, a.T], axis=0)

    def dsgn(a):
        return np.concatenate([-a.T, a.T], axis=0)
    txq = f32c(np.concatenate([dup(cf * sc), dsgn(sf * sc)], axis=1))
    txk = f32c(np.concatenate([dup(cf / sc), dsgn(sf / sc)], axis=1))
    return txq, txk


def run_f1_attn(x_chunks, w, layer):
    nc = _nc_cache('f1_attn', build_f1_attn)
    prevs = prev_rows(x_chunks, F1H)
    maps = []
    for cidx in range(NCORES):
        txq, txk = f1_core_tables(w, cidx)
        m = {"x": x_chunks[cidx], "xprev": prevs[cidx],
             "wqkvp": w["f1_wqkvp"][layer], "woT": w["f1_woT"][layer],
             "txq": txq, "txk": txk, "qs2": w["f1_qs2"][layer],
             "fmask": w["f1_mask_first"] if cidx % CPS == 0 else w["f1_mask_mid"]}
        m.update(_CONSTS)
        maps.append(m)
    res = run_bass_kernel_spmd(nc, maps, core_ids=list(range(NCORES)))
    return [res.results[i]["y"] for i in range(NCORES)]


def run_f1_ffn(x_chunks, w, layer):
    nc = _nc_cache('f1_ffn', build_f1_ffn)
    prevs = prev_rows(x_chunks, 1)
    maps = []
    for cidx in range(NCORES):
        m = {"x": x_chunks[cidx], "xprev1": prevs[cidx],
             "w1p": w["f1_w1p"][layer], "w2p": w["f1_w2p"][layer],
             "lngb": w["f1_lngb"][layer], "b1col": w["f1_b1c"][layer],
             "b2col": w["f1_b2c"][layer]}
        m.update(_CONSTS)
        maps.append(m)
    res = run_bass_kernel_spmd(nc, maps, core_ids=list(range(NCORES)))
    return [res.results[i]["y"] for i in range(NCORES)]



def build_a1a_pre():
    nc = bass.Bass("TRN2", target_bir_lowering=False)
    c = Ctx(nc)
    dr = {}
    _din(nc, dr, "x", (TC, D))
    _din(nc, dr, "xh3", (3, D))
    _din(nc, dr, "winT", (1024, 4096))
    _din(nc, dr, "convw", (128, 4 * 16))
    _din(nc, dr, "convb", (128, 16))
    _din(nc, dr, "wxT", (2048, 33))
    _din(nc, dr, "wdtcol", (128, 16))
    _din(nc, dr, "bdtcol", (128, 16))
    _din(nc, dr, "identf", (128, 128))
    dout = nc.dram_tensor("delta", [2048, TC], F32, kind="ExternalOutput").ap()
    uout = nc.dram_tensor("u", [2048, TC], F32, kind="ExternalOutput").ap()
    szout = nc.dram_tensor("sz", [2048, TC], F32, kind="ExternalOutput").ap()
    bcout = nc.dram_tensor("bc", [32, TC], F32, kind="ExternalOutput").ap()
    emit_setup(c, dr)
    cw = emit_load_small(c, dr["convw"], [128, 4, 16], "a1_cw")
    cb = emit_load_small(c, dr["convb"], [128, 16], "a1_cb")
    wdt = emit_load_small(c, dr["wdtcol"], [128, 16], "a1_wdt")
    bdt = emit_load_small(c, dr["bdtcol"], [128, 16], "a1_bdt")
    wx = c.sbuf("a1_wx", [128, 16, 33], BF16)
    emit_load_weight(c, None, 2048, 33, wx, dram_ap=dr["wxT"])
    XB8 = c.sbuf("a1_x", [128, NCH, TC // 2], BF16)
    XN = c.sbuf("XN", [128, NCH, TC], BF16)
    for half in range(2):
        emit_load_fm(c, [(XB8.rearrange("p a b -> p (a b)"), j * (TC // 2))
                         for j in range(NCH)], dr["x"], TC // 2,
                     dram_row0=half * (TC // 2))
        emit_ln(c, [(XB8.rearrange("p a b -> p (a b)"), j * (TC // 2))
                    for j in range(NCH)], XN, TC // 2, dst_off=half * (TC // 2))
    XH = c.sbuf("a1_xh", [128, NCH], F32)
    XNH = c.sbuf("a1_xnh", [128, NCH, 3], BF16)
    for hrow in range(3):
        emit_load_fm(c, [(XH, j) for j in range(NCH)], dr["xh3"], 1, dram_row0=hrow)
        emit_ln1(c, lambda j: XH[:, j:j + 1], XNH, hrow)
    XD = c.sbuf("a1_xd", [128, 16, TC + 3], BF16)
    wpair = c.sbuf("a1_wp", [128, NCH, 128], BF16)
    for m in range(16):
        emit_load_weight(c, None, 1024, 128, wpair,
                         dram_ap=dr["winT"][:, m * 128:(m + 1) * 128])
        pp = PsumPool(nc)
        with nc.Block() as ph:
            mm_loop(c, ph, pp, wpair, NCH, [0], lambda m_: (0, 128), 3,
                    lambda k, ns, w: XNH[:, k, 0:3],
                    lambda e, m_, ns, w, ps, m=m: e.tensor_copy(XD[:, m, 0:3], ps),
                    sem_role="a1_xzh")
            mm_loop(c, ph, pp, wpair, NCH, [0], lambda m_: (0, 128), TC,
                    lambda k, ns, w: XN[:, k, ns * 512:ns * 512 + w],
                    lambda e, m_, ns, w, ps, m=m: e.tensor_copy(
                        XD[:, m, 3 + ns * 512:3 + ns * 512 + w], ps),
                    sem_role="a1_xz")
        pp.close()
    ZF = c.sbuf("a1_zf", [128, 512], F32)
    for m in range(16):
        emit_load_weight(c, None, 1024, 128, wpair,
                         dram_ap=dr["winT"][:, 2048 + m * 128:2048 + (m + 1) * 128])
        for ns in range(4):
            pp = PsumPool(nc)
            with nc.Block() as ph:
                mm_loop(c, ph, pp, wpair, NCH, [0], lambda m_: (0, 128), 512,
                        lambda k, ns_, w, ns=ns: XN[:, k, ns * 512:ns * 512 + 512],
                        lambda e, m_, ns_, w, ps: e.activation(ZF[:, :], ps, AF.Silu),
                        evac_eng="scalar", sem_role="a1_z")
                zgoal = c.sem("a1_z_e").n
                st = c.sem("a1_zst")

                def gp(e, m=m, ns=ns):
                    e.wait_ge(c.sem("a1_z_e").h, zgoal)
                    st.inc(e.dma_start(out=szout[m * 128:(m + 1) * 128,
                                                 ns * 512:(ns + 1) * 512],
                                       in_=ZF[:, :]), 16)
                    e.wait_ge(st.h, st.n)
                ph.gpsimd(gp)
            pp.close()
    PRE = c.sbuf("a1_pre", [128, 2, 512], F32)
    USL = c.sbuf("a1_usl", [128, 16, 512], BF16)
    CS = c.sbuf("a1_cs", [128, 2, 512], F32)
    DROW = c.sbuf("a1_drow", [1, 512], F32)
    DRB = c.sbuf("a1_drb", [1, 512], BF16)
    SPX = c.sbuf("a1_spx", [128, 2, 512], F32)
    LNO = c.sbuf("a1_lno", [128, 2, 512], F32)
    DPP = c.sbuf("a1_dpp", [128, 2, 512], F32)
    DSL = c.sbuf("a1_dsl", [128, 2, 512], F32)
    BCS = c.sbuf("a1_bcs", [16, 2, 512], F32)
    sCV = c.sem("a1_cv")
    sSI = c.sem("a1_si")
    sUD = c.sem("a1_ud")
    ones_row = c.sbuf("ones_row_bf", [1, 128], BF16)
    for ns in range(4):
        bCV, bSI = sCV.n, sSI.n
        with nc.Block() as ph:
            @ph.vector
            def _(e, ns=ns, bSI=bSI):
                for j in range(16):
                    base = 3 + ns * 512
                    if j >= 2:
                        e.wait_ge(sSI.h, bSI + j - 1)
                    e.tensor_scalar(CS[:, 0, :], XD[:, j, base - 3:base - 3 + 512],
                                    cw[:, 0, j:j + 1], None, ALU.mult)
                    e.scalar_tensor_tensor(CS[:, 1, :], XD[:, j, base - 2:base - 2 + 512],
                                           cw[:, 1, j:j + 1], CS[:, 0, :],
                                           ALU.mult, ALU.add)
                    e.scalar_tensor_tensor(CS[:, 0, :], XD[:, j, base - 1:base - 1 + 512],
                                           cw[:, 2, j:j + 1], CS[:, 1, :],
                                           ALU.mult, ALU.add)
                    e.scalar_tensor_tensor(CS[:, 1, :], XD[:, j, base:base + 512],
                                           cw[:, 3, j:j + 1], CS[:, 0, :],
                                           ALU.mult, ALU.add)
                    sCV.inc(e.tensor_scalar(PRE[:, j % 2, :], CS[:, 1, :],
                                            cb[:, j:j + 1], None, ALU.add), 1)

            @ph.scalar
            def _(e, bCV=bCV):
                for j in range(16):
                    e.wait_ge(sCV.h, bCV + j + 1)
                    sSI.inc(e.activation(USL[:, j, :], PRE[:, j % 2, :], AF.Silu), 1)
        with nc.Block() as ph:
            @ph.gpsimd
            def _(e, ns=ns):
                st = c.sem("a1_ust")
                b0 = st.n
                for j in range(16):
                    st.inc(e.dma_start(out=uout[j * 128:(j + 1) * 128,
                                                ns * 512:(ns + 1) * 512],
                                       in_=USL[:, j, :]), 16)
                e.wait_ge(st.h, b0 + 256)
        pp = PsumPool(nc)
        psdt = pp.get([1, 512])
        psB = pp.get([16, 512])
        psC = pp.get([16, 512])
        sWX = c.sem("a1_wxm")
        sWV = c.sem("a1_wxv")
        bWX, bWV = sWX.n, sWV.n
        with nc.Block() as ph:
            @ph.tensor
            def _(e):
                for k in range(16):
                    e.matmul(psdt[0:1, :], wx[:, k, 0:1], USL[:, k, :],
                             start=(k == 0), stop=(k == 15))
                for k in range(16):
                    e.matmul(psB[:, :], wx[:, k, 1:17], USL[:, k, :],
                             start=(k == 0), stop=(k == 15))
                for k in range(16):
                    mm = e.matmul(psC[:, :], wx[:, k, 17:33], USL[:, k, :],
                                  start=(k == 0), stop=(k == 15))
                sWX.inc(mm, 1)

            @ph.vector
            def _(e, bWX=bWX):
                e.wait_ge(sWX.h, bWX + 1)
                e.tensor_copy(BCS[:, 0, :], psB[:, :])
                e.tensor_copy(BCS[:, 1, :], psC[:, :])
                e.tensor_copy(DROW[0:1, :], psdt[0:1, :])
                sWV.inc(e.tensor_copy(DRB[0:1, :], DROW[0:1, :]), 1)

            @ph.gpsimd
            def _(e, ns=ns, bWV=bWV):
                e.wait_ge(sWV.h, bWV + 1)
                st = c.sem("a1_bcst")
                b0 = st.n
                st.inc(e.dma_start(out=bcout[0:16, ns * 512:(ns + 1) * 512],
                                   in_=BCS[:, 0, :]), 16)
                st.inc(e.dma_start(out=bcout[16:32, ns * 512:(ns + 1) * 512],
                                   in_=BCS[:, 1, :]), 16)
                e.wait_ge(st.h, b0 + 32)
        pp.close()
        pp = PsumPool(nc)
        psbc = pp.get([128, 512])
        sBC2 = c.sem("a1_bc2")
        sDP = c.sem("a1_dp")
        sLN = c.sem("a1_ln")
        sDD = c.sem("a1_dd")
        sDS = c.sem("a1_dst")
        bBC2, bDP, bLN, bDD, bDS = sBC2.n, sDP.n, sLN.n, sDD.n, sDS.n
        with nc.Block() as ph:
            @ph.tensor
            def _(e):
                sBC2.inc(e.matmul(psbc[:, :], ones_row[0:1, :], DRB[0:1, :],
                                  start=True, stop=True), 1)

            @ph.vector
            def _(e, ns=ns):
                for j in range(16):
                    e.wait_ge(sBC2.h, bBC2 + 1)
                    if j >= 2:
                        e.wait_ge(sLN.h, bLN + j - 1)
                    sDP.inc(e.tensor_scalar(DPP[:, j % 2, :], psbc[:, :],
                                            wdt[:, j:j + 1], bdt[:, j:j + 1],
                                            ALU.mult, ALU.add), 1)
                    e.wait_ge(sLN.h, bLN + j + 1)
                    if j >= 2:
                        e.wait_ge(sDS.h, bDS + 16 * (j - 1))
                    e.tensor_scalar(CS[:, 0, :], DPP[:, j % 2, :], 0.0, None,
                                    ALU.max)
                    sDD.inc(e.tensor_tensor(DSL[:, j % 2, :], CS[:, 0, :],
                                            LNO[:, j % 2, :], ALU.add), 1)

            @ph.scalar
            def _(e):
                for j in range(16):
                    e.wait_ge(sDP.h, bDP + j + 1)
                    if j >= 2:
                        e.wait_ge(sDD.h, bDD + j - 1)
                    e.activation(SPX[:, 0, :], DPP[:, j % 2, :], AF.Abs)
                    e.activation(SPX[:, 1, :], SPX[:, 0, :], AF.Exp, scale=-1.0)
                    sLN.inc(e.activation(LNO[:, j % 2, :], SPX[:, 1, :],
                                         AF.Ln, bias=1.0), 1)

            @ph.gpsimd
            def _(e, ns=ns):
                for j in range(16):
                    e.wait_ge(sDD.h, bDD + j + 1)
                    sDS.inc(e.dma_start(out=dout[j * 128:(j + 1) * 128,
                                                 ns * 512:(ns + 1) * 512],
                                        in_=DSL[:, j % 2, :]), 16)
                e.wait_ge(sDS.h, sDS.n)
        pp.close()
    c.close()
    return nc


def build_a1a_scan():
    nc = bass.Bass("TRN2", target_bir_lowering=False)
    c = Ctx(nc)
    dr = {}
    _din(nc, dr, "dsl", (512, T))
    _din(nc, dr, "usl", (512, T))
    _din(nc, dr, "bf", (16, T))
    _din(nc, dr, "cf", (16, T))
    _din(nc, dr, "alog", (128, 4 * 16))
    _din(nc, dr, "identf", (128, 128))
    yout = nc.dram_tensor("y", [512, T], F32, kind="ExternalOutput").ap()
    emit_setup(c, dr)
    AL = emit_load_small(c, dr["alog"], [128, 4, 16], "as_alog")
    AC = c.sbuf("as_a", [128, 4, 16], F32)
    with nc.Block() as ph:
        @ph.scalar
        def _(e):
            e.activation(AC.rearrange("p a b -> p (a b)"),
                         AL.rearrange("p a b -> p (a b)"), AF.Exp)
    with nc.Block() as ph:
        @ph.vector
        def _(e):
            e.tensor_scalar(AL.rearrange("p a b -> p (a b)"),
                            AC.rearrange("p a b -> p (a b)"), -1.0, None, ALU.mult)
    ACN = AL
    DSLB = c.sbuf("as_d", [128, 4, 2048], F32)
    USLB = c.sbuf("as_u", [128, 4, 2048], BF16)
    DU = c.sbuf("as_du", [128, 4, 2048], BF16)
    ROWB = c.sbuf("as_rb", [1, 2, 2048], BF16)
    ABUF = c.sbuf("as_ab", [128, 2048], BF16)
    WBUF = c.sbuf("as_wb", [128, 2048], BF16)
    HBUF = c.sbuf("as_hb", [128, 2048], BF16)
    YPP = c.sbuf("as_y", [128, 2, 2048], F32)
    TBUF = c.sbuf("as_t", [128, 2048], BF16)
    STC = c.sbuf("as_st", [128, 4, 16], F32)
    ones_row = c.sbuf("ones_row_bf", [1, 128], BF16)
    with nc.Block() as ph:
        @ph.vector
        def _(e):
            e.memset(STC.rearrange("p a b -> p (a b)"), 0.0)
    ld = c.sem("as_ld")
    for ns in range(4):
        tsl = slice(ns * 2048, (ns + 1) * 2048)
        b0 = ld.n
        with nc.Block() as ph:
            @ph.gpsimd
            def _(e, tsl=tsl, b0=b0):
                for jd in range(4):
                    ld.inc(e.dma_start(out=DSLB[:, jd, :],
                                       in_=dr["dsl"][jd * 128:(jd + 1) * 128, tsl]), 16)
                    ld.inc(e.dma_start(out=USLB[:, jd, :],
                                       in_=dr["usl"][jd * 128:(jd + 1) * 128, tsl]), 16)
                e.wait_ge(ld.h, b0 + 128)
        with nc.Block() as ph:
            @ph.vector
            def _(e):
                for jd in range(4):
                    e.tensor_tensor(DU[:, jd, :], DSLB[:, jd, :], USLB[:, jd, :],
                                    ALU.mult)
        for jd in range(4):
            ycur = 0
            for s in range(16):
                b0 = ld.n
                with nc.Block() as ph:
                    @ph.gpsimd
                    def _(e, s=s, tsl=tsl, b0=b0):
                        ld.inc(e.dma_start(out=ROWB[0:1, 0, :],
                                           in_=dr["bf"][s:s + 1, tsl]), 16)
                        ld.inc(e.dma_start(out=ROWB[0:1, 1, :],
                                           in_=dr["cf"][s:s + 1, tsl]), 16)
                        e.wait_ge(ld.h, b0 + 32)
                pp = PsumPool(nc)
                psb = pp.get([128, 4, 512])
                psc = pp.get([128, 4, 512])
                with nc.Block() as ph:
                    @ph.tensor
                    def _(e):
                        for q4 in range(4):
                            e.matmul(psb[:, q4, :], ones_row[0:1, :],
                                     ROWB[0:1, 0, q4 * 512:(q4 + 1) * 512],
                                     start=True, stop=True)
                            e.matmul(psc[:, q4, :], ones_row[0:1, :],
                                     ROWB[0:1, 1, q4 * 512:(q4 + 1) * 512],
                                     start=True, stop=True)

                    @ph.scalar
                    def _(e, jd=jd, s=s):
                        e.activation(ABUF[:, :], DSLB[:, jd, :], AF.Exp,
                                     scale=ACN[:, jd, s:s + 1])
                with nc.Block() as ph:
                    @ph.vector
                    def _(e, jd=jd, s=s, ycur=ycur):
                        e.tensor_tensor(WBUF[:, :], DU[:, jd, :],
                                        psb.rearrange("p a b -> p (a b)"), ALU.mult)
                        e.tensor_tensor_scan(HBUF[:, :], ABUF[:, :], WBUF[:, :],
                                             STC[:, jd, s:s + 1], ALU.mult, ALU.add)
                        e.tensor_copy(STC[:, jd, s:s + 1], HBUF[:, 2047:2048])
                        e.tensor_tensor(TBUF[:, :], HBUF[:, :],
                                        psc.rearrange("p a b -> p (a b)"), ALU.mult)
                        if s == 0:
                            e.tensor_copy(YPP[:, 0, :], TBUF[:, :])
                        else:
                            e.tensor_tensor(YPP[:, (s % 2)], YPP[:, 1 - (s % 2)],
                                            TBUF[:, :], ALU.add)
                pp.close()
            with nc.Block() as ph:
                @ph.gpsimd
                def _(e, jd=jd, tsl=tsl):
                    st = c.sem("as_st2")
                    b0 = st.n
                    st.inc(e.dma_start(out=yout[jd * 128:(jd + 1) * 128, tsl],
                                       in_=YPP[:, 15 % 2, :]), 16)
                    e.wait_ge(st.h, b0 + 16)
    c.close()
    return nc


def build_a1a_post():
    nc = bass.Bass("TRN2", target_bir_lowering=False)
    c = Ctx(nc)
    dr = {}
    _din(nc, dr, "x", (TC, D))
    _din(nc, dr, "yfm", (2048, TC))
    _din(nc, dr, "ufm", (2048, TC))
    _din(nc, dr, "szfm", (2048, TC))
    _din(nc, dr, "dcol", (128, 16))
    _din(nc, dr, "woutT", (2048, 1024))
    _din(nc, dr, "identf", (128, 128))
    y = nc.dram_tensor("y", [TC, D], F32, kind="ExternalOutput").ap()
    emit_setup(c, dr)
    dcol = emit_load_small(c, dr["dcol"], [128, 16], "ap_dcol")
    xb = XBufs(c, TC)
    emit_load_fm(c, xb.chunks(), dr["x"], TC)
    G = c.sbuf("ap_g", [128, 16, TC], BF16)
    SLB = c.sbuf("ap_slb", [128, 3, 512], F32)
    ld = c.sem("ap_ld")
    for j in range(16):
        for ns in range(4):
            b0 = ld.n
            with nc.Block() as ph:
                @ph.gpsimd
                def _(e, j=j, ns=ns, b0=b0):
                    rs = slice(j * 128, (j + 1) * 128)
                    cs = slice(ns * 512, (ns + 1) * 512)
                    ld.inc(e.dma_start(out=SLB[:, 0, :], in_=dr["yfm"][rs, cs]), 16)
                    ld.inc(e.dma_start(out=SLB[:, 1, :], in_=dr["ufm"][rs, cs]), 16)
                    ld.inc(e.dma_start(out=SLB[:, 2, :], in_=dr["szfm"][rs, cs]), 16)
                    e.wait_ge(ld.h, b0 + 48)
            with nc.Block() as ph:
                def vv(e, j=j, ns=ns):
                    e.scalar_tensor_tensor(G[:, j, ns * 512:(ns + 1) * 512],
                                           SLB[:, 1, :], dcol[:, j:j + 1],
                                           SLB[:, 0, :], ALU.mult, ALU.add)
                    e.tensor_tensor(SLB[:, 0, :],
                                    G[:, j, ns * 512:(ns + 1) * 512],
                                    SLB[:, 2, :], ALU.mult)
                    e.tensor_copy(G[:, j, ns * 512:(ns + 1) * 512], SLB[:, 0, :])
                ph.vector(vv)
    wpair = c.sbuf("ap_wp", [128, 16, 128], BF16)
    for m in range(NCH):
        emit_load_weight(c, None, 2048, 128, wpair,
                         dram_ap=dr["woutT"][:, m * 128:(m + 1) * 128])
        pp = PsumPool(nc)
        with nc.Block() as ph:
            def evac(e, m_, ns, w, ps, m=m):
                return xb.add_evac(e, m, slice(ns * 512, ns * 512 + w), ps)
            mm_loop(c, ph, pp, wpair, 16, [0], lambda m_: (0, 128), TC,
                    lambda k, ns, w: G[:, k, ns * 512:ns * 512 + w],
                    evac, sem_role="ap_wo")
        pp.close()
    emit_store_fm(c, xb.chunks(), y, TC)
    c.close()
    return nc


def prep_a1a(inp):
    w = {}
    w["a1_winT"] = f32c(np.asarray(inp["L1_attn_a_W_in"]).T)
    cwv = np.asarray(inp["L1_attn_a_conv_w"])[:, 0, :]
    w["a1_convw"] = f32c(np.concatenate(
        [col_layout(cwv[:, t]) for t in range(4)], axis=1))
    w["a1_convb"] = col_layout(inp["L1_attn_a_conv_b"])
    Wx = np.asarray(inp["L1_attn_a_W_x"])
    w["a1_wxT"] = f32c(Wx.T)
    w["a1_wdt"] = col_layout(np.asarray(inp["L1_attn_a_W_dt"])[:, 0])
    w["a1_bdt"] = col_layout(inp["L1_attn_a_b_dt"])
    al = np.asarray(inp["L1_attn_a_A_log"])
    w["a1_alog"] = [f32c(np.concatenate(
        [al[512 * q + 128 * jj:512 * q + 128 * (jj + 1)] for jj in range(4)],
        axis=1)) for q in range(4)]
    w["a1_dcol"] = col_layout(inp["L1_attn_a_D_param"])
    w["a1_woutT"] = f32c(np.asarray(inp["L1_attn_a_W_out"]).T)
    return w


def run_a1a(x_chunks, w):
    nc = _nc_cache('a1a_pre', build_a1a_pre)
    prevs = prev_rows(x_chunks, 3)
    maps = []
    for cidx in range(NCORES):
        m = {"x": x_chunks[cidx], "xh3": prevs[cidx], "winT": w["a1_winT"],
             "convw": w["a1_convw"], "convb": w["a1_convb"], "wxT": w["a1_wxT"],
             "wdtcol": w["a1_wdt"], "bdtcol": w["a1_bdt"]}
        m.update(_CONSTS)
        maps.append(m)
    res = run_bass_kernel_spmd(nc, maps, core_ids=list(range(NCORES)))
    pre = [dict(delta=res.results[i]["delta"], u=res.results[i]["u"],
                sz=res.results[i]["sz"], bc=res.results[i]["bc"])
           for i in range(NCORES)]
    nc = _nc_cache('a1a_scan', build_a1a_scan)
    maps = []
    for cidx in range(NCORES):
        s, q = divmod(cidx, CPS)
        dfull = np.concatenate([pre[4 * s + qq]["delta"] for qq in range(4)], axis=1)
        ufull = np.concatenate([pre[4 * s + qq]["u"] for qq in range(4)], axis=1)
        bcf = np.concatenate([pre[4 * s + qq]["bc"] for qq in range(4)], axis=1)
        rs = slice(512 * q, 512 * (q + 1))
        m = {"dsl": f32c(dfull[rs]), "usl": f32c(ufull[rs]),
             "bf": f32c(bcf[0:16]), "cf": f32c(bcf[16:32]),
             "alog": w["a1_alog"][q]}
        m.update(_CONSTS)
        maps.append(m)
    res = run_bass_kernel_spmd(nc, maps, core_ids=list(range(NCORES)))
    ys = [res.results[i]["y"] for i in range(NCORES)]
    nc = _nc_cache('a1a_post', build_a1a_post)
    maps = []
    for cidx in range(NCORES):
        s, q = divmod(cidx, CPS)
        yfull = np.concatenate([ys[4 * s + qq] for qq in range(4)], axis=0)
        cs = slice(TC * q, TC * (q + 1))
        m = {"x": x_chunks[cidx], "yfm": f32c(yfull[:, cs]),
             "ufm": f32c(pre[cidx]["u"]), "szfm": f32c(pre[cidx]["sz"]),
             "dcol": w["a1_dcol"], "woutT": w["a1_woutT"]}
        m.update(_CONSTS)
        maps.append(m)
    res = run_bass_kernel_spmd(nc, maps, core_ids=list(range(NCORES)))
    return [res.results[i]["y"] for i in range(NCORES)]



def build_a0b_kv():
    nc = bass.Bass("TRN2", target_bir_lowering=False)
    c = Ctx(nc)
    dr = {}
    _din(nc, dr, "x", (TC, D))
    _din(nc, dr, "wkT", (1024, 1024))
    _din(nc, dr, "wvT", (1024, 1024))
    _din(nc, dr, "bkv", (128, 2 * NCH))
    _din(nc, dr, "identf", (128, 128))
    kout = nc.dram_tensor("kfm", [1024, TC], F32, kind="ExternalOutput").ap()
    vout = nc.dram_tensor("vfm", [1024, TC], F32, kind="ExternalOutput").ap()
    emit_setup(c, dr)
    bkv = emit_load_small(c, dr["bkv"], [128, 2 * NCH], "b_bkv")
    xb = XBufs(c, TC)
    emit_load_fm(c, xb.chunks(), dr["x"], TC)
    XN = c.sbuf("XN", [128, NCH, TC], BF16)
    emit_ln(c, xb.chunks(), XN, TC)
    KF = c.sbuf("b_kf", [128, NCH, TC], BF16)
    VF = c.sbuf("b_vf", [128, NCH, TC], BF16)
    wpair = c.sbuf("b_wp", [128, NCH, 128], BF16)
    for tag, wname, dst, bo in (("k", "wkT", KF, 0), ("v", "wvT", VF, NCH)):
        for m in range(NCH):
            emit_load_weight(c, None, 1024, 128, wpair,
                             dram_ap=dr[wname][:, m * 128:(m + 1) * 128])
            pp = PsumPool(nc)
            with nc.Block() as ph:
                mm_loop(c, ph, pp, wpair, NCH, [0], lambda m_: (0, 128), TC,
                        lambda k, ns, w: XN[:, k, ns * 512:ns * 512 + w],
                        lambda e, m_, ns, w, ps, dst=dst, m=m, bo=bo:
                            e.tensor_scalar(dst[:, m, ns * 512:ns * 512 + w], ps,
                                            bkv[:, bo + m:bo + m + 1], None,
                                            ALU.add),
                        sem_role="b_kv")
            pp.close()
    with nc.Block() as ph:
        @ph.gpsimd
        def _(e):
            st = c.sem("b_kst")
            b0 = st.n
            for m in range(NCH):
                st.inc(e.dma_start(out=kout[m * 128:(m + 1) * 128, :],
                                   in_=KF[:, m, :]), 16)
            e.wait_ge(st.h, b0 + 16 * NCH)
    with nc.Block() as ph:
        @ph.gpsimd
        def _(e):
            st = c.sem("b_vst")
            b0 = st.n
            for m in range(NCH):
                st.inc(e.dma_start(out=vout[m * 128:(m + 1) * 128, :],
                                   in_=VF[:, m, :]), 16)
            e.wait_ge(st.h, b0 + 16 * NCH)
    c.close()
    return nc


def build_a0b_main():
    nc = bass.Bass("TRN2", target_bir_lowering=False)
    c = Ctx(nc)
    dr = {}
    _din(nc, dr, "x", (TC, D))
    _din(nc, dr, "kfm", (1024, T))
    _din(nc, dr, "vfm", (1024, T))
    _din(nc, dr, "wqT", (1024, 1024))
    _din(nc, dr, "bq", (128, NCH))
    _din(nc, dr, "woT", (1024, 1024))
    _din(nc, dr, "bogb", (128, 3 * NCH))
    _din(nc, dr, "identf", (128, 128))
    y = nc.dram_tensor("y", [TC, D], F32, kind="ExternalOutput").ap()
    emit_setup(c, dr)
    bogb = emit_load_small(c, dr["bogb"], [128, 3 * NCH], "b_bogb")
    xb = XBufs(c, TC)
    emit_load_fm(c, xb.chunks(), dr["x"], TC)
    XN = c.sbuf("XN", [128, NCH, TC], BF16)
    emit_ln(c, xb.chunks(), XN, TC)
    QF = c.sbuf("b_qf", [128, NCH, TC], BF16)
    wpair = c.sbuf("b_wp", [128, NCH, 128], BF16)
    bq = emit_load_small(c, dr["bq"], [128, NCH], "b_bq")
    for m in range(NCH):
        emit_load_weight(c, None, 1024, 128, wpair,
                         dram_ap=dr["wqT"][:, m * 128:(m + 1) * 128])
        pp = PsumPool(nc)
        with nc.Block() as ph:
            mm_loop(c, ph, pp, wpair, NCH, [0], lambda m_: (0, 128), TC,
                    lambda k, ns, w: XN[:, k, ns * 512:ns * 512 + w],
                    lambda e, m_, ns, w, ps, m=m:
                        e.tensor_scalar(QF[:, m, ns * 512:ns * 512 + w], ps,
                                        bq[:, m:m + 1], None, ALU.add),
                    sem_role="b_q")
        pp.close()
    ones_col = c.sbuf("ones_col_bf", [128, 1], BF16)
    ones_row = c.sbuf("ones_row_bf", [1, 128], BF16)
    stk = ExitStack()
    negM = stk.enter_context(nc.sbuf_tensor("b_negm", [128, 1], F32))
    MROWb = stk.enter_context(nc.sbuf_tensor("b_mrow", [1, 16], BF16))
    stk2 = ExitStack()
    SQ = stk2.enter_context(nc.sbuf_tensor("b_sq", [128, NCH, 512], BF16))
    KB8 = stk2.enter_context(nc.sbuf_tensor("b_kb8", [128, NCH, 512], BF16))
    NRM = stk2.enter_context(nc.sbuf_tensor("b_nrm", [1, 512], F32))
    MX = stk2.enter_context(nc.sbuf_tensor("b_mx", [1, 24], F32))
    MS = stk2.enter_context(nc.sbuf_tensor("b_ms", [1, 8], F32))
    mbf = stk2.enter_context(nc.sbuf_tensor("b_mbf", [1, 1], BF16))
    AO = QF
    hoc = stk2.enter_context(nc.sbuf_tensor("b_hoc", [128, 2], BF16))
    MXQ = stk2.enter_context(nc.sbuf_tensor("b_mxq", [2, NCH, 4], F32))
    MXK = stk2.enter_context(nc.sbuf_tensor("b_mxk", [2, NCH, 16], F32))
    QMK = stk2.enter_context(nc.sbuf_tensor("b_qmk", [2, NCH], F32))
    KMK = stk2.enter_context(nc.sbuf_tensor("b_kmk", [2, NCH], F32))
    PRD2 = stk2.enter_context(nc.sbuf_tensor("b_prd2", [2, NCH], F32))
    PRDb = stk2.enter_context(nc.sbuf_tensor("b_prdb", [2, NCH], BF16))
    NPS = stk2.enter_context(nc.sbuf_tensor("b_nps", [NCH, 2], BF16))
    identbf = c.sbuf("identbf", [128, 128], BF16)
    with nc.Block() as ph:
        @ph.vector
        def _(e):
            e.memset(hoc[0:64, 0:1], 1.0)
            e.memset(hoc[64:128, 0:1], 0.0)
            e.memset(hoc[0:64, 1:2], 0.0)
            e.memset(hoc[64:128, 1:2], 1.0)
    ppn = PsumPool(nc)
    psn8 = [ppn.get([2, 512]) for _ in range(NCH)]
    for ns in range(4):
        with nc.Block() as ph:
            @ph.scalar
            def _(e, ns=ns):
                for k in range(NCH):
                    e.activation(SQ[:, k, :], QF[:, k, ns * 512:(ns + 1) * 512],
                                 AF.Square)
        with nc.Block() as ph:
            @ph.tensor
            def _(e):
                for k in range(NCH):
                    e.matmul(psn8[k][0:2, :], hoc[:, :], SQ[:, k, :],
                             start=True, stop=True)
        with nc.Block() as ph:
            @ph.vector
            def _(e, ns=ns):
                for k in range(NCH):
                    e.reduce_max(MXQ[0:2, k, ns:ns + 1], psn8[k][0:2, :],
                                 axis=mybir.AxisListType.X)
    for ns in range(16):
        with nc.Block() as ph:
            @ph.gpsimd
            def _(e, ns=ns):
                st = c.sem("b_kld")
                b0 = st.n
                for k in range(NCH):
                    st.inc(e.dma_start(out=KB8[:, k, :],
                                       in_=dr["kfm"][k * 128:(k + 1) * 128,
                                                     ns * 512:(ns + 1) * 512]), 16)
                e.wait_ge(st.h, b0 + 16 * NCH)
        with nc.Block() as ph:
            @ph.scalar
            def _(e):
                for k in range(NCH):
                    e.activation(SQ[:, k, :], KB8[:, k, :], AF.Square)
        with nc.Block() as ph:
            @ph.tensor
            def _(e):
                for k in range(NCH):
                    e.matmul(psn8[k][0:2, :], hoc[:, :], SQ[:, k, :],
                             start=True, stop=True)
        with nc.Block() as ph:
            @ph.vector
            def _(e, ns=ns):
                for k in range(NCH):
                    e.reduce_max(MXK[0:2, k, ns:ns + 1], psn8[k][0:2, :],
                                 axis=mybir.AxisListType.X)
    ppn.close()
    with nc.Block() as ph:
        @ph.vector
        def _(e):
            for k in range(NCH):
                e.reduce_max(QMK[0:2, k:k + 1], MXQ[0:2, k, :],
                             axis=mybir.AxisListType.X)
            for k in range(NCH):
                e.reduce_max(KMK[0:2, k:k + 1], MXK[0:2, k, :],
                             axis=mybir.AxisListType.X)
    with nc.Block() as ph:
        @ph.vector
        def _(e):
            e.tensor_tensor(PRD2[:, :], QMK[:, :], KMK[:, :], ALU.mult)
    with nc.Block() as ph:
        @ph.scalar
        def _(e):
            e.activation(QMK[:, :], PRD2[:, :], AF.Ln)
    with nc.Block() as ph:
        @ph.scalar
        def _(e):
            e.activation(KMK[:, :], QMK[:, :], AF.Exp, scale=0.5)
    with nc.Block() as ph:
        @ph.vector
        def _(e):
            e.tensor_scalar(PRDb[:, :], KMK[:, :], -0.125, None, ALU.mult)
    ppt = PsumPool(nc)
    psT = ppt.get([NCH, 2])
    with nc.Block() as ph:
        @ph.tensor
        def _(e):
            e.matmul(psT[0:NCH, 0:2], PRDb[0:2, 0:NCH], identbf[0:2, 0:2],
                     start=True, stop=True)
    with nc.Block() as ph:
        @ph.vector
        def _(e):
            e.tensor_copy(NPS[:, :], psT[0:NCH, 0:2])
    ppt.close()
    with nc.Block() as ph:
        @ph.gpsimd
        def _(e):
            st = c.sem("b_mrd")
            b0 = st.n
            for k in range(NCH):
                st.inc(e.dma_start(out=MROWb[0:1, 2 * k:2 * k + 2],
                                   in_=NPS[k:k + 1, 0:2]), 16)
            e.wait_ge(st.h, b0 + 16 * NCH)
    stk2.close()

    negMall = stk.enter_context(nc.sbuf_tensor("b_negmall", [128, 16], F32))
    pp0 = PsumPool(nc)
    psb0 = pp0.get([128, 512], F32)
    with nc.Block() as ph:
        @ph.tensor
        def _(e):
            e.matmul(psb0[:, 0:16], ones_row[0:1, :], MROWb[0:1, 0:16],
                     start=True, stop=True)
    with nc.Block() as ph:
        @ph.vector
        def _(e):
            e.tensor_copy(negMall[:, :], psb0[:, 0:16])
    pp0.close()
    KVF = stk.enter_context(nc.sbuf_tensor("b_kvf", [128, T], BF16))
    KH = KVF[0:64, :]
    VT = stk.enter_context(nc.sbuf_tensor("b_vt", [128, 64, 65], BF16))
    scrS = stk.enter_context(nc.sbuf_tensor("b_scrs", [128, 2, 512], F32))
    PN = stk.enter_context(nc.sbuf_tensor("b_pn", [128, 2, 512], BF16))
    obuf = stk.enter_context(nc.sbuf_tensor("b_obuf", [64, 512], BF16))
    drow = stk.enter_context(nc.sbuf_tensor("b_drow", [1, 2, 512], F32))
    dbf = stk.enter_context(nc.sbuf_tensor("b_dbf", [1, 2, 512], BF16))
    QH = stk.enter_context(nc.sbuf_tensor("b_qh", [64, TC], BF16))
    AOH = stk.enter_context(nc.sbuf_tensor("b_aoh", [64, TC], BF16))
    VFh = KVF[64:128, :]
    with nc.Block() as ph:
        @ph.vector
        def _(e):
            e.memset(VT[:, :, 64:65], 1.0)
    for h in range(16):
        with nc.Block() as ph:
            @ph.gpsimd
            def _(e, h=h):
                st = c.sem("b_hld")
                b0 = st.n
                st.inc(e.dma_start(out=KVF[0:64, :],
                                   in_=dr["kfm"][h * 64:(h + 1) * 64, :]), 16)
                st.inc(e.dma_start(out=QH[:, :],
                                   in_=QF[(h % 2) * 64:(h % 2) * 64 + 64,
                                          h // 2, :]), 16)
                st.inc(e.dma_start(out=KVF[64:128, :],
                                   in_=dr["vfm"][h * 64:(h + 1) * 64, :]), 16)
                e.wait_ge(st.h, b0 + 16 * 3)
        fS, fM, fE, fO = (c.sem(f"b_a{k}") for k in "abcd")
        fD, fB, fV = (c.sem(f"b_a{k}") for k in "efg")
        bfS, bfM, bfE, bfO = fS.n, fM.n, fE.n, fO.n
        bfD, bfB, bfV = fD.n, fB.n, fV.n
        pp = PsumPool(nc)
        psS = pp.get([128, 2, 512], F32)
        psO = pp.get([128, 2, 512], F32)
        psD = pp.get([128, 2, 512], F32)
        pstv = pp.get([128, 2, 1024], BF16)
        qpart = slice((h % 2) * 64, (h % 2) * 64 + 64)
        qch = h // 2
        vt_s, vt_e = c.sem("b_vts"), c.sem("b_vte")
        bvs, bve = vt_s.n, vt_e.n
        with nc.Block() as ph:
            @ph.tensor
            def _(e):
                for kc in range(64):
                    if kc >= 2:
                        e.wait_ge(vt_e.h, bve + kc - 1)
                    vt_s.inc(e.transpose(pstv[:, kc % 2, 0:64],
                                         KVF[64:128, kc * 128:(kc + 1) * 128],
                                         identbf[64:128, 64:128]), 1)

            @ph.vector
            def _(e):
                for kc in range(64):
                    e.wait_ge(vt_s.h, bvs + kc + 1)
                    vt_e.inc(e.tensor_copy(VT[:, kc, 0:64],
                                           pstv[:, kc % 2, 0:64]), 1)
        with nc.Block() as ph:
            @ph.tensor
            def _(e, h=h):
                def emit_O(gp):
                    itp, kcp = divmod(gp, 64)
                    if kcp == 0 and itp >= 2:
                        e.wait_ge(fV.h, bfV + itp - 1)
                    e.wait_ge(fE.h, bfE + gp + 1)
                    fO.inc(e.matmul(psO[0:65, itp % 2, 0:512],
                                    VT[:, kcp, :], PN[:, gp % 2, :],
                                    start=(kcp == 0), stop=(kcp == 63)), 1)
                    if kcp == 63:
                        e.wait_ge(fD.h, bfD + itp + 1)
                        if itp >= 2:
                            e.wait_ge(fV.h, bfV + itp - 1)
                        fB.inc(e.matmul(psD[0:64, itp % 2, 0:512],
                                        ones_row[0:1, 0:64],
                                        dbf[0:1, itp % 2, :],
                                        start=True, stop=True), 1)
                for it in range(4):
                    for kc in range(64):
                        g = 64 * it + kc
                        if g >= 2:
                            e.wait_ge(fE.h, bfE + g - 1)
                        e.matmul(psS[:, g % 2, 0:512],
                                 KVF[0:64, kc * 128:(kc + 1) * 128],
                                 QH[:, it * 512:(it + 1) * 512],
                                 start=True, stop=True).then_inc(fS.h, 1)
                        fS.n += 1
                        if g >= 1:
                            emit_O(g - 1)
                emit_O(255)

            @ph.vector
            def _(e, h=h):
                for it in range(4 + 1):
                    if it >= 1:
                        jt = it - 1
                        e.wait_ge(fB.h, bfB + jt + 1)
                        e.tensor_copy(obuf[:, :], psO[0:64, jt % 2, 0:512])
                        fV.inc(e.tensor_tensor(
                            AOH[:, jt * 512:(jt + 1) * 512],
                            obuf[:, :], psD[0:64, jt % 2, 0:512], ALU.mult), 1)
                    if it < 4:
                        e.wait_ge(fO.h, bfO + 64 * (it + 1))
                        if it >= 2:
                            e.wait_ge(fB.h, bfB + it - 1)
                        e.reciprocal(drow[0:1, it % 2, :],
                                     psO[64:65, it % 2, 0:512])
                        fD.inc(e.tensor_copy(dbf[0:1, it % 2, :],
                                             drow[0:1, it % 2, :]), 1)

            @ph.scalar
            def _(e, h=h):
                for it in range(4):
                    for kc in range(64):
                        g = 64 * it + kc
                        e.wait_ge(fS.h, bfS + g + 1)
                        if g >= 2:
                            e.wait_ge(fO.h, bfO + g - 1)
                        fE.inc(e.activation(PN[:, g % 2, :],
                                            psS[:, g % 2, 0:512], AF.Exp,
                                            bias=negMall[:, h:h + 1],
                                            scale=0.125), 1)
        with nc.Block() as ph:
            @ph.gpsimd
            def _(e, h=h, qpart=qpart, qch=qch):
                st = c.sem("b_hst")
                b0 = st.n
                st.inc(e.dma_start(out=AO[qpart, qch, :], in_=AOH[:, :]), 16)
                e.wait_ge(st.h, b0 + 16)
        pp.close()
    stk.close()
    T1 = c.sbuf("b_t1", [128, NCH, TC], BF16)
    for m in range(NCH):
        emit_load_weight(c, None, 1024, 128, wpair,
                         dram_ap=dr["woT"][:, m * 128:(m + 1) * 128])
        pp = PsumPool(nc)
        with nc.Block() as ph:
            def evac(e, m_, ns, w, ps, m=m):
                sl = slice(ns * 512, ns * 512 + w)
                e.tensor_scalar(xb.scr[:, 0:w], ps, bogb[:, m:m + 1], None,
                                ALU.add)
                return e.tensor_tensor(T1[:, m, sl], xb.scr[:, 0:w],
                                       XN[:, m, sl], ALU.add)
            mm_loop(c, ph, pp, wpair, NCH, [0], lambda m_: (0, 128), TC,
                    lambda k, ns, w: AO[:, k, ns * 512:ns * 512 + w],
                    evac, sem_role="b_wo")
        pp.close()
    XN2 = XN
    emit_ln(c, [(T1.rearrange("p a b -> p (a b)"), j * TC) for j in range(NCH)],
            XN2, TC,
            gamma=lambda j: bogb[:, NCH + j:NCH + j + 1],
            beta=lambda j: bogb[:, 2 * NCH + j:2 * NCH + j + 1])
    with nc.Block() as ph:
        @ph.vector
        def _(e):
            for j in range(NCH):
                for ns in range(4):
                    sl = slice(ns * 512, (ns + 1) * 512)
                    e.tensor_tensor(xb.scr[:, 0:512], XN2[:, j, sl],
                                    xb.cur(j)[:, sl], ALU.add)
                    e.tensor_copy(xb.cur(j)[:, sl], xb.scr[:, 0:512])
    emit_store_fm(c, xb.chunks(), y, TC)
    c.close()
    return nc


def run_a0b(x_chunks, w):
    nc = _nc_cache('a0b_kv', build_a0b_kv)
    maps = []
    for cidx in range(NCORES):
        m = {"x": x_chunks[cidx], "wkT": w["b_wkT"], "wvT": w["b_wvT"],
             "bkv": w["b_bkv"]}
        m.update(_CONSTS)
        maps.append(m)
    res = run_bass_kernel_spmd(nc, maps, core_ids=list(range(NCORES)))
    kv = [(res.results[i]["kfm"], res.results[i]["vfm"]) for i in range(NCORES)]
    nc = _nc_cache('a0b_main', build_a0b_main)
    maps = []
    for cidx in range(NCORES):
        s = cidx // CPS
        kf = np.concatenate([kv[4 * s + q][0] for q in range(4)], axis=1)
        vt = np.concatenate([kv[4 * s + q][1] for q in range(4)], axis=1)
        m = {"x": x_chunks[cidx], "kfm": f32c(kf), "vfm": f32c(vt),
             "wqT": w["b_wqT"], "bq": w["b_bq"], "woT": w["b_woT"],
             "bogb": w["b_bogb"]}
        m.update(_CONSTS)
        maps.append(m)
    res = run_bass_kernel_spmd(nc, maps, core_ids=list(range(NCORES)))
    return [res.results[i]["y"] for i in range(NCORES)]



_NCC = {}


def _nc_cache(key, fn):
    if key not in _NCC:
        _NCC[key] = fn()
    return _NCC[key]


def prep_a1b(inp):
    w = {}
    w["1b_wqkv"] = f32c(inp["L1_attn_b_W_qkv"])
    w["1b_wug"] = f32c(inp["L1_attn_b_W_ug"])
    w["1b_bw"] = f32c(inp["L1_attn_b_B_w"])
    w["1b_bug"] = col_layout(inp["L1_attn_b_b_ug"])
    w["1b_g1b1"] = f32c(np.concatenate(
        [col_layout(inp["L1_attn_b_ln1_g"]),
         col_layout(inp["L1_attn_b_ln1_b"])], axis=1))
    w["1b_acol"] = f32c(np.asarray(inp["L1_attn_b_A"]).reshape(128, 1))
    w["1b_wo"] = f32c(inp["L1_attn_b_W_O"])
    w["1b_boc"] = col_layout(inp["L1_attn_b_b_O"])
    w["1b_cw"] = f32c(inp["L1_attn_b_C_w"])
    w["1b_g2b2"] = f32c(np.concatenate(
        [col_layout(inp["L1_attn_b_ln2_g"]),
         col_layout(inp["L1_attn_b_ln2_b"])], axis=1))
    w["1b_w1"] = f32c(inp["L1_attn_b_mlp_W1"])
    w["1b_b1c"] = col_layout(inp["L1_attn_b_mlp_b1"])
    w["1b_w2"] = f32c(inp["L1_attn_b_mlp_W2"])
    w["1b_b2c"] = col_layout(inp["L1_attn_b_mlp_b2"])
    q = np.arange(512)[None, :]
    m = np.zeros((128, 12, 512), np.float32)
    for kc in range(6):
        krel = (np.arange(kc * 128, kc * 128 + 128) - 256)[:, None]
        mid = (krel > q - 256) & (krel <= q)
        first = mid & (krel >= 0)
        m[:, 6 + kc, :] = np.where(mid, 0.0, -30000.0)
        m[:, kc, :] = np.where(first, 0.0, -30000.0)
    w["1b_masks"] = f32c(m.reshape(128, 12 * 512))
    mm = m.copy()
    mm[:, 0:6, :] = m[:, 6:12, :]
    w["1b_masks_mid"] = f32c(mm.reshape(128, 12 * 512))
    return w


def build_a1b_p1():
    nc = bass.Bass("TRN2", target_bir_lowering=False)
    c = Ctx(nc)
    dr = {}
    _din(nc, dr, "x", (TC, D))
    _din(nc, dr, "wqkv", (1024, 3072))
    _din(nc, dr, "wug", (1024, 1152))
    _din(nc, dr, "bw", (1024, 128))
    _din(nc, dr, "bug", (128, 9))
    _din(nc, dr, "g1b1", (128, 16))
    _din(nc, dr, "identf", (128, 128))
    qkvo = nc.dram_tensor("qkvfm", [3072, TC], F32, kind="ExternalOutput").ap()
    gout = nc.dram_tensor("gfm", [1024, TC], F32, kind="ExternalOutput").ap()
    dout = nc.dram_tensor("drv", [128, TC], F32, kind="ExternalOutput").ap()
    emit_setup(c, dr)
    bug = emit_load_small(c, dr["bug"], [128, 9], "1b_bug")
    g1b1 = emit_load_small(c, dr["g1b1"], [128, 16], "1b_g1b1")
    xb = XBufs(c, TC)
    emit_load_fm(c, xb.chunks(), dr["x"], TC)
    XNI = c.sbuf("XN", [128, NCH, TC], BF16)
    emit_ln(c, xb.chunks(), XNI, TC)
    XN1 = c.sbuf("1b_xn1", [128, NCH, TC], BF16)
    emit_ln(c, [(XNI.rearrange("p a b -> p (a b)"), j * TC) for j in range(NCH)],
            XN1, TC, gamma=lambda j: g1b1[:, j:j + 1],
            beta=lambda j: g1b1[:, NCH + j:NCH + j + 1])
    wpair = c.sbuf("1b_wp", [128, NCH, 128], BF16)
    wpair2 = c.sbuf("1b_wp2", [128, NCH, 128], BF16)
    OB = c.sbuf("1b_ob", [128, TC], BF16)
    st = c.sem("1b_st")
    jobs = [(dr["wqkv"], m, qkvo, m, None) for m in range(24)] + \
           [(dr["wug"], m, gout, m - 1, m) for m in range(1, 9)]
    for (wsrc, m, dst, mo, bci) in jobs:
        emit_load_weight(c, None, 1024, 128, wpair,
                         dram_ap=wsrc[:, m * 128:(m + 1) * 128])
        pp = PsumPool(nc)
        with nc.Block() as ph:
            if bci is None:
                def evac(e, m_, ns, w, ps):
                    return e.tensor_copy(OB[:, ns * 512:ns * 512 + w], ps)
            else:
                def evac(e, m_, ns, w, ps, bci=bci):
                    return e.activation(OB[:, ns * 512:ns * 512 + w], ps,
                                        AF.Sigmoid, bias=bug[:, bci:bci + 1])
            mm_loop(c, ph, pp, wpair, NCH, [0], lambda m_: (0, 128), TC,
                    lambda k, ns, w: XN1[:, k, ns * 512:ns * 512 + w],
                    evac, evac_eng=("vector" if bci is None else "scalar"),
                    sem_role="1b_mm")
        pp.close()
        with nc.Block() as ph:
            @ph.gpsimd
            def _(e, dst=dst, mo=mo):
                b0 = st.n
                st.inc(e.dma_start(out=dst[mo * 128:(mo + 1) * 128, :],
                                   in_=OB[:, :]), 16)
                e.wait_ge(st.h, b0 + 16)
    emit_load_weight(c, None, 1024, 128, wpair, dram_ap=dr["wug"][:, 0:128])
    emit_load_weight(c, None, 1024, 128, wpair2, dram_ap=dr["bw"])
    DRV = c.sbuf("1b_drv", [128, TC], F32)
    pp = PsumPool(nc)

    def wb2(k, m):
        return wpair[:, k, 0:128] if k < NCH else wpair2[:, k - NCH, 0:128]
    with nc.Block() as ph:
        mm_loop(c, ph, pp, wb2, 2 * NCH, [0], lambda m_: (0, 128), TC,
                lambda k, ns, w: XN1[:, k % NCH, ns * 512:ns * 512 + w],
                lambda e, m_, ns, w, ps: e.tensor_scalar(
                    DRV[:, ns * 512:ns * 512 + w], ps, bug[:, 0:1], None,
                    ALU.add),
                sem_role="1b_dv")
    pp.close()
    with nc.Block() as ph:
        @ph.gpsimd
        def _(e):
            b0 = st.n
            st.inc(e.dma_start(out=dout, in_=DRV[:, :]), 16)
            e.wait_ge(st.h, b0 + 16)
    c.close()
    return nc


def build_a1b_p2():
    nc = bass.Bass("TRN2", target_bir_lowering=False)
    c = Ctx(nc)
    dr = {}
    _din(nc, dr, "drv", (128, T))
    _din(nc, dr, "acol", (128, 1))
    y = nc.dram_tensor("sfm", [128, T], F32, kind="ExternalOutput").ap()
    DB = c.sbuf("2_db", [128, T], F32)
    HB = c.sbuf("2_hb", [128, T], F32)
    AB = c.sbuf("2_ab", [128, TC], F32)
    AC = c.sbuf("2_ac", [128, 1], F32)
    ON = c.sbuf("2_on", [128, TC], F32)
    STC = c.sbuf("2_stc", [128, 2], F32)
    ld = c.sem("2_ld")
    st = c.sem("2_sto")
    with nc.Block() as ph:
        @ph.gpsimd
        def _(e):
            ld.inc(e.dma_start(out=DB[:, :], in_=dr["drv"]), 16)
            ld.inc(e.dma_start(out=AC[:, :], in_=dr["acol"]), 16)
            e.wait_ge(ld.h, ld.n)
    with nc.Block() as ph:
        @ph.vector
        def _(e):
            e.memset(ON[:, :], 1.0)
            e.memset(STC[:, :], 0.0)
    with nc.Block() as ph:
        @ph.vector
        def _(e):
            e.tensor_scalar(AB[:, :], ON[:, :], AC[:, 0:1], None, ALU.mult)
    for s in range(4):
        with nc.Block() as ph:
            @ph.vector
            def _(e, s=s):
                sl = slice(s * TC, (s + 1) * TC)
                e.tensor_tensor_scan(HB[:, sl], AB[:, :], DB[:, sl],
                                     STC[:, s % 2:s % 2 + 1], ALU.mult, ALU.add)
                e.tensor_copy(STC[:, (s + 1) % 2:(s + 1) % 2 + 1],
                              HB[:, (s + 1) * TC - 1:(s + 1) * TC])
    with nc.Block() as ph:
        @ph.gpsimd
        def _(e):
            st.inc(e.dma_start(out=y, in_=HB[:, :]), 16)
            e.wait_ge(st.h, st.n)
    c.close()
    return nc


def build_a1b_p3():
    nc = bass.Bass("TRN2", target_bir_lowering=False)
    c = Ctx(nc)
    dr = {}
    _din(nc, dr, "x", (TC, D))
    _din(nc, dr, "qfm", (1024, TC))
    _din(nc, dr, "kext", (1024, TC + 256))
    _din(nc, dr, "vext", (1024, TC + 256))
    _din(nc, dr, "gfm", (1024, TC))
    _din(nc, dr, "sfm", (128, TC))
    _din(nc, dr, "wo", (1024, 1024))
    _din(nc, dr, "boc", (128, NCH))
    _din(nc, dr, "cw", (128, 1024))
    _din(nc, dr, "g2b2", (128, 16))
    _din(nc, dr, "w1", (1024, 4096))
    _din(nc, dr, "b1c", (128, 32))
    _din(nc, dr, "w2", (4096, 1024))
    _din(nc, dr, "b2c", (128, NCH))
    _din(nc, dr, "masks", (128, 12 * 512))
    _din(nc, dr, "identf", (128, 128))
    y = nc.dram_tensor("y", [TC, D], F32, kind="ExternalOutput").ap()
    emit_setup(c, dr)
    boc = emit_load_small(c, dr["boc"], [128, NCH], "3_boc")
    g2b2 = emit_load_small(c, dr["g2b2"], [128, 16], "3_g2b2")
    b1c = emit_load_small(c, dr["b1c"], [128, 32], "3_b1c")
    b2c = emit_load_small(c, dr["b2c"], [128, NCH], "3_b2c")
    ones_col = c.sbuf("ones_col_bf", [128, 1], BF16)
    ones_row = c.sbuf("ones_row_bf", [1, 128], BF16)
    identbf = c.sbuf("identbf", [128, 128], BF16)
    c.sbuf("ld_stage", [128, 1024], F32)
    c.sbuf("ln_row", [1, 2, 2, 512], F32)
    c.sbuf("ln_var", [1, 2, 512], F32)
    c.sbuf("ln_scrB", [128, 512], F32)
    c.sbuf("ln_rbf", [1, 2, 2, 512], BF16)
    c.sbuf("ln_scr2", [1, 512], F32)
    c.sbuf("ln_sq", [128, 2, 512], BF16)
    c.sbuf("ln_scr", [128, 512], F32)
    XNI = c.sbuf("XN", [128, NCH, TC], BF16)
    wpair = c.sbuf("3_wp", [128, NCH, 128], BF16)
    scrA = c.sbuf("3_scra", [128, 512], F32)
    scrB = c.sbuf("3_scrb", [128, 512], F32)
    scrC = c.sbuf("3_scrc", [128, 512], F32)
    stkX = ExitStack()
    xb = XBufs.__new__(XBufs)
    xb.tcols = TC
    xb.bufs = [stkX.enter_context(nc.sbuf_tensor(f"3x{i}", [128, TC], F32))
               for i in range(NCH)]
    xb.scr = stkX.enter_context(nc.sbuf_tensor("3xscr", [128, 512], F32))
    emit_load_fm(c, xb.chunks(), dr["x"], TC)
    emit_ln(c, xb.chunks(), XNI, TC)
    stkX.close()
    stk = ExitStack()
    AO = stk.enter_context(nc.sbuf_tensor("3_ao", [128, NCH, TC], BF16))
    MSKS = stk.enter_context(nc.sbuf_tensor("3_msk", [128, 12, 512], F32))
    negM = stk.enter_context(nc.sbuf_tensor("3_negm", [128, 1], F32))
    MROWb = stk.enter_context(nc.sbuf_tensor("3_mrow", [1, 16], BF16))
    with nc.Block() as ph:
        @ph.gpsimd
        def _(e):
            st = c.sem("3_mld")
            b0 = st.n
            st.inc(e.dma_start(out=MSKS.rearrange("p a b -> p (a b)"),
                               in_=dr["masks"]), 16)
            e.wait_ge(st.h, b0 + 16)
    stk2 = ExitStack()
    SQ = stk2.enter_context(nc.sbuf_tensor("3_sq", [128, NCH, 512], BF16))
    KB8 = stk2.enter_context(nc.sbuf_tensor("3_kb8", [128, NCH, 512], BF16))
    hoc = stk2.enter_context(nc.sbuf_tensor("3_hoc", [128, 2], BF16))
    MXQ = stk2.enter_context(nc.sbuf_tensor("3_mxq", [2, NCH, 4], F32))
    MXK = stk2.enter_context(nc.sbuf_tensor("3_mxk", [2, NCH, 5], F32))
    QMK = stk2.enter_context(nc.sbuf_tensor("3_qmk", [2, NCH], F32))
    KMK = stk2.enter_context(nc.sbuf_tensor("3_kmk", [2, NCH], F32))
    PRD2 = stk2.enter_context(nc.sbuf_tensor("3_prd2", [2, NCH], F32))
    PRDb = stk2.enter_context(nc.sbuf_tensor("3_prdb", [2, NCH], BF16))
    NPS = stk2.enter_context(nc.sbuf_tensor("3_nps", [NCH, 2], BF16))
    with nc.Block() as ph:
        @ph.vector
        def _(e):
            e.memset(hoc[0:64, 0:1], 1.0)
            e.memset(hoc[64:128, 0:1], 0.0)
            e.memset(hoc[0:64, 1:2], 0.0)
            e.memset(hoc[64:128, 1:2], 1.0)
    ppn = PsumPool(nc)
    psn8 = [ppn.get([2, 512]) for _ in range(NCH)]
    for src_name, n_sl, MXT in (("qfm", 4, MXQ), ("kext", 5, MXK)):
        wtot = TC if src_name == "qfm" else TC + 256
        for ns in range(n_sl):
            w_ = min(512, wtot - ns * 512)
            with nc.Block() as ph:
                @ph.gpsimd
                def _(e, ns=ns, w_=w_, src_name=src_name):
                    st = c.sem("3_kld")
                    b0 = st.n
                    for k in range(NCH):
                        st.inc(e.dma_start(
                            out=KB8[:, k, 0:w_],
                            in_=dr[src_name][k * 128:(k + 1) * 128,
                                             ns * 512:ns * 512 + w_]), 16)
                    e.wait_ge(st.h, b0 + 16 * NCH)
            with nc.Block() as ph:
                @ph.scalar
                def _(e, w_=w_):
                    for k in range(NCH):
                        e.activation(SQ[:, k, 0:w_], KB8[:, k, 0:w_], AF.Square)
            with nc.Block() as ph:
                @ph.tensor
                def _(e, w_=w_):
                    for k in range(NCH):
                        e.matmul(psn8[k][0:2, 0:w_], hoc[:, :], SQ[:, k, 0:w_],
                                 start=True, stop=True)
            with nc.Block() as ph:
                @ph.vector
                def _(e, ns=ns, w_=w_, MXT=MXT):
                    for k in range(NCH):
                        e.reduce_max(MXT[0:2, k, ns:ns + 1], psn8[k][0:2, 0:w_],
                                     axis=mybir.AxisListType.X)
    ppn.close()
    with nc.Block() as ph:
        @ph.vector
        def _(e):
            for k in range(NCH):
                e.reduce_max(QMK[0:2, k:k + 1], MXQ[0:2, k, :],
                             axis=mybir.AxisListType.X)
            for k in range(NCH):
                e.reduce_max(KMK[0:2, k:k + 1], MXK[0:2, k, :],
                             axis=mybir.AxisListType.X)
    with nc.Block() as ph:
        @ph.vector
        def _(e):
            e.tensor_tensor(PRD2[:, :], QMK[:, :], KMK[:, :], ALU.mult)
    with nc.Block() as ph:
        @ph.scalar
        def _(e):
            e.activation(QMK[:, :], PRD2[:, :], AF.Ln)
    with nc.Block() as ph:
        @ph.scalar
        def _(e):
            e.activation(KMK[:, :], QMK[:, :], AF.Exp, scale=0.5)
    with nc.Block() as ph:
        @ph.vector
        def _(e):
            e.tensor_scalar(PRDb[:, :], KMK[:, :], -1.0, None, ALU.mult)
    ppt = PsumPool(nc)
    psT = ppt.get([NCH, 2])
    with nc.Block() as ph:
        @ph.tensor
        def _(e):
            e.matmul(psT[0:NCH, 0:2], PRDb[0:2, 0:NCH], identbf[0:2, 0:2],
                     start=True, stop=True)
    with nc.Block() as ph:
        @ph.vector
        def _(e):
            e.tensor_copy(NPS[:, :], psT[0:NCH, 0:2])
    ppt.close()
    with nc.Block() as ph:
        @ph.gpsimd
        def _(e):
            st = c.sem("3_mrd")
            b0 = st.n
            for k in range(NCH):
                st.inc(e.dma_start(out=MROWb[0:1, 2 * k:2 * k + 2],
                                   in_=NPS[k:k + 1, 0:2]), 16)
            e.wait_ge(st.h, b0 + 16 * NCH)
    stk2.close()
    stkA = ExitStack()
    negMall = stkA.enter_context(nc.sbuf_tensor("3_negmall", [128, 16], F32))
    pp0 = PsumPool(nc)
    psb0 = pp0.get([128, 512], F32)
    with nc.Block() as ph:
        @ph.tensor
        def _(e):
            e.matmul(psb0[:, 0:16], ones_row[0:1, :], MROWb[0:1, 0:16],
                     start=True, stop=True)
    with nc.Block() as ph:
        @ph.vector
        def _(e):
            e.tensor_copy(negMall[:, :], psb0[:, 0:16])
    pp0.close()
    KVF = stkA.enter_context(nc.sbuf_tensor("3_kvf", [128, TC + 256], BF16))
    KH = KVF[0:64, :]
    VT = stkA.enter_context(nc.sbuf_tensor("3_vt", [128, 18, 65], BF16))
    QH = stkA.enter_context(nc.sbuf_tensor("3_qh", [64, TC], BF16))
    AOH = stkA.enter_context(nc.sbuf_tensor("3_aoh", [64, TC], BF16))
    scrS = stkA.enter_context(nc.sbuf_tensor("3_scrs", [128, 2, 512], F32))
    PN = stkA.enter_context(nc.sbuf_tensor("3_pn", [128, 2, 512], BF16))
    obuf = stkA.enter_context(nc.sbuf_tensor("3_obuf", [64, 512], BF16))
    drow = stkA.enter_context(nc.sbuf_tensor("3_drow", [1, 2, 512], F32))
    dbf = stkA.enter_context(nc.sbuf_tensor("3_dbf", [1, 2, 512], BF16))
    with nc.Block() as ph:
        @ph.vector
        def _(e):
            e.memset(VT[:, :, 64:65], 1.0)
    for h in range(16):
        qpart = slice((h % 2) * 64, (h % 2) * 64 + 64)
        qch = h // 2
        with nc.Block() as ph:
            @ph.gpsimd
            def _(e, h=h):
                st = c.sem("3_hld")
                b0 = st.n
                st.inc(e.dma_start(out=KVF[0:64, :],
                                   in_=dr["kext"][h * 64:(h + 1) * 64, :]), 16)
                st.inc(e.dma_start(out=QH[:, :],
                                   in_=dr["qfm"][h * 64:(h + 1) * 64, :]), 16)
                st.inc(e.dma_start(out=KVF[64:128, :],
                                   in_=dr["vext"][h * 64:(h + 1) * 64, :]), 16)
                e.wait_ge(st.h, b0 + 16 * 3)
        fS, fM, fE, fO = (c.sem(f"3a{k}") for k in "abcd")
        fD, fB, fV = (c.sem(f"3a{k}") for k in "efg")
        bfS, bfM, bfE, bfO = fS.n, fM.n, fE.n, fO.n
        bfD, bfB, bfV = fD.n, fB.n, fV.n
        pp = PsumPool(nc)
        psS = pp.get([128, 2, 512], F32)
        psO = pp.get([128, 2, 512], F32)
        psD = pp.get([128, 2, 512], F32)
        pstv = pp.get([128, 2, 1024], BF16)
        vt_s, vt_e = c.sem("3_vts"), c.sem("3_vte")
        bvs, bve = vt_s.n, vt_e.n
        with nc.Block() as ph:
            @ph.tensor
            def _(e):
                for tc_ in range(18):
                    if tc_ >= 2:
                        e.wait_ge(vt_e.h, bve + tc_ - 1)
                    vt_s.inc(e.transpose(pstv[:, tc_ % 2, 0:64],
                                         KVF[64:128, tc_ * 128:(tc_ + 1) * 128],
                                         identbf[64:128, 64:128]), 1)

            @ph.vector
            def _(e):
                for tc_ in range(18):
                    e.wait_ge(vt_s.h, bvs + tc_ + 1)
                    vt_e.inc(e.tensor_copy(VT[:, tc_, 0:64],
                                           pstv[:, tc_ % 2, 0:64]), 1)
        with nc.Block() as ph:
            @ph.tensor
            def _(e, h=h):
                def emit_O(gp):
                    itp, kcp = divmod(gp, 6)
                    if kcp == 0 and itp >= 2:
                        e.wait_ge(fV.h, bfV + itp - 1)
                    e.wait_ge(fE.h, bfE + gp + 1)
                    fO.inc(e.matmul(psO[0:65, itp % 2, 0:512],
                                    VT[:, 4 * itp + kcp, :], PN[:, gp % 2, :],
                                    start=(kcp == 0), stop=(kcp == 5)), 1)
                    if kcp == 5:
                        e.wait_ge(fD.h, bfD + itp + 1)
                        if itp >= 2:
                            e.wait_ge(fV.h, bfV + itp - 1)
                        fB.inc(e.matmul(psD[0:64, itp % 2, 0:512],
                                        ones_row[0:1, 0:64],
                                        dbf[0:1, itp % 2, :],
                                        start=True, stop=True), 1)
                for it in range(4):
                    for kc in range(6):
                        g = 6 * it + kc
                        if g >= 2:
                            e.wait_ge(fM.h, bfM + g - 1)
                        k0 = it * 512 + kc * 128
                        e.matmul(psS[:, g % 2, 0:512],
                                 KVF[0:64, k0:k0 + 128],
                                 QH[:, it * 512:(it + 1) * 512],
                                 start=True, stop=True).then_inc(fS.h, 1)
                        fS.n += 1
                        if g >= 1:
                            emit_O(g - 1)
                emit_O(23)

            @ph.vector
            def _(e, h=h):
                for it in range(4 + 1):
                    if it < 4:
                        for kc in range(6):
                            g = 6 * it + kc
                            slot = kc if it == 0 else 6 + kc
                            e.wait_ge(fS.h, bfS + g + 1)
                            if g >= 2:
                                e.wait_ge(fE.h, bfE + g - 1)
                            fM.inc(e.scalar_tensor_tensor(
                                scrS[:, g % 2, :], psS[:, g % 2, 0:512],
                                negMall[:, h:h + 1], MSKS[:, slot, :],
                                ALU.add, ALU.add), 1)
                    if it >= 1:
                        jt = it - 1
                        e.wait_ge(fB.h, bfB + jt + 1)
                        e.tensor_copy(obuf[:, :], psO[0:64, jt % 2, 0:512])
                        fV.inc(e.tensor_tensor(
                            AOH[:, jt * 512:(jt + 1) * 512],
                            obuf[:, :], psD[0:64, jt % 2, 0:512], ALU.mult), 1)
                    if it < 4:
                        e.wait_ge(fO.h, bfO + 6 * (it + 1))
                        if it >= 2:
                            e.wait_ge(fB.h, bfB + it - 1)
                        e.reciprocal(drow[0:1, it % 2, :],
                                     psO[64:65, it % 2, 0:512])
                        fD.inc(e.tensor_copy(dbf[0:1, it % 2, :],
                                             drow[0:1, it % 2, :]), 1)

            @ph.scalar
            def _(e, h=h):
                for it in range(4):
                    for kc in range(6):
                        g = 6 * it + kc
                        e.wait_ge(fM.h, bfM + g + 1)
                        if g >= 2:
                            e.wait_ge(fO.h, bfO + g - 1)
                        fE.inc(e.activation(PN[:, g % 2, :], scrS[:, g % 2, :],
                                            AF.Exp, scale=0.125), 1)
        with nc.Block() as ph:
            @ph.gpsimd
            def _(e, h=h, qpart=qpart, qch=qch):
                st = c.sem("3_hst")
                b0 = st.n
                st.inc(e.dma_start(out=AO[qpart, qch, :], in_=AOH[:, :]), 16)
                e.wait_ge(st.h, b0 + 16)
        pp.close()
    stkA.close()
    stkC = ExitStack()
    SFMB = stkC.enter_context(nc.sbuf_tensor("3_sfmb", [128, TC], BF16))
    CWB = stkC.enter_context(nc.sbuf_tensor("3_cwb", [128, 1024], BF16))
    GSm = stkC.enter_context(nc.sbuf_tensor("3_gsm", [128, TC], F32))
    YSm = stkC.enter_context(nc.sbuf_tensor("3_ysm", [128, TC], F32))
    with nc.Block() as ph:
        @ph.gpsimd
        def _(e):
            st = c.sem("3_cl")
            b0 = st.n
            st.inc(e.dma_start(out=SFMB[:, :], in_=dr["sfm"]), 16)
            st.inc(e.dma_start(out=CWB[:, :], in_=dr["cw"]), 16)
            e.wait_ge(st.h, b0 + 32)
    for m in range(NCH):
        emit_load_weight(c, None, 1024, 128, wpair,
                         dram_ap=dr["wo"][:, m * 128:(m + 1) * 128])
        with nc.Block() as ph:
            @ph.gpsimd
            def _(e, m=m):
                st = c.sem("3_gl")
                b0 = st.n
                st.inc(e.dma_start(out=GSm[:, :],
                                   in_=dr["gfm"][m * 128:(m + 1) * 128, :]), 16)
                e.wait_ge(st.h, b0 + 16)
        pp = PsumPool(nc)
        with nc.Block() as ph:
            mm_loop(c, ph, pp,
                    lambda k, m_, m=m: CWB[:, m * 128:(m + 1) * 128],
                    1, [0], lambda m_: (0, 128), TC,
                    lambda k, ns, w: SFMB[:, ns * 512:ns * 512 + w],
                    lambda e, m_, ns, w, ps:
                        e.tensor_copy(YSm[:, ns * 512:ns * 512 + w], ps),
                    sem_role="3_ys")
        pp.close()
        pp = PsumPool(nc)
        with nc.Block() as ph:
            def evac(e, m_, ns, w, ps, m=m):
                sl = slice(ns * 512, ns * 512 + w)
                e.tensor_scalar(scrA[:, 0:w], ps, boc[:, m:m + 1], None,
                                ALU.add)
                e.tensor_tensor(scrB[:, 0:w], scrA[:, 0:w], YSm[:, sl],
                                ALU.subtract)
                e.tensor_tensor(scrA[:, 0:w], GSm[:, sl], scrB[:, 0:w],
                                ALU.mult)
                e.tensor_tensor(scrB[:, 0:w], scrA[:, 0:w], YSm[:, sl],
                                ALU.add)
                e.tensor_copy(scrC[:, 0:w], XNI[:, m, sl])
                e.tensor_tensor(scrA[:, 0:w], scrB[:, 0:w], scrC[:, 0:w],
                                ALU.add)
                return e.tensor_copy(XNI[:, m, sl], scrA[:, 0:w])
            mm_loop(c, ph, pp, wpair, NCH, [0], lambda m_: (0, 128), TC,
                    lambda k, ns, w: AO[:, k, ns * 512:ns * 512 + w],
                    evac, sem_role="3_wo")
        pp.close()
    stkC.close()
    stk.close()
    H = c.sbuf("3_h", [128, NCH, TC], BF16)
    PRDm = c.sbuf("3_prdm", [128, 32, 512], BF16)
    w2b = c.sbuf("3_w2b", [128, 32, 128], BF16)
    emit_ln(c, [(XNI.rearrange("p a b -> p (a b)"), j * TC) for j in range(NCH)],
            H, TC, gamma=lambda j: g2b2[:, j:j + 1],
            beta=lambda j: g2b2[:, NCH + j:NCH + j + 1])
    xb2 = XBufs(c, TC)
    emit_load_fm(c, xb2.chunks(), dr["x"], TC)
    for ns in range(4):
        for mi in range(32):
            emit_load_weight(c, None, 1024, 128, wpair,
                             dram_ap=dr["w1"][:, mi * 128:(mi + 1) * 128])
            pp = PsumPool(nc)
            with nc.Block() as ph:
                mm_loop(c, ph, pp, wpair, NCH, [0], lambda m_: (0, 128), 512,
                        lambda k, ns_, w, ns=ns:
                            H[:, k, ns * 512:ns * 512 + w],
                        lambda e, m_, ns_, w, ps, mi=mi:
                            e.activation(PRDm[:, mi, 0:w], ps, AF.Gelu,
                                         bias=b1c[:, mi:mi + 1]),
                        evac_eng="scalar", sem_role="3_w1")
            pp.close()
        for mo in range(NCH):
            emit_load_weight(c, None, 4096, 128, w2b,
                             dram_ap=dr["w2"][:, mo * 128:(mo + 1) * 128])
            pp = PsumPool(nc)
            with nc.Block() as ph:
                def evac(e, m_, ns_, w, ps, mo=mo, ns=ns):
                    sl = slice(ns * 512, ns * 512 + w)
                    e.tensor_scalar(scrA[:, 0:w], ps, b2c[:, mo:mo + 1], None,
                                    ALU.add)
                    e.tensor_copy(scrC[:, 0:w], XNI[:, mo, sl])
                    e.tensor_tensor(scrB[:, 0:w], scrA[:, 0:w], scrC[:, 0:w],
                                    ALU.add)
                    return xb2.add_evac(e, mo, sl, scrB[:, 0:w])
                mm_loop(c, ph, pp, w2b, 32, [0], lambda m_: (0, 128), 512,
                        lambda k, ns_, w: PRDm[:, k, 0:w],
                        evac, sem_role="3_w2")
            pp.close()
    emit_store_fm(c, xb2.chunks(), y, TC)
    c.close()
    return nc


def run_a1b(x_chunks, w):
    nc = _nc_cache("a1b1", build_a1b_p1)
    maps = []
    for i in range(NCORES):
        m = {"x": x_chunks[i], "wqkv": w["1b_wqkv"], "wug": w["1b_wug"],
             "bw": w["1b_bw"], "bug": w["1b_bug"], "g1b1": w["1b_g1b1"]}
        m.update(_CONSTS)
        maps.append(m)
    r1 = run_bass_kernel_spmd(nc, maps, core_ids=list(range(NCORES)))
    qkv = [r1.results[i]["qkvfm"] for i in range(NCORES)]
    gf = [r1.results[i]["gfm"] for i in range(NCORES)]
    dv = [r1.results[i]["drv"] for i in range(NCORES)]
    drs = [f32c(np.concatenate([dv[CPS * s + q] for q in range(CPS)], axis=1))
           for s in range(2)]
    nc = _nc_cache("a1b2", build_a1b_p2)
    maps = [{"drv": drs[i // CPS], "acol": w["1b_acol"]}
            for i in range(NCORES)]
    r2 = run_bass_kernel_spmd(nc, maps, core_ids=list(range(NCORES)))
    nc = _nc_cache("a1b3", build_a1b_p3)
    maps = []
    for i in range(NCORES):
        kf = qkv[i][1024:2048]
        vf = qkv[i][2048:3072]
        if i % CPS == 0:
            kh = np.zeros((1024, 256), np.float32)
            vh = np.zeros((1024, 256), np.float32)
            msk = w["1b_masks"]
        else:
            kh = qkv[i - 1][1024:2048][:, -256:]
            vh = qkv[i - 1][2048:3072][:, -256:]
            msk = w["1b_masks_mid"]
        sfm = r2.results[i]["sfm"][:, (i % CPS) * TC:(i % CPS + 1) * TC]
        m = {"x": x_chunks[i], "qfm": f32c(qkv[i][0:1024]),
             "kext": f32c(np.concatenate([kh, kf], axis=1)),
             "vext": f32c(np.concatenate([vh, vf], axis=1)),
             "gfm": f32c(gf[i]), "sfm": f32c(sfm),
             "wo": w["1b_wo"], "boc": w["1b_boc"], "cw": w["1b_cw"],
             "g2b2": w["1b_g2b2"], "w1": w["1b_w1"], "b1c": w["1b_b1c"],
             "w2": w["1b_w2"], "b2c": w["1b_b2c"], "masks": msk}
        m.update(_CONSTS)
        maps.append(m)
    r3 = run_bass_kernel_spmd(nc, maps, core_ids=list(range(NCORES)))
    return [r3.results[i]["y"] for i in range(NCORES)]



def build_ln_only():
    nc = bass.Bass("TRN2", target_bir_lowering=False)
    c = Ctx(nc)
    dr = {}
    _din(nc, dr, "x", (TC, D))
    _din(nc, dr, "identf", (128, 128))
    y = nc.dram_tensor("y", [TC, D], F32, kind="ExternalOutput").ap()
    emit_setup(c, dr)
    xb = XBufs(c, TC)
    emit_load_fm(c, xb.chunks(), dr["x"], TC)
    XF = c.sbuf("lo_xf", [128, NCH, TC], BF16)
    emit_ln(c, xb.chunks(), XF, TC)
    emit_store_fm(c, [(XF.rearrange("p a b -> p (a b)"), j * TC)
                      for j in range(NCH)], y, TC, dt=BF16)
    c.close()
    return nc


def build_add():
    nc = bass.Bass("TRN2", target_bir_lowering=False)
    c = Ctx(nc)
    dr = {}
    _din(nc, dr, "a", (TC, D))
    _din(nc, dr, "b", (TC, D))
    y = nc.dram_tensor("y", [TC, D], F32, kind="ExternalOutput").ap()
    TA = c.sbuf("ad_a", [128, 1024], F32)
    TB = c.sbuf("ad_b", [128, 1024], F32)
    TO = c.sbuf("ad_o", [128, 1024], F32)
    ld = c.sem("ad_ld")
    st = c.sem("ad_st")
    for t in range(TC // 128):
        sl = slice(t * 128, (t + 1) * 128)
        with nc.Block() as ph:
            @ph.gpsimd
            def _(e, sl=sl):
                b0 = ld.n
                ld.inc(e.dma_start(out=TA[:, :], in_=dr["a"][sl, :]), 16)
                ld.inc(e.dma_start(out=TB[:, :], in_=dr["b"][sl, :]), 16)
                e.wait_ge(ld.h, b0 + 32)
        with nc.Block() as ph:
            @ph.vector
            def _(e):
                e.tensor_tensor(TO[:, :], TA[:, :], TB[:, :], ALU.add)
        with nc.Block() as ph:
            @ph.gpsimd
            def _(e, sl=sl):
                b0 = st.n
                st.inc(e.dma_start(out=y[sl, :], in_=TO[:, :]), 16)
                e.wait_ge(st.h, b0 + 16)
    c.close()
    return nc


def run_ln(x_chunks):
    nc = _nc_cache("lnon", build_ln_only)
    maps = []
    for i in range(NCORES):
        m = {"x": x_chunks[i]}
        m.update(_CONSTS)
        maps.append(m)
    res = run_bass_kernel_spmd(nc, maps, core_ids=list(range(NCORES)))
    return [res.results[i]["y"] for i in range(NCORES)]


def run_add(a_chunks, b_chunks):
    nc = _nc_cache("add", build_add)
    maps = [{"a": a_chunks[i], "b": b_chunks[i]} for i in range(NCORES)]
    res = run_bass_kernel_spmd(nc, maps, core_ids=list(range(NCORES)))
    return [res.results[i]["y"] for i in range(NCORES)]
